# version 11
# baseline (speedup 1.0000x reference)
import contextlib
import math
import numpy as np
import ml_dtypes
import concourse.bass as bass
import concourse.mybir as mybir
from concourse.bass_utils import run_bass_kernel_spmd

F32 = mybir.dt.float32
BF16 = mybir.dt.bfloat16
AF = mybir.ActivationFunctionType
ALU = mybir.AluOpType
AX = mybir.AxisListType
BF_NP = ml_dtypes.bfloat16

NCORES = 8
D = 1024
SEQ = 4096
DT = 8
EPS = 1e-5



class Buf:
    __slots__ = ("ap", "name", "wtok", "rtoks")

    def __init__(self, ap, name):
        self.ap = ap
        self.name = name
        self.wtok = None
        self.rtoks = []

    def __getitem__(self, idx):
        return self.ap[idx]


class Eng:
    def __init__(self, prog, name, step):
        self.prog = prog
        self.name = name
        self.sem = None
        self.n = 0
        self.step = step
        self.waited = {}
        self.queue = []
        self.pending = []
        self.pos_of_cnt = {}
        self.flushes = set()

    def push(self, fn):
        self.prog.seqctr += 1
        self.queue.append((float(self.prog.seqctr), fn))

    def ensure_flushed(self, cnt):
        if cnt in self.flushes:
            return
        pos = self.pos_of_cnt[cnt] + 1
        seq = self.queue[pos - 1][0] + 0.5
        self.queue.insert(pos, (seq, lambda e, s=self, cnt=cnt: e.wait_ge(s.sem, cnt)))
        for c, i in self.pos_of_cnt.items():
            if i >= pos:
                self.pos_of_cnt[c] = i + 1
        self.flushes.add(cnt)

    def wait_tok(self, tok):
        if tok is None:
            return
        eng, cnt = tok
        if eng is self and cnt > self.n:
            return
        if self.waited.get(eng.name, 0) >= cnt:
            return
        if eng.step == 16:
            eng.ensure_flushed(cnt)
        self.push(lambda e, eng=eng, cnt=cnt: e.wait_ge(eng.sem, cnt))
        self.waited[eng.name] = cnt


class Prog:

    def __init__(self, name="prog"):
        self.nc = bass.Bass("TRN2", target_bir_lowering=False)
        self.es = contextlib.ExitStack()
        self.consts = {}
        self.const_aps = {}
        self._uid = 0
        self.seqctr = 0
        self.tens = Eng(self, "tens", 1)
        self.vect = Eng(self, "vect", 1)
        self.scal = Eng(self, "scal", 1)
        self.pool = Eng(self, "pool", 1)
        self.qsync = Eng(self, "qsync", 16)
        self.qgp = Eng(self, "qgp", 16)
        self.engines = [self.tens, self.vect, self.scal, self.pool,
                        self.qsync, self.qgp]
        self.sbuf_bytes = 0
        self.hostw = {}

    def uid(self):
        self._uid += 1
        return self._uid

    def dram_in(self, name, shape, dtype=F32):
        if not hasattr(self, "extern_names"):
            self.extern_names = []
        self.extern_names.append(name)
        return self.nc.dram_tensor(name, list(shape), dtype, kind="ExternalInput").ap()

    def dram_out(self, name, shape, dtype=F32):
        return self.nc.dram_tensor(name, list(shape), dtype, kind="ExternalOutput").ap()

    def dram_tmp(self, name, shape, dtype=F32):
        return self.nc.dram_tensor(name, list(shape), dtype, kind="Internal").ap()

    def dram_tmp_buf(self, name, shape, dtype=F32):
        return Buf(self.dram_tmp(name, shape, dtype), name)

    def const(self, name, arr):
        if name in self.const_aps:
            return self.const_aps[name]
        arr = np.ascontiguousarray(arr)
        dt = {np.dtype(np.float32): F32, np.dtype(BF_NP): BF16}[arr.dtype]
        ap = self.nc.dram_tensor(name, list(arr.shape), dt, kind="ExternalInput").ap()
        self.consts[name] = arr
        self.const_aps[name] = ap
        return ap

    def sbuf(self, name, shape, dtype=F32):
        uname = f"{name}_{self.uid()}"
        es = getattr(self, "_scope_es", None) or self.es
        ap = es.enter_context(self.nc.sbuf_tensor(uname, list(shape), dtype))
        nbytes = int(np.prod(shape[1:])) * (2 if dtype == BF16 else 4)
        self.sbuf_bytes += nbytes
        return Buf(ap[:], uname)

    def open_scope(self):
        import contextlib as _ct
        assert getattr(self, "_scope_es", None) is None
        self._scope_es = _ct.ExitStack()

    def close_scope(self):
        self.barrier()
        self._scope_es.close()
        self._scope_es = None

    def barrier(self):
        for q in (self.qsync, self.qgp):
            if q.n > 0:
                q.ensure_flushed(q.n)
        snap = [(e, e.n) for e in self.engines]
        for e in self.engines:
            for oe, n in snap:
                if n > 0:
                    e.wait_tok((oe, n))
        if hasattr(self, "_pool"):
            for b in self._pool.values():
                b.wtok = None
                b.rtoks = []

    def pbank(self, i):
        if not hasattr(self, "_pool"):
            self._pool = {}
        if i not in self._pool:
            self._pool[i] = self.psum(f"bank{i}")
        return self._pool[i]

    def psum(self, name, shape=(128, 512), dtype=F32):
        uname = f"{name}_{self.uid()}"
        ap = self.es.enter_context(self.nc.psum_tensor(uname, list(shape), dtype))
        return Buf(ap[:], uname)

    def op(self, eng, fn, reads=(), writes=(), inc=True):
        for b in reads:
            if b is not None:
                eng.wait_tok(b.wtok)
        for b in writes:
            if b is not None:
                eng.wait_tok(b.wtok)
                for t in b.rtoks:
                    eng.wait_tok(t)
        if inc:
            def emit(e, fn=fn, eng=eng):
                fn(e).then_inc(eng.sem, eng.step)
            eng.push(emit)
            eng.n += eng.step
            if eng.step == 16:
                eng.pos_of_cnt[eng.n] = len(eng.queue) - 1
            tok = (eng, eng.n)
            eng.pending = []
            for b in reads:
                if b is not None:
                    b.rtoks.append(tok)
            for b in writes:
                if b is not None:
                    b.wtok = tok
                    b.rtoks = []
            return tok
        else:
            eng.push(lambda e, fn=fn: fn(e))
            ftok = (eng, eng.n + eng.step)
            eng.pending.append(ftok)
            for b in reads:
                if b is not None:
                    b.rtoks.append(ftok)
            for b in writes:
                if b is not None:
                    b.wtok = ftok
            return None

    def dma(self, q, out, in_, reads=(), writes=()):
        out_ap = out.ap if isinstance(out, Buf) else out
        in_ap = in_.ap if isinstance(in_, Buf) else in_
        reads = list(reads) + ([in_] if isinstance(in_, Buf) else [])
        writes = list(writes) + ([out] if isinstance(out, Buf) else [])
        return self.op(q, lambda e: e.dma_start(out=out_ap, in_=in_ap),
                       reads=reads, writes=writes)

    def tt(self, o, a, b, op, oap=None, aap=None, bap=None, eng=None, extra_reads=()):
        eng = eng or self.vect
        return self.op(eng, lambda e: e.tensor_tensor(
            out=oap if oap is not None else o.ap,
            in0=aap if aap is not None else a.ap,
            in1=bap if bap is not None else b.ap, op=op),
            reads=[a, b] + list(extra_reads), writes=[o])

    def ts(self, o, a, scalar1, op0, scalar2=None, op1=None, oap=None, aap=None,
           extra_reads=(), eng=None):
        eng = eng or self.vect
        if op1 is not None:
            fn = lambda e: e.tensor_scalar(
                out=oap if oap is not None else o.ap,
                in0=aap if aap is not None else a.ap,
                scalar1=scalar1, scalar2=scalar2, op0=op0, op1=op1)
        else:
            fn = lambda e: e.tensor_scalar(
                out=oap if oap is not None else o.ap,
                in0=aap if aap is not None else a.ap,
                scalar1=scalar1, scalar2=None, op0=op0)
        return self.op(eng, fn, reads=[a] + list(extra_reads), writes=[o])

    def stt(self, o, a, scalar, b, op0, op1, oap=None, aap=None, bap=None,
            extra_reads=(), eng=None):
        eng = eng or self.vect
        return self.op(eng, lambda e: e.scalar_tensor_tensor(
            out=oap if oap is not None else o.ap,
            in0=aap if aap is not None else a.ap, scalar=scalar,
            in1=bap if bap is not None else b.ap, op0=op0, op1=op1),
            reads=[a, b] + list(extra_reads), writes=[o])

    def act(self, o, a, func, bias=0.0, scale=1.0, accum=None, accum_ap=None,
            oap=None, aap=None, extra_reads=()):
        kw = {}
        if accum is not None:
            kw["accum_out"] = accum_ap if accum_ap is not None else accum.ap
        return self.op(self.scal, lambda e: e.activation(
            out=oap if oap is not None else o.ap,
            in_=aap if aap is not None else a.ap,
            func=func, bias=bias, scale=scale, **kw),
            reads=[a] + list(extra_reads),
            writes=[o] + ([accum] if accum is not None else []))

    def copy(self, o, a, oap=None, aap=None, eng=None):
        eng = eng or self.vect
        if eng is self.scal:
            return self.op(eng, lambda e: e.copy(
                out=oap if oap is not None else o.ap,
                in_=aap if aap is not None else a.ap), reads=[a], writes=[o])
        return self.op(eng, lambda e: e.tensor_copy(
            oap if oap is not None else o.ap,
            aap if aap is not None else a.ap), reads=[a], writes=[o])

    def memset(self, o, val, oap=None):
        return self.op(self.vect, lambda e: e.memset(
            oap if oap is not None else o.ap, val), writes=[o])

    def mm(self, o, lhsT, rhs, start, stop, oap=None, lap=None, rap=None,
           extra_reads=(), inc=None):
        if inc is None:
            inc = stop
        _l = lap if lap is not None else lhsT.ap
        _o = oap if oap is not None else o.ap
        _r = rap if rap is not None else rhs.ap
        assert _l.shape[-1] == _o.shape[0], \
            f"mm M mismatch: lhsT {_l.shape} out {_o.shape}"
        assert _l.shape[0] == _r.shape[0], \
            f"mm K mismatch: lhsT {_l.shape} rhs {_r.shape}"
        return self.op(self.tens, lambda e: e.matmul(
            oap if oap is not None else o.ap,
            lap if lap is not None else lhsT.ap,
            rap if rap is not None else rhs.ap, start=start, stop=stop),
            reads=[lhsT, rhs] + list(extra_reads), writes=[o], inc=inc)

    def transpose(self, o, a, ident, oap=None, aap=None):
        return self.op(self.tens, lambda e: e.transpose(
            oap if oap is not None else o.ap,
            aap if aap is not None else a.ap, ident.ap),
            reads=[a, ident], writes=[o])

    def reduce(self, o, a, op, negate=False, oap=None, aap=None, eng=None):
        eng = eng or self.vect
        return self.op(eng, lambda e: e.tensor_reduce(
            out=oap if oap is not None else o.ap,
            in_=aap if aap is not None else a.ap,
            axis=AX.X, op=op, negate=negate),
            reads=[a], writes=[o])

    def scan(self, o, a0, a1, initial, oap=None, a0ap=None, a1ap=None,
             extra_reads=(), eng=None):
        return self.op(eng or self.vect, lambda e: e.tensor_tensor_scan(
            out=oap if oap is not None else o.ap,
            data0=a0ap if a0ap is not None else a0.ap,
            data1=a1ap if a1ap is not None else a1.ap,
            initial=initial, op0=ALU.mult, op1=ALU.add),
            reads=[a0, a1] + list(extra_reads), writes=[o])

    def finish(self):
        for eng in self.engines:
            eng.sem = self.es.enter_context(self.nc.semaphore(f"sem_{eng.name}"))
        for eng in self.engines:
            assert not eng.pending, f"unclosed matmul group on {eng.name}"
        with self.es, self.nc.Block() as block:
            def run(entries):
                def body(e):
                    for _, fn in entries:
                        fn(e)
                return body
            merged = sorted(self.qgp.queue + self.pool.queue, key=lambda t: t[0])
            block.sync(run(self.qsync.queue))
            block.gpsimd(run(merged))
            block.tensor(run(self.tens.queue))
            block.vector(run(self.vect.queue))
            block.scalar(run(self.scal.queue))
        return self.nc


def wT(a):
    return np.ascontiguousarray(np.asarray(a, np.float32).T)



class Shared:

    def __init__(self, p: Prog):
        self.p = p
        self.ones_row_f32 = None
        self.c1024_col = None
        self.ident_bf = None
        self.ident_f32 = None

    def setup(self, need_ident_bf=False, need_ident_f32=False):
        p = self.p
        self.ones_row_f32 = p.sbuf("ones_row", (1, 128), F32)
        p.dma(p.qsync, self.ones_row_f32, p.const("c_ones_row", np.ones((1, 128), np.float32)))
        self.c1024_col = p.sbuf("c1024", (128, 1), BF16)
        p.dma(p.qsync, self.c1024_col,
              p.const("c_rcpD", np.full((128, 1), 1.0 / 1024.0, BF_NP)))
        if need_ident_bf:
            self.ident_bf = p.sbuf("identb", (128, 128), BF16)
            p.dma(p.qsync, self.ident_bf, p.const("c_ident_b", np.eye(128, dtype=BF_NP)))
        if need_ident_f32:
            self.ident_f32 = p.sbuf("identf", (128, 128), F32)
            p.dma(p.qsync, self.ident_f32, p.const("c_ident_f", np.eye(128, dtype=np.float32)))
        self.ones_row_bf = p.sbuf("ones_row_bf", (1, 128), BF16)
        p.dma(p.qsync, self.ones_row_bf, p.const("c_ones_row_bf", np.ones((1, 128), BF_NP)))
        self.ones_col_f32 = p.sbuf("ones_col_f32", (128, 1), F32)
        p.dma(p.qsync, self.ones_col_f32, p.const("c_ones_col_f32", np.ones((128, 1), np.float32)))


class LNArena:

    def __init__(self, p: Prog, sh: Shared, T, nk=8):
        self.p, self.sh, self.T, self.nk = p, sh, T, nk
        assert T <= 512
        self.ps_stats = p.pbank(5)
        self.ps_BA = p.pbank(6)
        self.ps_BB = p.pbank(7)
        self.sq = [p.sbuf("ln_sq", (128, T), BF16) for _ in range(2)]
        self.row_var = p.sbuf("ln_var", (1, T), F32)
        self.row_r = p.sbuf("ln_r", (1, T), F32)
        self.row_mr = p.sbuf("ln_mr", (1, T), F32)
        self.row_r1 = p.sbuf("ln_r1", (1, T), F32)
        self.row_mr1 = p.sbuf("ln_mr1", (1, T), F32)
        self.scr = [p.sbuf("ln_scr", (128, T), F32) for _ in range(2)]

    def stats(self, src_bf, n_feat=None, double=False, srcaps=None,
              also_single=False):
        p, sh, T = self.p, self.sh, self.T
        nk = len(src_bf)
        n_feat = n_feat or (128 * nk)
        rcp = 1.0 / n_feat
        for k in range(nk):
            ap = srcaps[k] if srcaps is not None else None
            p.mm(self.ps_stats, sh.c1024_col, src_bf[k], start=(k == 0),
                 stop=(k == nk - 1), oap=self.ps_stats.ap[0:1, :], rap=ap)
        for k in range(nk):
            ap = srcaps[k] if srcaps is not None else None
            p.act(self.sq[k % 2], src_bf[k], AF.Square, aap=ap)
            p.mm(self.ps_stats, sh.c1024_col, self.sq[k % 2], start=(k == 0),
                 stop=(k == nk - 1), oap=self.ps_stats.ap[64:65, :], inc=True)
        corr = (128.0 * nk) / 1024.0 / n_feat
        scale_m = 1024.0 * rcp
        p.act(self.row_var, self.ps_stats, AF.Square, scale=scale_m,
              aap=self.ps_stats.ap[0:1, :])
        p.stt(self.row_var, self.ps_stats, scale_m, self.row_var,
              ALU.mult, ALU.subtract, aap=self.ps_stats.ap[64:65, :])
        if also_single:
            p.ts(self.row_r1, self.row_var, EPS, ALU.add)
            p.act(self.row_r1, self.row_r1, AF.Ln)
            p.act(self.row_r1, self.row_r1, AF.Exp, scale=-0.5)
            p.stt(self.row_mr1, self.ps_stats, scale_m, self.row_r1,
                  ALU.mult, ALU.mult, aap=self.ps_stats.ap[0:1, :])
        if double:
            p.ts(self.row_var, self.row_var, 1.0 + EPS, ALU.mult,
                 scalar2=EPS * (1.0 + EPS), op1=ALU.add)
        else:
            p.ts(self.row_var, self.row_var, EPS, ALU.add)
        p.act(self.row_r, self.row_var, AF.Ln)
        p.act(self.row_r, self.row_r, AF.Exp, scale=-0.5)
        p.stt(self.row_mr, self.ps_stats, scale_m, self.row_r,
              ALU.mult, ALU.mult, aap=self.ps_stats.ap[0:1, :])
        p.mm(self.ps_BA, sh.ones_row_f32, self.row_r, start=True, stop=True)
        p.mm(self.ps_BB, sh.ones_row_f32, self.row_mr, start=True, stop=True)

    def apply(self, src, outs, srcaps=None, outaps=None):
        p = self.p
        for k in range(len(src)):
            sa = srcaps[k] if srcaps is not None else None
            oa = outaps[k] if outaps is not None else None
            scr = self.scr[k % 2]
            p.tt(scr, src[k], self.ps_BA, ALU.mult, aap=sa)
            p.tt(outs[k], scr, self.ps_BB, ALU.subtract, oap=oa)

    def bcast_single(self, pA, pB):
        p, sh = self.p, self.sh
        p.mm(pA, sh.ones_row_f32, self.row_r1, start=True, stop=True)
        p.mm(pB, sh.ones_row_f32, self.row_mr1, start=True, stop=True)

    def apply_single(self, src, outs, pA, pB):
        p = self.p
        for k in range(len(src)):
            scr = self.scr[k % 2]
            p.tt(scr, src[k], pA, ALU.mult)
            p.tt(outs[k], scr, pB, ALU.subtract)

    def apply_add(self, src, res, outs, srcaps=None, resaps=None, outaps=None):
        p = self.p
        for k in range(len(src)):
            sa = srcaps[k] if srcaps is not None else None
            ra = resaps[k] if resaps is not None else None
            oa = outaps[k] if outaps is not None else None
            s0, s1 = self.scr[0], self.scr[1]
            p.tt(s0, src[k], self.ps_BA, ALU.mult, aap=sa)
            p.tt(s1, s0, self.ps_BB, ALU.subtract)
            p.tt(outs[k], s1, res[k], ALU.add, bap=ra, oap=oa)


class WLoader:

    def __init__(self, p: Prog, name, wt_np, mwidth=None):
        self.p = p
        K, M = wt_np.shape
        assert K % 128 == 0
        self.K, self.M = K, M
        self.nk = K // 128
        self.dram = p.const(name, np.ascontiguousarray(wt_np.astype(np.float32)))
        self.mw = mwidth or M
        self.strips = [p.sbuf(f"{name}_s{k}", (128, self.mw), BF16)
                       for k in range(self.nk)]
        self.loaded_m0 = [None] * self.nk

    def load(self, m0=0):
        p = self.p
        for k in range(self.nk):
            if self.loaded_m0[k] == m0:
                continue
            p.dma(p.qgp, self.strips[k],
                  self.dram[k * 128:(k + 1) * 128, m0:m0 + self.mw])
            self.loaded_m0[k] = m0

    def tile(self, k, m):
        return self.strips[k].ap[:, m * 128:(m + 1) * 128]


def load_cols(p: Prog, sh: Shared, dram_1d_ap, n, name, dtype=F32):
    nk = (n + 127) // 128
    row = p.sbuf(f"{name}_row", (1, nk * 128), F32)
    src2 = dram_1d_ap.rearrange("(o n) -> o n", o=1)
    if n % 128 == 0:
        p.dma(p.qsync, row, src2)
    else:
        p.memset(row, 0.0)
        p.op(p.qsync, lambda e: e.dma_start(out=row.ap[0:1, 0:n], in_=src2),
             writes=[row])
    ps = p.pbank(4)
    cols = []
    for k in range(nk):
        p.op(p.tens, lambda e, k=k: e.transpose(
            ps.ap[:, 0:1], row.ap[0:1, k * 128:(k + 1) * 128],
            sh.ident_f32.ap[0:1, 0:1]), reads=[row, sh.ident_f32], writes=[ps])
        c = p.sbuf(f"{name}_c{k}", (128, 1), F32)
        p.copy(c, ps, aap=ps.ap[:, 0:1], eng=p.scal)
        cols.append(c)
    return cols


def emit_entry_transpose(p: Prog, sh: Shared, x_in, x_res):
    tm = [p.sbuf("ent_tm", (128, 1024), F32) for _ in range(2)]
    st = [p.sbuf("ent_st", (128, 128), F32) for _ in range(2)]
    ps = [p.pbank(3), p.pbank(4)]
    for tt in range(32):
        tmb = tm[tt % 2]
        p.dma(p.qsync, tmb, x_in[tt * 128:(tt + 1) * 128, :])
        for dd in range(DT):
            j = (tt * DT + dd) % 2
            p.transpose(ps[j], tmb, sh.ident_f32,
                        aap=tmb.ap[:, dd * 128:(dd + 1) * 128],
                        oap=ps[j].ap[:, 0:128])
            p.copy(st[j], ps[j], eng=p.scal, aap=ps[j].ap[:, 0:128])
            p.dma(p.qgp, x_res[dd * 128:(dd + 1) * 128, tt * 128:(tt + 1) * 128],
                  st[j])


def emit_exit_transpose(p: Prog, sh: Shared, x_res, x_out):
    fm = [p.sbuf("ext_fm", (128, 4096), F32) for _ in range(2)]
    st = [p.sbuf("ext_st", (128, 128), F32) for _ in range(2)]
    ps = [p.pbank(3), p.pbank(4)]
    for dd in range(DT):
        fmb = fm[dd % 2]
        p.dma(p.qsync, fmb, x_res[dd * 128:(dd + 1) * 128, :])
        for tt in range(32):
            j = (dd * 32 + tt) % 2
            p.transpose(ps[j], fmb, sh.ident_f32,
                        aap=fmb.ap[:, tt * 128:(tt + 1) * 128],
                        oap=ps[j].ap[:, 0:128])
            p.copy(st[j], ps[j], eng=p.scal, aap=ps[j].ap[:, 0:128])
            p.dma(p.qgp, x_out[tt * 128:(tt + 1) * 128, dd * 128:(dd + 1) * 128],
                  st[j])


class XStream:

    def __init__(self, p: Prog, T, double=False):
        n = 2 if double else 1
        self.p, self.T = p, T
        self.x = [[p.sbuf("xs_f", (128, T), F32) for _ in range(DT)] for _ in range(n)]
        self.xb = [[p.sbuf("xs_b", (128, T), BF16) for _ in range(DT)] for _ in range(n)]

    def load(self, x_res, tb, buf=0, need_f32=True):
        p, T = self.p, self.T
        t0 = tb * T
        for k in range(DT):
            if need_f32:
                p.dma(p.qsync, self.x[buf][k], x_res[k * 128:(k + 1) * 128, t0:t0 + T])
            p.dma(p.qgp, self.xb[buf][k], x_res[k * 128:(k + 1) * 128, t0:t0 + T])


def store_x(p: Prog, xnew, x_out, tb, T):
    t0 = tb * T
    for k in range(DT):
        p.dma(p.qsync, x_out[k * 128:(k + 1) * 128, t0:t0 + T], xnew[k])



def emit_l2ffn(p: Prog, sh: Shared, ln: "LNArena", xs: "XStream",
               x_in, x_out, T=512):
    g_cols = load_cols(p, sh, p.dram_in("L2_ffn_ln_g", (D,)), D, "l2f_g")
    b_cols = load_cols(p, sh, p.dram_in("L2_ffn_ln_b", (D,)), D, "l2f_b")
    z2 = [p.sbuf("l2f_z2", (128, T), F32) for _ in range(DT)]
    yn = [p.sbuf("l2f_yn", (128, T), BF16) for _ in range(DT)]
    xnew = [p.sbuf("l2f_xn", (128, T), F32) for _ in range(DT)]
    nt = SEQ // T
    for tb in range(nt):
        xs.load(x_in, tb, buf=0)
        xf, xb = xs.x[0], xs.xb[0]
        ln.stats(xb, double=True)
        ln.apply(xb, z2)
        for k in range(DT):
            p.ts(yn[k], z2[k], g_cols[k].ap[:, 0:1], ALU.mult,
                 scalar2=b_cols[k].ap[:, 0:1], op1=ALU.add,
                 extra_reads=[g_cols[k], b_cols[k]])
        ln.stats(yn, double=False)
        ln.apply_add(yn, xf, xnew)
        store_x(p, xnew, x_out, tb, T)



def emit_l1ffn(p: Prog, sh: Shared, ln: "LNArena", xs: "XStream",
               x_in, x_out, T=512):
    G1R = p.dram_in("L1_ffn_G1_R", (1, 32, 4, 32))
    G2R = p.dram_in("L1_ffn_G2_R", (4, 16, 1, 16))
    G1I = p.dram_in("L1_ffn_G1_I", (1, 32, 4, 32))
    G2I = p.dram_in("L1_ffn_G2_I", (4, 16, 1, 16))
    br_cols = load_cols(p, sh, p.dram_in("L1_ffn_bias_real", (512,)), 512, "l1f_br")
    bi_cols = load_cols(p, sh, p.dram_in("L1_ffn_bias_imag", (512,)), 512, "l1f_bi")

    j = np.arange(512)
    Eb = (j[None, :] // 16 == np.arange(32)[:, None]).astype(np.float32)
    Ee = (j[None, :] % 16 == np.arange(16)[:, None]).astype(np.float32)
    Ed = (j[None, :] // 16 == np.arange(32)[:, None]).astype(np.float32)
    Eg = (j[None, :] % 16 == np.arange(16)[:, None]).astype(np.float32)
    EbC = p.sbuf("l1f_Eb", (32, 512)); p.dma(p.qsync, EbC, p.const("c_l1f_Eb", Eb))
    EeC = p.sbuf("l1f_Ee", (16, 512)); p.dma(p.qsync, EeC, p.const("c_l1f_Ee", Ee))
    EdC = p.sbuf("l1f_Ed", (32, 512)); p.dma(p.qsync, EdC, p.const("c_l1f_Ed", Ed))
    EgC = p.sbuf("l1f_Eg", (16, 512)); p.dma(p.qsync, EgC, p.const("c_l1f_Eg", Eg))

    g1 = {"R": p.sbuf("l1f_g1r", (32, 128)), "I": p.sbuf("l1f_g1i", (32, 128))}
    g2 = {"R": p.sbuf("l1f_g2r", (16, 64)), "I": p.sbuf("l1f_g2i", (16, 64))}
    p.dma(p.qsync, g1["R"], G1R.rearrange("o b c d -> (o b) (c d)"))
    p.dma(p.qsync, g1["I"], G1I.rearrange("o b c d -> (o b) (c d)"))
    p.op(p.qsync, lambda e: e.dma_start(
        out=g2["R"].ap.rearrange("e (c g) -> e c g", c=4),
        in_=G2R.rearrange("c e o g -> (e o) c g")), writes=[g2["R"]])
    p.op(p.qsync, lambda e: e.dma_start(
        out=g2["I"].ap.rearrange("e (c g) -> e c g", c=4),
        in_=G2I.rearrange("c e o g -> (e o) c g")), writes=[g2["I"]])

    ps_t = p.pbank(0)
    ps_A = p.pbank(1)
    ps_B = p.pbank(2)
    t1s = p.sbuf("l1f_t1", (32, 512))
    t2s = p.sbuf("l1f_t2", (16, 512))
    m1s = p.sbuf("l1f_m1", (128, 512))
    wtmp = p.sbuf("l1f_wtmp", (128, 512))
    Wbf = {}
    for RI in ("R", "I"):
        Wacc = [p.sbuf(f"l1f_Wac{RI}{m}", (128, 512)) for m in range(4)]
        for c in range(4):
            p.mm(ps_t, g1[RI], EbC, start=True, stop=True,
                 lap=g1[RI].ap[:, c * 32:(c + 1) * 32], oap=ps_t.ap[0:32, :])
            p.copy(t1s, ps_t, aap=ps_t.ap[0:32, :], eng=p.scal)
            p.mm(ps_t, g2[RI], EeC, start=True, stop=True,
                 lap=g2[RI].ap[:, c * 16:(c + 1) * 16], oap=ps_t.ap[0:16, :])
            p.copy(t2s, ps_t, aap=ps_t.ap[0:16, :], eng=p.scal)
            for m in range(4):
                p.mm(ps_A, EdC, t1s, start=True, stop=True,
                     lap=EdC.ap[:, m * 128:(m + 1) * 128])
                p.mm(ps_B, EgC, t2s, start=True, stop=True,
                     lap=EgC.ap[:, m * 128:(m + 1) * 128])
                p.copy(m1s, ps_A)
                if c == 0:
                    p.tt(Wacc[m], m1s, ps_B, ALU.mult)
                else:
                    p.tt(wtmp, m1s, ps_B, ALU.mult)
                    p.tt(Wacc[m], Wacc[m], wtmp, ALU.add)
        for k in range(4):
            b = p.sbuf(f"l1f_W{RI}b{k}", (128, 512), BF16)
            p.copy(b, Wacc[k], eng=p.scal)
            Wbf[(RI, k)] = b
    negWi = []
    for k in range(4):
        b = p.sbuf(f"l1f_nWib{k}", (128, 512), BF16)
        p.ts(b, Wbf[("I", k)], -1.0, ALU.mult)
        negWi.append(b)

    xn = [p.sbuf("l1f_xn", (128, T), BF16) for _ in range(DT)]
    yb = [p.sbuf("l1f_y", (128, T), BF16) for _ in range(DT)]
    xnew = [p.sbuf("l1f_xo", (128, T), F32) for _ in range(DT)]
    ps_mm = [p.pbank(0), p.pbank(1)]
    nt = SEQ // T
    for tb in range(nt):
        xs.load(x_in, tb, buf=0)
        xf, xb = xs.x[0], xs.xb[0]
        ln.stats(xb, double=False)
        ln.apply(xb, xn)
        for m in range(8):
            ps = ps_mm[m % 2]
            if m < 4:
                pairs = [(Wbf[("R", k)], xn[k]) for k in range(4)] + \
                        [(negWi[k], xn[4 + k]) for k in range(4)]
                bcol = br_cols[m]
            else:
                pairs = [(Wbf[("I", k)], xn[k]) for k in range(4)] + \
                        [(Wbf[("R", k)], xn[4 + k]) for k in range(4)]
                bcol = bi_cols[m - 4]
            mm0 = m % 4
            for i, (w, a) in enumerate(pairs):
                p.mm(ps, w, a, start=(i == 0), stop=(i == 7),
                     lap=w.ap[:, mm0 * 128:(mm0 + 1) * 128])
            p.ts(yb[m], ps, bcol.ap[:, 0:1], ALU.add, extra_reads=[bcol])
        ln.stats(yb, double=False)
        ln.apply_add(yb, xf, xnew)
        store_x(p, xnew, x_out, tb, T)


class WCache:

    def __init__(self, p: Prog):
        self.p = p
        self.done = {}

    def get(self, name, arr_np):
        if name in self.done:
            return self.done[name]
        p = self.p
        src = p.const(name, np.ascontiguousarray(arr_np.astype(np.float32)))
        dst_t = p.nc.dram_tensor(name + "_bf", list(arr_np.shape), BF16, kind="Internal")
        dst = dst_t.ap()
        buf = Buf(dst, name + "_bf")
        p.dma(p.qgp, buf, src)
        self.done[name] = buf
        return buf


def strip_batch_ap(dram_ap, k0, nk, m0, mw, M):
    v = dram_ap[k0 * 128:(k0 + nk) * 128, m0:m0 + mw]
    return v.rearrange("(k p) m -> p k m", p=128)



def emit_l0ffn(p: Prog, sh: Shared, ln: "LNArena", xs: "XStream",
               x_in, x_out, wc: WCache, T=512):
    nt = SEQ // T
    nch = T // 128

    W1T = p.hostw["L0_ffn_w1T"]
    W2T = p.hostw["L0_ffn_w2T"]
    S1T = p.hostw["L0_ffn_sfc1T"]
    S2T = p.hostw["L0_ffn_sfc2T"]
    S3T = p.hostw["L0_ffn_sfc3T"]
    w1d = wc.get("l0f_w1T", W1T)
    w2d = wc.get("l0f_w2T", W2T)
    s1d = wc.get("l0f_s1T", S1T)
    s2d = wc.get("l0f_s2T", S2T)
    s3d = wc.get("l0f_s3T", S3T)

    gwT = p.hostw["L0_ffn_gate_wT"]
    gwc = p.const("l0f_gwT", gwT)
    gstr = [p.sbuf("l0f_gw", (128, 16), F32) for _ in range(8)]
    for k in range(8):
        p.dma(p.qsync, gstr[k], gwc[k * 128:(k + 1) * 128, :])
    ps4 = p.pbank(4)
    for k in range(8):
        p.mm(ps4, sh.ones_col_f32, gstr[k], start=(k == 0), stop=(k == 7),
             oap=ps4.ap[0:1, 0:16])
    gsum_row = p.sbuf("l0f_gsum", (1, 16))
    p.copy(gsum_row, ps4, aap=ps4.ap[0:1, 0:16], eng=p.scal)
    gsum_col = p.sbuf("l0f_gsumc", (16, 1))
    p.op(p.tens, lambda e: e.transpose(
        ps4.ap[0:16, 0:1], gsum_row.ap, sh.ident_f32.ap[0:1, 0:1]),
        reads=[gsum_row, sh.ident_f32], writes=[ps4])
    p.copy(gsum_col, ps4, aap=ps4.ap[0:16, 0:1], eng=p.scal)
    glog_t1 = p.sbuf("l0f_glt1", (16, T))
    glog_t2 = p.sbuf("l0f_glt2", (16, T))
    glog_t0 = p.sbuf("l0f_glt0", (16, T))

    ddr_disp = p.dram_tmp_buf("l0f_dispddr", (16, T))
    xn = [p.sbuf("l0f_xn", (128, T), BF16) for _ in range(DT)]
    glog = p.sbuf("l0f_glog", (16, T))
    lgt = p.sbuf("l0f_lgt", (128, 16))
    m1n = p.sbuf("l0f_m1n", (128, 1))
    m2 = p.sbuf("l0f_m2", (128, 1))
    e2 = p.sbuf("l0f_e2", (128, 1))
    winv = p.sbuf("l0f_winv", (128, 1))
    w2v = p.sbuf("l0f_w2v", (128, 1))
    eq1 = p.sbuf("l0f_eq1", (128, 16))
    eq2 = p.sbuf("l0f_eq2", (128, 16))
    dtok = p.sbuf("l0f_dtok", (128, 16))
    disp_fm = p.sbuf("l0f_dispfm", (16, T))
    disp_flat = p.sbuf("l0f_dispflat", (1, 16 * T), BF16)
    gg = [p.sbuf("l0f_gg", (128, T), BF16) for _ in range(16)]
    scaled = [p.sbuf("l0f_sc", (128, T), BF16) for _ in range(64)]
    sfc_ar = [p.sbuf("l0f_sfar", (128, 8 * 128), BF16) for _ in range(2)]
    w1_ar = [p.sbuf("l0f_w1ar", (128, 8 * 512), BF16) for _ in range(2)]
    kstrip = [p.sbuf("l0f_kstr", (128, 1024), BF16) for _ in range(2)]
    hscr = [p.sbuf("l0f_h", (128, T), BF16) for _ in range(2)]
    g2scr = [p.sbuf("l0f_g2", (128, T), BF16) for _ in range(2)]
    yb = [p.sbuf("l0f_y", (128, T), BF16) for _ in range(DT)]

    for tb in range(nt):
        t0 = tb * T
        xs.load(x_in, tb, buf=0)
        xf, xb = xs.x[0], xs.xb[0]
        ln.stats(xb, double=True)
        ln.apply(xb, xn)
        psg = p.pbank(0)
        for k in range(8):
            p.mm(psg, gstr[k], xf[k], start=(k == 0), stop=(k == 7),
                 oap=psg.ap[0:16, :])
        p.copy(glog_t0, psg, aap=psg.ap[0:16, :], eng=p.scal)
        p.tt(glog_t1, glog_t0, ln.ps_BA, ALU.mult, bap=ln.ps_BA.ap[0:16, :])
        p.ts(glog_t2, ln.ps_BB, gsum_col.ap[:, 0:1], ALU.mult,
             aap=ln.ps_BB.ap[0:16, :], extra_reads=[gsum_col])
        p.tt(glog, glog_t1, glog_t2, ALU.subtract)
        ps3 = p.pbank(3)
        for c in range(nch):
            p.op(p.tens, lambda e, c=c: e.transpose(
                ps3.ap[:, 0:16], glog.ap[:, c * 128:(c + 1) * 128],
                sh.ident_f32.ap[0:16, 0:16]), reads=[glog, sh.ident_f32], writes=[ps3])
            p.copy(lgt, ps3, aap=ps3.ap[:, 0:16])
            p.reduce(m1n, lgt, ALU.max, negate=True)
            p.ts(eq1, lgt, m1n.ap[:, 0:1], ALU.add, extra_reads=[m1n])
            p.ts(eq1, eq1, 0.0, ALU.is_equal)
            p.stt(eq2, eq1, -1e30, lgt, ALU.mult, ALU.add)
            p.reduce(m2, eq2, ALU.max)
            p.act(e2, m2, AF.Exp, bias=m1n.ap[:, 0:1], extra_reads=[m1n])
            p.ts(winv, e2, 1.0, ALU.add)
            p.op(p.vect, lambda e: e.reciprocal(out=winv.ap, in_=winv.ap),
                 reads=[winv], writes=[winv])
            p.tt(w2v, e2, winv, ALU.mult)
            p.ts(eq2, lgt, m2.ap[:, 0:1], ALU.is_equal, extra_reads=[m2])
            p.ts(dtok, eq1, winv.ap[:, 0:1], ALU.mult, extra_reads=[winv])
            p.stt(dtok, eq2, w2v.ap[:, 0:1], dtok, ALU.mult, ALU.add,
                  extra_reads=[w2v])
            p.op(p.tens, lambda e, c=c: e.transpose(
                ps3.ap[0:16, 0:128], dtok.ap, sh.ident_f32.ap),
                reads=[dtok, sh.ident_f32], writes=[ps3])
            p.copy(disp_fm, ps3, oap=disp_fm.ap[:, c * 128:(c + 1) * 128],
                   aap=ps3.ap[0:16, 0:128], eng=p.scal)
        p.dma(p.qgp, ddr_disp, disp_fm)
        p.dma(p.qgp, disp_flat, ddr_disp.ap.rearrange("a b -> (a b)"),
              reads=[ddr_disp])
        for m in range(16):
            ar = sfc_ar[m % 2]
            ps0, ps1b = p.pbank(0), p.pbank(1)
            p.dma(p.qsync, ar, strip_batch_ap(s1d.ap, 0, 8, m * 128, 128, 2048), reads=[s1d])
            for k in range(8):
                p.mm(ps0, ar, xn[k], start=(k == 0), stop=(k == 7),
                     lap=ar.ap[:, k * 128:(k + 1) * 128])
            p.act(gg[m], ps0, AF.Silu)
            p.dma(p.qsync, ar, strip_batch_ap(s2d.ap, 0, 8, m * 128, 128, 2048), reads=[s2d])
            for k in range(8):
                p.mm(ps1b, ar, xn[k], start=(k == 0), stop=(k == 7),
                     lap=ar.ap[:, k * 128:(k + 1) * 128])
            p.copy(g2scr[m % 2], ps1b, eng=p.scal)
            p.tt(gg[m], gg[m], g2scr[m % 2], ALU.mult)
        ps_db = p.pbank(2)
        for e_i in range(16):
            ar = w1_ar[e_i % 2]
            p.dma(p.qsync, ar, strip_batch_ap(
                w1d.ap[e_i * 1024:(e_i + 1) * 1024, :], 0, 8, 0, 512, 512),
                reads=[w1d])
            p.mm(ps_db, sh.ones_row_bf, disp_flat, start=True, stop=True,
                 rap=disp_flat.ap[0:1, e_i * T:(e_i + 1) * T])
            for mh in range(4):
                ps = p.pbank(mh % 2)
                for k in range(8):
                    p.mm(ps, ar, xn[k], start=(k == 0), stop=(k == 7),
                         lap=ar.ap[:, k * 512 + mh * 128:k * 512 + (mh + 1) * 128])
                hb = hscr[mh % 2]
                p.act(hb, ps, AF.Silu)
                p.tt(scaled[e_i * 4 + mh], hb, ps_db, ALU.mult)
        banks = [p.pbank(i) for i in range(8)]
        for k in range(64):
            ks = kstrip[k % 2]
            p.dma(p.qsync, ks, w2d.ap[k * 128:(k + 1) * 128, :], reads=[w2d])
            for m in range(8):
                p.mm(banks[m], ks, scaled[k], start=(k == 0), stop=False,
                     lap=ks.ap[:, m * 128:(m + 1) * 128], inc=(m == 7))
        for k in range(16):
            ks = kstrip[k % 2]
            p.dma(p.qsync, ks, s3d.ap[k * 128:(k + 1) * 128, :], reads=[s3d])
            for m in range(8):
                p.mm(banks[m], ks, gg[k], start=False, stop=(k == 15),
                     lap=ks.ap[:, m * 128:(m + 1) * 128], inc=(m == 7))
        for m in range(8):
            p.copy(yb[m], banks[m], eng=p.scal)
        ln.stats(yb, double=False)
        ln.apply_add(yb, xf, xf)
        store_x(p, xf, x_out, tb, T)


def emit_l3attn(p: Prog, sh: Shared, ln: "LNArena", xs: "XStream",
                x_in, x_out, wc: WCache, T=512):
    nt = SEQ // T
    wq = wc.get("l3a_Wq", p.hostw["L3_attn_Wq"])
    wk = wc.get("l3a_Wk", p.hostw["L3_attn_Wk"])
    wv = wc.get("l3a_Wv", p.hostw["L3_attn_Wv"])
    wo = wc.get("l3a_Wo", p.hostw["L3_attn_Wo"])

    Dh = 128
    inv = 1.0 / (10000.0 ** (np.arange(0, Dh, 2, dtype=np.float64) / Dh))
    tpos = np.arange(SEQ, dtype=np.float64)
    fr = np.outer(tpos, inv)
    emb = np.concatenate([fr, fr], axis=1)
    cos = np.cos(emb).T.astype(np.float32)
    sin = np.sin(emb).T.astype(np.float32)
    ssin = sin.copy()
    ssin[0:64] = -ssin[0:64]
    sc = 1.0 / math.sqrt(Dh)
    qcos = p.sbuf("l3a_cos", (128, SEQ), BF16)
    qsin = p.sbuf("l3a_sin", (128, SEQ), BF16)
    kcos, ksin = qcos, qsin
    p.dma(p.qsync, qcos, p.const("c_l3a_cos", cos.astype(BF_NP)))
    p.dma(p.qsync, qsin, p.const("c_l3a_sin", ssin.astype(BF_NP)))
    msks = []
    for r in range(4):
        m = np.full((128, 512), -1e30, np.float32)
        for row in range(128):
            m[row, :r * 128 + row + 1] = 0.0
        mb = p.sbuf(f"l3a_msk{r}", (128, 512))
        p.dma(p.qsync, mb, p.const(f"c_l3a_msk{r}", m))
        msks.append(mb)

    q_ddr = p.dram_tmp_buf("l3a_q", (D, SEQ), BF16)
    v_ddr = p.dram_tmp_buf("l3a_v", (SEQ, 512), BF16)
    k_res = [p.sbuf("l3a_kres", (128, SEQ), BF16) for _ in range(4)]

    xn = [p.sbuf("l3a_xn", (128, T), BF16) for _ in range(DT)]
    wstr = [p.sbuf("l3a_wstr", (128, 4096), BF16) for _ in range(2)]
    ev = [p.sbuf("l3a_ev", (128, 512), BF16) for _ in range(2)]
    rs1 = p.sbuf("l3a_rs1", (128, 512))
    rs2 = p.sbuf("l3a_rs2", (128, 512))

    def rope(dst_bf, src_ps, cosd, sind, t0, Tw):
        p.tt(rs1, src_ps, cosd, ALU.mult, bap=cosd.ap[:, t0:t0 + Tw])
        p.tt(rs2, src_ps, sind, ALU.mult, aap=src_ps.ap[64:128, :],
             bap=sind.ap[0:64, t0:t0 + Tw], oap=rs2.ap[0:64, :])
        p.tt(rs2, src_ps, sind, ALU.mult, aap=src_ps.ap[0:64, :],
             bap=sind.ap[64:128, t0:t0 + Tw], oap=rs2.ap[64:128, :])
        p.tt(dst_bf, rs1, rs2, ALU.add)

    for tb in range(nt):
        t0 = tb * T
        xs.load(x_in, tb, buf=0)
        xf, xb = xs.x[0], xs.xb[0]
        ln.stats(xb, double=False)
        ln.apply(xb, xn)
        for m in range(8):
            ps = p.pbank(m % 2)
            ar = wstr[m % 2]
            p.op(p.qsync, lambda e, ar=ar, m=m: e.dma_start(
                out=ar.ap[:, 0:1024],
                in_=strip_batch_ap(wq.ap, 0, 8, m * 128, 128, D)),
                reads=[wq], writes=[ar])
            for k in range(8):
                p.mm(ps, ar, xn[k], start=(k == 0), stop=(k == 7),
                     lap=ar.ap[:, k * 128:(k + 1) * 128])
            rope(ev[m % 2], ps, qcos, qsin, t0, T)
            p.dma(p.qgp, q_ddr.ap[m * 128:(m + 1) * 128, t0:t0 + T], ev[m % 2],
                  writes=[q_ddr])
        for m in range(4):
            ps = p.pbank(m % 2)
            ar = wstr[m % 2]
            p.op(p.qsync, lambda e, ar=ar, m=m: e.dma_start(
                out=ar.ap[:, 0:1024],
                in_=strip_batch_ap(wk.ap, 0, 8, m * 128, 128, 512)),
                reads=[wk], writes=[ar])
            for k in range(8):
                p.mm(ps, ar, xn[k], start=(k == 0), stop=(k == 7),
                     lap=ar.ap[:, k * 128:(k + 1) * 128])
            kr = Buf(k_res[m].ap[:, t0:t0 + T], k_res[m].name)
            rope(kr, ps, kcos, ksin, t0, T)
            k_res[m].wtok = kr.wtok
        for tc in range(T // 128):
            ps = p.pbank(2)
            ar = wstr[tc % 2]
            p.op(p.qsync, lambda e, ar=ar: e.dma_start(
                out=ar.ap[:, 0:4096],
                in_=strip_batch_ap(wv.ap, 0, 8, 0, 512, 512)),
                reads=[wv], writes=[ar])
            for k in range(8):
                p.mm(ps, xn[k], ar, start=(k == 0), stop=(k == 7),
                     lap=xn[k].ap[:, tc * 128:(tc + 1) * 128],
                     rap=ar.ap[:, k * 512:(k + 1) * 512])
            e2 = ev[tc % 2]
            p.copy(e2, ps, eng=p.scal)
            p.dma(p.qgp, v_ddr.ap[t0 + tc * 128:t0 + (tc + 1) * 128, :], e2,
                  writes=[v_ddr])

    q_sb = p.sbuf("l3a_qsb", (128, 8 * 128), BF16)
    v_sb = [p.sbuf("l3a_vsb", (128, 512), BF16) for _ in range(32)]
    p_exp4 = [p.sbuf("l3a_pexp", (128, 1024)) for _ in range(4)]
    pT = [p.sbuf("l3a_pT", (128, 128), BF16) for _ in range(2)]
    nmA = [p.sbuf("l3a_nmA", (128, 1)) for _ in range(2)]
    nmN = [p.sbuf("l3a_nmN", (128, 1)) for _ in range(2)]
    nmS = [p.sbuf("l3a_nmS", (128, 1)) for _ in range(2)]
    lsum = [p.sbuf("l3a_lsum", (128, 1)) for _ in range(2)]
    lB = [p.sbuf("l3a_lB", (128, 1)) for _ in range(2)]
    att_tok = p.sbuf("l3a_atok", (128, 1024))
    att_fm = [p.sbuf("l3a_afm", (128, T), BF16) for _ in range(DT)]
    yb = [p.sbuf("l3a_y", (128, T), BF16) for _ in range(DT)]
    banks = [p.pbank(i) for i in range(8)]

    for qt in range(32):
        p.dma(p.qsync, q_sb,
              q_ddr.ap[:, qt * 128:(qt + 1) * 128].rearrange("(k p) m -> p k m", p=128),
              reads=[q_ddr])
        p.dma(p.qsync, v_sb[qt], v_ddr.ap[qt * 128:(qt + 1) * 128, :], reads=[v_ddr])
        nb = qt // 4 + 1
        lastw = (qt % 4 + 1) * 128
        for qh in range(8):
            kv = qh // 2
            par = qh % 2
            widths = [512 if b < nb - 1 else lastw for b in range(nb)]
            for b in range(nb):
                w = widths[b]
                bk = banks[b % 4]
                p.mm(bk, q_sb, k_res[kv], start=True, stop=True,
                     lap=q_sb.ap[:, qh * 128:(qh + 1) * 128],
                     rap=k_res[kv].ap[:, b * 512:b * 512 + w],
                     oap=bk.ap[:, 0:w])
                if b == nb - 1:
                    p.tt(bk, bk, msks[qt % 4], ALU.add,
                         oap=bk.ap[:, 0:lastw], aap=bk.ap[:, 0:lastw],
                         bap=msks[qt % 4].ap[:, 0:lastw])
                pe = p_exp4[b // 2]
                po = (b % 2) * 512
                p.copy(pe, bk, oap=pe.ap[:, po:po + w],
                       aap=bk.ap[:, 0:w], eng=p.scal)
            for b in range(nb):
                pe = p_exp4[b // 2]
                po = (b % 2) * 512
                dst = nmA[par] if b == 0 else nmN[par]
                p.reduce(dst, pe, ALU.max, negate=True,
                         aap=pe.ap[:, po:po + widths[b]])
                if b > 0:
                    p.tt(nmA[par], nmA[par], nmN[par], ALU.min)
            p.ts(nmS[par], nmA[par], 1.0 / math.sqrt(128.0), ALU.mult)
            for b in range(nb):
                w = widths[b]
                pe = p_exp4[b // 2]
                po = (b % 2) * 512
                p.act(pe, pe, AF.Exp, bias=nmS[par].ap[:, 0:1],
                      scale=1.0 / math.sqrt(128.0),
                      accum=(lsum[par] if b == 0 else lB[par]),
                      accum_ap=(lsum[par] if b == 0 else lB[par]).ap[:, 0:1],
                      oap=pe.ap[:, po:po + w],
                      aap=pe.ap[:, po:po + w],
                      extra_reads=[nmS[par]])
                if b > 0:
                    p.tt(lsum[par], lsum[par], lB[par], ALU.add)
            tbk = banks[4 + par]
            abk = banks[6 + par]
            nsub = (qt + 1)
            first = True
            for s_i in range(nsub):
                pe = p_exp4[s_i // 8]
                so = (s_i % 8) * 128
                p.op(p.tens, lambda e, pe=pe, so=so, tbk=tbk: e.transpose(
                    tbk.ap[:, 0:128],
                    pe.ap[:, so:so + 128],
                    sh.ident_f32.ap), reads=[pe, sh.ident_f32], writes=[tbk])
                p.copy(pT[par], tbk, aap=tbk.ap[:, 0:128])
                p.mm(abk, pT[par], v_sb[s_i],
                     start=first, stop=(s_i == nsub - 1),
                     rap=v_sb[s_i].ap[:, kv * 128:(kv + 1) * 128],
                     oap=abk.ap[:, 0:128], inc=True)
                first = False
            p.op(p.vect, lambda e, par=par: e.reciprocal(
                out=lsum[par].ap, in_=lsum[par].ap),
                reads=[lsum[par]], writes=[lsum[par]])
            p.ts(att_tok, abk, lsum[par].ap[:, 0:1], ALU.mult,
                 aap=abk.ap[:, 0:128],
                 oap=att_tok.ap[:, qh * 128:(qh + 1) * 128],
                 extra_reads=[lsum[par]])
        co = (qt % 4) * 128
        for dd in range(DT):
            p.op(p.tens, lambda e, dd=dd: e.transpose(
                banks[0].ap[:, 0:128], att_tok.ap[:, dd * 128:(dd + 1) * 128],
                sh.ident_f32.ap), reads=[att_tok, sh.ident_f32], writes=[banks[0]])
            p.copy(att_fm[dd], banks[0], oap=att_fm[dd].ap[:, co:co + 128],
                   aap=banks[0].ap[:, 0:128])
        if qt % 4 == 3:
            tb = qt // 4
            xs.load(x_in, tb, buf=0)
            xf = xs.x[0]
            for m in range(8):
                ps = p.pbank(0 if m % 2 == 0 else 1)
                ar = wstr[m % 2]
                p.op(p.qsync, lambda e, ar=ar, m=m: e.dma_start(
                    out=ar.ap[:, 0:1024],
                    in_=strip_batch_ap(wo.ap, 0, 8, m * 128, 128, D)),
                    reads=[wo], writes=[ar])
                for k in range(8):
                    p.mm(ps, ar, att_fm[k], start=(k == 0), stop=(k == 7),
                         lap=ar.ap[:, k * 128:(k + 1) * 128])
                p.copy(yb[m], ps, eng=p.scal)
            ln.stats(yb, double=False)
            ln.apply_add(yb, xf, xf)
            store_x(p, xf, x_out, tb, T)


def host_weights(I):
    H = {}
    H["L3_ffn_WiT"] = wT(I["L3_ffn_Wi"])
    H["L3_ffn_WoT"] = wT(I["L3_ffn_Wo"])
    H["L3_ffn_AT"] = np.ascontiguousarray(
        np.transpose(np.asarray(I["L3_ffn_A"], np.float32), (0, 2, 1)))
    H["L3_ffn_BT"] = np.ascontiguousarray(
        np.transpose(np.asarray(I["L3_ffn_B"], np.float32), (0, 2, 1)))
    H["L3_ffn_WgT"] = wT(I["L3_ffn_Wg"])
    H["L2_attn_WqT"] = wT(I["L2_attn_Wq"])
    H["L2_attn_WkT"] = wT(I["L2_attn_Wk"])
    H["L2_attn_WvT"] = wT(I["L2_attn_Wv"])
    H["L2_attn_WoT"] = wT(I["L2_attn_Wo"])
    for nm in ("Wq", "Wk", "Wv", "Wo"):
        H["L3_attn_" + nm] = np.ascontiguousarray(np.asarray(I["L3_attn_" + nm], np.float32))
    H["L0_attn_in_projT"] = wT(I["L0_attn_in_proj_w"])
    H["L0_attn_x_projT"] = wT(I["L0_attn_x_proj_w"])
    H["L0_attn_dt_projT"] = wT(I["L0_attn_dt_proj_w"])
    H["L0_attn_out_projT"] = wT(I["L0_attn_out_proj_w"])
    H["L0_attn_convw"] = np.ascontiguousarray(
        np.asarray(I["L0_attn_conv_w"], np.float32).reshape(2048, 4))
    H["L0_attn_in_proj_b2"] = np.asarray(I["L0_attn_in_proj_b"], np.float32).reshape(-1, 1)
    H["L0_attn_dt_proj_b2"] = np.asarray(I["L0_attn_dt_proj_b"], np.float32).reshape(-1, 1)
    H["L0_attn_conv_b2"] = np.asarray(I["L0_attn_conv_b"], np.float32).reshape(-1, 1)
    H["L0_attn_out_proj_b2"] = np.asarray(I["L0_attn_out_proj_b"], np.float32).reshape(-1, 1)
    H["L0_attn_D_param2"] = np.asarray(I["L0_attn_D_param"], np.float32).reshape(-1, 1)
    H["L0_attn_norm_w2"] = np.asarray(I["L0_attn_norm_w"], np.float32).reshape(-1, 1)
    H["L0_attn_norm_b2"] = np.asarray(I["L0_attn_norm_b"], np.float32).reshape(-1, 1)
    H["L0_attn_A_log"] = np.ascontiguousarray(np.asarray(I["L0_attn_A_log"], np.float32))
    H["L1_attn_in_projT"] = wT(I["L1_attn_in_proj_w"])
    H["L1_attn_out_projT"] = wT(I["L1_attn_out_proj_w"])
    H["L1_attn_convw17"] = np.ascontiguousarray(np.asarray(I["L1_attn_conv_w"], np.float32))
    H["L1_attn_conv_b17"] = np.asarray(I["L1_attn_conv_b"], np.float32).reshape(-1, 1)
    H["L1_attn_norm_w2"] = np.asarray(I["L1_attn_norm_w"], np.float32).reshape(-1, 1)
    H["L1_attn_init_states"] = np.asarray(I["L1_attn_init_states"], np.float32)
    H["L0_ffn_gate_wT"] = wT(I["L0_ffn_gate_w"])
    H["L0_ffn_w1T"] = np.ascontiguousarray(
        np.transpose(np.asarray(I["L0_ffn_w1"], np.float32), (0, 2, 1)).reshape(16 * 1024, 512))
    H["L0_ffn_w2T"] = np.ascontiguousarray(
        np.transpose(np.asarray(I["L0_ffn_w2"], np.float32), (0, 2, 1)).reshape(16 * 512, 1024))
    H["L0_ffn_sfc1T"] = wT(I["L0_ffn_sfc1"])
    H["L0_ffn_sfc2T"] = wT(I["L0_ffn_sfc2"])
    H["L0_ffn_sfc3T"] = wT(I["L0_ffn_sfc3"])
    return H



def emit_l3ffn(p: Prog, sh: Shared, ln: "LNArena", xs: "XStream",
               x_in, x_out, wc: WCache, T=512):
    nt = SEQ // T
    nch = T // 128
    WiT = p.hostw["L3_ffn_WiT"]
    WoT = p.hostw["L3_ffn_WoT"]
    A_T = p.hostw["L3_ffn_AT"]
    B_T = p.hostw["L3_ffn_BT"]
    WgT = p.hostw["L3_ffn_WgT"]
    wid = wc.get("l3f_WiT", WiT)
    wod = wc.get("l3f_WoT", WoT)

    gwc = p.const("l3f_WgT", WgT)
    gstr = [p.sbuf("l3f_gw", (128, 4), F32) for _ in range(8)]
    for k in range(8):
        p.dma(p.qsync, gstr[k], gwc[k * 128:(k + 1) * 128, :])
    ps4 = p.pbank(4)
    for k in range(8):
        p.mm(ps4, sh.ones_col_f32, gstr[k], start=(k == 0), stop=(k == 7),
             oap=ps4.ap[0:1, 0:4])
    gsum_row = p.sbuf("l3f_gsum", (1, 4))
    p.copy(gsum_row, ps4, aap=ps4.ap[0:1, 0:4], eng=p.scal)
    gsum_col = p.sbuf("l3f_gsumc", (4, 1))
    p.op(p.tens, lambda e: e.transpose(
        ps4.ap[0:4, 0:1], gsum_row.ap, sh.ident_f32.ap[0:1, 0:1]),
        reads=[gsum_row, sh.ident_f32], writes=[ps4])
    p.copy(gsum_col, ps4, aap=ps4.ap[0:4, 0:1], eng=p.scal)
    bg_cols = load_cols(p, sh, p.dram_in("L3_ffn_bg", (4,)), 4, "l3f_bg")

    a_c = p.const("l3f_AT", A_T.reshape(4 * 1024, 4))
    astr = [p.sbuf("l3f_astr", (128, 16), BF16) for _ in range(8)]
    for k in range(8):
        p.op(p.qgp, lambda e, k=k: e.dma_start(
            out=astr[k].ap.rearrange("p (e r) -> p e r", e=4),
            in_=a_c.rearrange("(e d) r -> d e r", e=4)[k * 128:(k + 1) * 128]),
            writes=[astr[k]])
    b_c = p.const("l3f_BT", B_T.reshape(16, 4096))
    bstr = [p.sbuf("l3f_bstr", (4, 4096), BF16) for _ in range(4)]
    for e_i in range(4):
        p.dma(p.qgp, bstr[e_i], b_c[e_i * 4:(e_i + 1) * 4, :])

    ddr_low = p.dram_tmp_buf("l3f_lowddr", (16, T))
    ddr_w = p.dram_tmp_buf("l3f_wddr", (4, T))
    glog = p.sbuf("l3f_glog", (4, T))
    gl_t1 = p.sbuf("l3f_glt1", (4, T))
    gl_t2 = p.sbuf("l3f_glt2", (4, T))
    gl_t0 = p.sbuf("l3f_glt0", (4, T))
    low_all = p.sbuf("l3f_lowall", (16, T))
    low_e = [p.sbuf("l3f_lowe", (4, T), BF16) for _ in range(4)]
    lgt = p.sbuf("l3f_lgt", (128, 4))
    m1n = p.sbuf("l3f_m1n", (128, 1))
    m2 = p.sbuf("l3f_m2", (128, 1))
    ssum = p.sbuf("l3f_ssum", (128, 1))
    stok = p.sbuf("l3f_stok", (128, 4))
    keep = p.sbuf("l3f_keep", (128, 4))
    mask4 = p.sbuf("l3f_mask4", (128, 4))
    w_fm = p.sbuf("l3f_wfm", (4, T))
    w_flat = p.sbuf("l3f_wflat", (1, 4 * T), BF16)
    xn = [p.sbuf("l3f_xn", (128, T), BF16) for _ in range(DT)]
    base_sb2 = [p.sbuf("l3f_base", (128, T), BF16) for _ in range(2)]
    hscr = [p.sbuf("l3f_h", (128, T), BF16) for _ in range(2)]
    htmp = p.sbuf("l3f_ht", (128, T))
    htmp2 = p.sbuf("l3f_ht2", (128, T))
    wbc = [p.sbuf("l3f_wbc", (128, T), BF16) for _ in range(4)]
    eprod = [p.sbuf("l3f_ep", (128, T), BF16) for _ in range(4)]
    hsum = [p.sbuf("l3f_hsum", (128, T), BF16) for _ in range(32)]
    wi_ar = [p.sbuf("l3f_wiar", (128, 1024), BF16) for _ in range(2)]
    wo_ar = [p.sbuf("l3f_woar", (128, 1024), BF16) for _ in range(2)]
    yb = [p.sbuf("l3f_y", (128, T), BF16) for _ in range(DT)]

    for tb in range(nt):
        xs.load(x_in, tb, buf=0)
        xf, xb = xs.x[0], xs.xb[0]
        ln.stats(xb, double=True)
        ln.apply(xb, xn)
        psg = p.pbank(3)
        for k in range(8):
            p.mm(psg, gstr[k], xf[k], start=(k == 0), stop=(k == 7),
                 oap=psg.ap[0:4, :])
        p.copy(gl_t0, psg, aap=psg.ap[0:4, :], eng=p.scal)
        p.tt(gl_t1, gl_t0, ln.ps_BA, ALU.mult, bap=ln.ps_BA.ap[0:4, :])
        p.ts(gl_t2, ln.ps_BB, gsum_col.ap[:, 0:1], ALU.mult,
             aap=ln.ps_BB.ap[0:4, :], extra_reads=[gsum_col])
        p.tt(glog, gl_t1, gl_t2, ALU.subtract)
        p.ts(glog, glog, bg_cols[0].ap[0:4, 0:1], ALU.add,
             extra_reads=[bg_cols[0]])
        psl = p.pbank(3)
        for k in range(8):
            p.mm(psl, astr[k], xn[k], start=(k == 0), stop=(k == 7),
                 oap=psl.ap[0:16, :])
        p.copy(low_all, psl, aap=psl.ap[0:16, :], eng=p.scal)
        p.dma(p.qgp, ddr_low, low_all)
        for e_i in range(4):
            p.dma(p.qgp, low_e[e_i], ddr_low.ap[e_i * 4:(e_i + 1) * 4, :],
                  reads=[ddr_low])
        ps3 = p.pbank(3)
        for c in range(nch):
            p.op(p.tens, lambda e, c=c: e.transpose(
                ps3.ap[:, 0:4], glog.ap[:, c * 128:(c + 1) * 128],
                sh.ident_f32.ap[0:4, 0:4]), reads=[glog, sh.ident_f32], writes=[ps3])
            p.copy(lgt, ps3, aap=ps3.ap[:, 0:4])
            p.reduce(m1n, lgt, ALU.max, negate=True)
            p.act(stok, lgt, AF.Exp, bias=m1n.ap[:, 0:1], extra_reads=[m1n])
            p.reduce(ssum, stok, ALU.add)
            p.op(p.vect, lambda e: e.reciprocal(out=ssum.ap, in_=ssum.ap),
                 reads=[ssum], writes=[ssum])
            p.ts(stok, stok, ssum.ap[:, 0:1], ALU.mult, extra_reads=[ssum])
            p.reduce(m1n, stok, ALU.max, negate=True)
            p.ts(keep, stok, m1n.ap[:, 0:1], ALU.add, extra_reads=[m1n])
            p.ts(keep, keep, 0.0, ALU.is_equal)
            p.stt(mask4, keep, -1e30, stok, ALU.mult, ALU.add)
            p.reduce(m2, mask4, ALU.max)
            p.ts(keep, stok, m2.ap[:, 0:1], ALU.is_ge, extra_reads=[m2])
            p.tt(stok, stok, keep, ALU.mult)
            p.op(p.tens, lambda e, c=c: e.transpose(
                ps3.ap[0:4, 0:128], stok.ap, sh.ident_f32.ap),
                reads=[stok, sh.ident_f32], writes=[ps3])
            p.copy(w_fm, ps3, oap=w_fm.ap[:, c * 128:(c + 1) * 128],
                   aap=ps3.ap[0:4, 0:128], eng=p.scal)
        p.dma(p.qgp, ddr_w, w_fm)
        p.dma(p.qgp, w_flat, ddr_w.ap.rearrange("a b -> (a b)"), reads=[ddr_w])
        ps_w = p.pbank(2)
        for e_i in range(4):
            p.mm(ps_w, sh.ones_row_bf, w_flat, start=True, stop=True,
                 rap=w_flat.ap[0:1, e_i * T:(e_i + 1) * T])
            p.copy(wbc[e_i], ps_w, eng=p.scal)
        for mf in range(32):
            ar = wi_ar[mf % 2]
            ps_base = p.pbank(0 if mf % 2 == 0 else 2)
            p.dma(p.qsync, ar, strip_batch_ap(wid.ap, 0, 8, mf * 128, 128, 4096),
                  reads=[wid])
            for k in range(8):
                p.mm(ps_base, ar, xn[k], start=(k == 0), stop=(k == 7),
                     lap=ar.ap[:, k * 128:(k + 1) * 128])
            bsb = base_sb2[mf % 2]
            p.copy(bsb, ps_base, eng=p.scal)
            for e_i in range(4):
                ps_e = p.pbank(1 if e_i % 2 == 0 else 3)
                p.mm(ps_e, sh.ident_bf, bsb, start=True, stop=False)
                p.mm(ps_e, bstr[e_i], low_e[e_i], start=False, stop=True,
                     lap=bstr[e_i].ap[:, mf * 128:(mf + 1) * 128])
                hb = hscr[e_i % 2]
                p.act(hb, ps_e, AF.Relu)
                p.tt(eprod[e_i], hb, wbc[e_i], ALU.mult,
                     eng=(p.vect if e_i < 2 else p.pool))
            p.tt(htmp, eprod[0], eprod[1], ALU.add)
            p.tt(htmp2, eprod[2], eprod[3], ALU.add, eng=p.pool)
            p.tt(hsum[mf], htmp, htmp2, ALU.add)
        banks = [p.pbank(i) for i in range(8)]
        for k in range(32):
            ks = wo_ar[k % 2]
            p.dma(p.qsync, ks, wod.ap[k * 128:(k + 1) * 128, :], reads=[wod])
            for m in range(8):
                p.mm(banks[m], ks, hsum[k], start=(k == 0), stop=(k == 31),
                     lap=ks.ap[:, m * 128:(m + 1) * 128], inc=(m == 7))
        for m in range(8):
            p.copy(yb[m], banks[m], eng=p.scal)
        ln.stats(yb, double=False)
        ln.apply_add(yb, xf, xf)
        store_x(p, xf, x_out, tb, T)



def emit_l2attn(p: Prog, sh: Shared, ln: "LNArena", xs: "XStream",
                x_in, x_out, wc: WCache, T=512):
    nt = SEQ // T
    wq = wc.get("l2a_WqT", p.hostw["L2_attn_WqT"])
    wk = wc.get("l2a_WkT", p.hostw["L2_attn_WkT"])
    wv = wc.get("l2a_WvT", p.hostw["L2_attn_WvT"])
    wo = wc.get("l2a_WoT", p.hostw["L2_attn_WoT"])
    bo_cols = load_cols(p, sh, p.dram_in("L2_attn_bo", (D,)), D, "l2a_bo")

    q_ddr = p.dram_tmp_buf("l2a_q", (D, SEQ), BF16)
    k_ddr = p.dram_tmp_buf("l2a_k", (D, SEQ + 64), BF16)
    v_ddr = p.dram_tmp_buf("l2a_v", (SEQ + 64, D), BF16)

    band = np.full((128, 192), -1e30, np.float32)
    for r in range(128):
        band[r, r:r + 64] = 0.0
    band_sb = p.sbuf("l2a_band", (128, 192))
    p.dma(p.qsync, band_sb, p.const("c_l2a_band", band))

    xn = [p.sbuf("l2a_xn", (128, T), BF16) for _ in range(DT)]
    wstr = [p.sbuf("l2a_wstr", (128, 4096), BF16) for _ in range(2)]
    ev = [p.sbuf("l2a_ev", (128, 512), BF16) for _ in range(2)]
    zero_sb = p.sbuf("l2a_zero", (128, 1024), BF16)
    p.memset(zero_sb, 0.0)
    for kk in range(8):
        p.dma(p.qgp, k_ddr.ap[kk * 128:(kk + 1) * 128, 0:32], zero_sb.ap[:, 0:32],
              reads=[zero_sb], writes=[k_ddr])
        p.dma(p.qgp, k_ddr.ap[kk * 128:(kk + 1) * 128, SEQ + 32:SEQ + 64],
              zero_sb.ap[:, 0:32], reads=[zero_sb], writes=[k_ddr])
    p.dma(p.qgp, v_ddr.ap[0:32, :], zero_sb.ap[0:32, :], reads=[zero_sb], writes=[v_ddr])
    p.dma(p.qgp, v_ddr.ap[SEQ + 32:SEQ + 64, :], zero_sb.ap[0:32, :],
          reads=[zero_sb], writes=[v_ddr])

    for tb in range(nt):
        t0 = tb * T
        xs.load(x_in, tb, buf=0)
        xf, xb = xs.x[0], xs.xb[0]
        ln.stats(xb, double=False)
        ln.apply(xb, xn)
        for (wd, dst, coff) in ((wq, q_ddr, 0), (wk, k_ddr, 32)):
            for m in range(8):
                ps = p.pbank(m % 2)
                ar = wstr[m % 2]
                p.op(p.qsync, lambda e, ar=ar, wd=wd, m=m: e.dma_start(
                    out=ar.ap[:, 0:1024],
                    in_=strip_batch_ap(wd.ap, 0, 8, m * 128, 128, D)),
                    reads=[wd], writes=[ar])
                for k in range(8):
                    p.mm(ps, ar, xn[k], start=(k == 0), stop=(k == 7),
                         lap=ar.ap[:, k * 128:(k + 1) * 128])
                e = ev[m % 2]
                p.copy(e, ps, eng=p.scal)
                p.dma(p.qgp, dst.ap[m * 128:(m + 1) * 128, coff + t0:coff + t0 + T],
                      e, writes=[dst])
        for tc in range(T // 128):
            for half in range(2):
                ps = p.pbank(2 + (tc * 2 + half) % 2)
                ar = wstr[(tc * 2 + half) % 2]
                p.dma(p.qsync, ar,
                      strip_batch_ap(wv.ap, 0, 8, half * 512, 512, D),
                      reads=[wv])
                for k in range(8):
                    p.mm(ps, xn[k], ar, start=(k == 0), stop=(k == 7),
                         lap=xn[k].ap[:, tc * 128:(tc + 1) * 128],
                         rap=ar.ap[:, k * 512:(k + 1) * 512])
                e = ev[(tc * 2 + half) % 2]
                p.copy(e, ps, eng=p.scal)
                p.dma(p.qgp, v_ddr.ap[32 + t0 + tc * 128:32 + t0 + (tc + 1) * 128,
                                      half * 512:(half + 1) * 512],
                      e, writes=[v_ddr])

    q_sb = p.sbuf("l2a_qsb", (128, 8 * 128), BF16)
    k_sb = p.sbuf("l2a_ksb", (128, 8 * 192), BF16)
    v_sb1 = p.sbuf("l2a_vsb1", (128, 1024), BF16)
    v_sb2 = p.sbuf("l2a_vsb2", (128, 1024), BF16)
    sc_m = [p.sbuf("l2a_scm", (128, 192)) for _ in range(2)]
    negmax = [p.sbuf("l2a_negmax", (128, 1)) for _ in range(2)]
    rowsum = p.sbuf("l2a_rowsum", (128, 16))
    p_exp = [p.sbuf("l2a_pexp", (128, 192)) for _ in range(2)]
    pT_sb = [p.sbuf("l2a_pT", (128, 256), BF16) for _ in range(2)]
    att_tok = p.sbuf("l2a_atok", (128, 1024))
    att_fm = [p.sbuf("l2a_afm", (128, T), BF16) for _ in range(DT)]
    yb = [p.sbuf("l2a_y", (128, T), BF16) for _ in range(DT)]
    for qt in range(32):
        p.dma(p.qsync, q_sb,
              q_ddr.ap[:, qt * 128:(qt + 1) * 128].rearrange("(k p) m -> p k m", p=128),
              reads=[q_ddr])
        p.dma(p.qsync, k_sb,
              k_ddr.ap[:, qt * 128:qt * 128 + 192].rearrange("(k p) m -> p k m", p=128),
              reads=[k_ddr])
        p.dma(p.qsync, v_sb1, v_ddr.ap[qt * 128:(qt + 1) * 128, :], reads=[v_ddr])
        p.op(p.qsync, lambda e, qt=qt: e.dma_start(
            out=v_sb2.ap[0:64, :],
            in_=v_ddr.ap[(qt + 1) * 128:(qt + 1) * 128 + 64, :]),
            reads=[v_ddr], writes=[v_sb2])
        ps_o = [p.pbank(2), p.pbank(3)]
        for h in range(16):
            pb = (h % 2) * 64
            ks = h // 2
            par = h % 2
            ps_s = p.pbank(par)
            ps_t = p.pbank(4 + par)
            scm, nmx, pex, pts = sc_m[par], negmax[par], p_exp[par], pT_sb[par]
            p.mm(ps_s, q_sb, k_sb, start=True, stop=True,
                 lap=q_sb.ap[pb:pb + 64, ks * 128:(ks + 1) * 128],
                 rap=k_sb.ap[pb:pb + 64, ks * 192:(ks + 1) * 192],
                 oap=ps_s.ap[:, 0:192])
            p.stt(scm, ps_s, 0.125, band_sb, ALU.mult, ALU.add,
                  aap=ps_s.ap[:, 0:192])
            p.reduce(nmx, scm, ALU.max, negate=True)
            p.act(pex, scm, AF.Exp, bias=nmx.ap[:, 0:1],
                  accum=rowsum, accum_ap=rowsum.ap[:, h:h + 1],
                  extra_reads=[nmx])
            p.op(p.tens, lambda e, ps_t=ps_t, pex=pex: e.transpose(
                ps_t.ap[:, 0:128], pex.ap[:, 0:128], sh.ident_f32.ap),
                reads=[pex, sh.ident_f32], writes=[ps_t])
            p.copy(pts, ps_t, oap=pts.ap[:, 0:128], aap=ps_t.ap[:, 0:128])
            p.op(p.tens, lambda e, ps_t=ps_t, pex=pex: e.transpose(
                ps_t.ap[0:64, 128:256], pex.ap[:, 128:192], sh.ident_f32.ap),
                reads=[pex, sh.ident_f32], writes=[ps_t])
            p.copy(pts, ps_t, oap=pts.ap[0:64, 128:256],
                   aap=ps_t.ap[0:64, 128:256], eng=p.scal)
            po = ps_o[h // 8]
            oc = (h % 8) * 64
            p.mm(po, pts, v_sb1, start=True, stop=False,
                 lap=pts.ap[:, 0:128], rap=v_sb1.ap[:, h * 64:(h + 1) * 64],
                 oap=po.ap[:, oc:oc + 64])
            p.mm(po, pts, v_sb2, start=False, stop=True,
                 lap=pts.ap[0:64, 128:256], rap=v_sb2.ap[0:64, h * 64:(h + 1) * 64],
                 oap=po.ap[:, oc:oc + 64])
        p.op(p.vect, lambda e: e.reciprocal(out=rowsum.ap, in_=rowsum.ap),
             reads=[rowsum], writes=[rowsum])
        for h in range(16):
            po = ps_o[h // 8]
            oc = (h % 8) * 64
            p.ts(att_tok, po, rowsum.ap[:, h:h + 1], ALU.mult,
                 aap=po.ap[:, oc:oc + 64], oap=att_tok.ap[:, h * 64:(h + 1) * 64],
                 extra_reads=[rowsum])
        co = (qt % 4) * 128
        for dd in range(DT):
            pso = p.pbank(4 + dd % 2)
            p.op(p.tens, lambda e, dd=dd, pso=pso: e.transpose(
                pso.ap[:, 0:128], att_tok.ap[:, dd * 128:(dd + 1) * 128],
                sh.ident_f32.ap), reads=[att_tok, sh.ident_f32], writes=[pso])
            p.copy(att_fm[dd], pso, oap=att_fm[dd].ap[:, co:co + 128],
                   aap=pso.ap[:, 0:128],
                   eng=(p.scal if dd % 2 else p.vect))
        if qt % 4 == 3:
            tb = qt // 4
            xs.load(x_in, tb, buf=0)
            xf = xs.x[0]
            for m in range(8):
                ps = p.pbank(0 if m % 2 == 0 else 1)
                ar = wstr[m % 2]
                p.op(p.qsync, lambda e, ar=ar, m=m: e.dma_start(
                    out=ar.ap[:, 0:1024],
                    in_=strip_batch_ap(wo.ap, 0, 8, m * 128, 128, D)),
                    reads=[wo], writes=[ar])
                for k in range(8):
                    p.mm(ps, ar, att_fm[k], start=(k == 0), stop=(k == 7),
                         lap=ar.ap[:, k * 128:(k + 1) * 128])
                p.ts(yb[m], ps, bo_cols[m].ap[:, 0:1], ALU.add,
                     extra_reads=[bo_cols[m]])
            ln.stats(yb, double=False)
            ln.apply_add(yb, xf, xf)
            store_x(p, xf, x_out, tb, T)


def emit_l0attn(p: Prog, sh: Shared, ln: "LNArena", xs: "XStream",
                x_in, x_out, wc: WCache, T=512):
    nt = SEQ // T
    DI = 2048
    NS = 16
    ipw = wc.get("l0a_ipT", p.hostw["L0_attn_in_projT"])
    xpw = wc.get("l0a_xpT", p.hostw["L0_attn_x_projT"])
    dtw = wc.get("l0a_dtT", p.hostw["L0_attn_dt_projT"])
    opw = wc.get("l0a_opT", p.hostw["L0_attn_out_projT"])

    def colload(name, arr2d, mw):
        c = p.const(name, np.ascontiguousarray(arr2d.astype(np.float32)))
        nk = arr2d.shape[0] // 128
        b = p.sbuf(name + "_sb", (128, nk * mw))
        p.op(p.qsync, lambda e: e.dma_start(
            out=b.ap, in_=strip_batch_ap(c, 0, nk, 0, mw, mw),
            allow_slow_non_contiguous=True), writes=[b])
        return b

    convw = colload("l0a_convw", p.hostw["L0_attn_convw"], 4)
    ipb = colload("l0a_ipb", p.hostw["L0_attn_in_proj_b2"], 1)
    dtb = colload("l0a_dtb", p.hostw["L0_attn_dt_proj_b2"], 1)
    cvb = colload("l0a_cvb", p.hostw["L0_attn_conv_b2"], 1)
    opb = colload("l0a_opb", p.hostw["L0_attn_out_proj_b2"], 1)
    Dpar = colload("l0a_D", p.hostw["L0_attn_D_param2"], 1)
    nrmw = colload("l0a_nw", p.hostw["L0_attn_norm_w2"], 1)
    nrmb = colload("l0a_nb", p.hostw["L0_attn_norm_b2"], 1)
    alog = colload("l0a_alog", p.hostw["L0_attn_A_log"], NS)
    xpb_c = load_cols(p, sh, p.dram_in("L0_attn_x_proj_b", (32,)), 32, "l0a_xpb")
    negA = p.sbuf("l0a_negA", (128, 16 * NS))
    p.act(negA, alog, AF.Exp)
    negdtb = p.sbuf("l0a_negdtb", (128, 16))
    p.ts(negdtb, dtb, -1.0, ALU.mult)
    neg_ones_bf = p.sbuf("l0a_negones", (1, 128), BF16)
    p.dma(p.qsync, neg_ones_bf, p.const("c_neg_ones_bf", np.full((1, 128), -1.0, BF_NP)))

    xpstr = p.sbuf("l0a_xpstr", (128, 16 * 32), BF16)
    p.op(p.qgp, lambda e: e.dma_start(
        out=xpstr.ap, in_=strip_batch_ap(xpw.ap, 0, 16, 0, 32, 32)),
        reads=[xpw], writes=[xpstr])

    h_sb = [p.sbuf("l0a_h", (128, NS)) for _ in range(16)]
    for _b in h_sb:
        p.memset(_b, 0.0)
    tail = p.sbuf("l0a_tail", (128, 16 * 3))
    p.memset(tail, 0.0)

    xn = [p.sbuf("l0a_xn", (128, T), BF16) for _ in range(DT)]
    xc = [p.sbuf("l0a_xc", (128, T + 3)) for _ in range(16)]
    z = [p.sbuf("l0a_z", (128, T), BF16) for _ in range(16)]
    u = [p.sbuf("l0a_u", (128, T), BF16) for _ in range(16)]
    du = [p.sbuf("l0a_du", (128, T), BF16) for _ in range(16)]
    y = [p.sbuf("l0a_y", (128, T)) for _ in range(16)]
    ar2 = [p.sbuf("l0a_ar", (128, 2048), BF16) for _ in range(2)]
    ari = ar2
    cscr = p.sbuf("l0a_cscr", (128, T))
    dA = [p.sbuf("l0a_dA", (128, T)) for _ in range(3)]
    dBu = [p.sbuf("l0a_dBu", (128, T)) for _ in range(3)]
    hs = [p.sbuf("l0a_hs", (128, T)) for _ in range(3)]
    cscr2 = [cscr, p.sbuf("l0a_cscr2", (128, T))]
    psb_sb = [p.sbuf("l0a_psbs", (128, T), BF16)]
    psc_sb = [p.sbuf("l0a_pscs", (128, T), BF16)]
    sz = p.sbuf("l0a_sz", (128, T), BF16)
    ddr_bc = p.dram_tmp_buf("l0a_bc", (32, T), BF16)
    bc_sb = p.sbuf("l0a_bcsb", (32, T), BF16)
    flatB = p.sbuf("l0a_flatB", (1, 2 * T), BF16)
    flatC = p.sbuf("l0a_flatC", (1, 2 * T), BF16)
    yo = xn

    z1s = [sz, sz]
    for tb in range(nt):
        xs.load(x_in, tb, buf=0)
        xf, xb = xs.x[0], xs.xb[0]
        ln.stats(xb, double=True, also_single=True)
        ln.apply(xb, xn)
        for k in range(DT):
            p.ts(xn[k], xn[k], nrmw.ap[:, k:k + 1], ALU.mult,
                 scalar2=nrmb.ap[:, k:k + 1], op1=ALU.add,
                 extra_reads=[nrmw, nrmb])
        for m in range(16):
            if tb == 0:
                p.memset(xc[m], 0.0, oap=xc[m].ap[:, 0:3])
            else:
                p.copy(xc[m], tail, oap=xc[m].ap[:, 0:3],
                       aap=tail.ap[:, m * 3:(m + 1) * 3])
        for m in range(32):
            ps = p.pbank(m % 2)
            ar = ari[m % 2]
            p.op(p.qsync, lambda e, ar=ar, m=m: e.dma_start(
                out=ar.ap[:, 0:1024],
                in_=strip_batch_ap(ipw.ap, 0, 8, m * 128, 128, 4096)),
                reads=[ipw], writes=[ar])
            for k in range(DT):
                p.mm(ps, ar, xn[k], start=(k == 0), stop=(k == 7),
                     lap=ar.ap[:, k * 128:(k + 1) * 128])
            if m < 16:
                p.ts(xc[m], ps, ipb.ap[:, m:m + 1], ALU.add,
                     oap=xc[m].ap[:, 3:T + 3], extra_reads=[ipb])
            else:
                p.ts(z[m - 16], ps, ipb.ap[:, m:m + 1], ALU.add,
                     extra_reads=[ipb])
        for m in range(16):
            p.ts(cscr, xc[m], convw.ap[:, m * 4:m * 4 + 1], ALU.mult,
                 aap=xc[m].ap[:, 0:T], extra_reads=[convw])
            for tap in range(1, 4):
                p.stt(cscr, xc[m], convw.ap[:, m * 4 + tap:m * 4 + tap + 1],
                      cscr, ALU.mult, ALU.add, aap=xc[m].ap[:, tap:tap + T],
                      extra_reads=[convw])
            p.act(u[m], cscr, AF.Silu, bias=cvb.ap[:, m:m + 1],
                  extra_reads=[cvb])
            p.copy(tail, xc[m], oap=tail.ap[:, m * 3:(m + 1) * 3],
                   aap=xc[m].ap[:, T:T + 3])
        psx = p.pbank(2)
        for k in range(16):
            p.mm(psx, xpstr, u[k], start=(k == 0), stop=(k == 15),
                 lap=xpstr.ap[:, k * 32:(k + 1) * 32], oap=psx.ap[0:32, :])
        p.ts(bc_sb, psx, xpb_c[0].ap[0:32, 0:1], ALU.add, aap=psx.ap[0:32, :],
             extra_reads=[xpb_c[0]])
        p.dma(p.qgp, ddr_bc, bc_sb)
        for m in range(16):
            ps = p.pbank(m % 2)
            ar = ar2[m % 2]
            p.op(p.qsync, lambda e, ar=ar, m=m: e.dma_start(
                out=ar.ap, in_=strip_batch_ap(dtw.ap, 0, 16, m * 128, 128, 2048)),
                reads=[dtw], writes=[ar])
            for k in range(16):
                p.mm(ps, ar, u[k], start=(k == 0), stop=(k == 15),
                     lap=ar.ap[:, k * 128:(k + 1) * 128])
            p.act(cscr, ps, AF.Sigmoid, scale=-1.0, bias=negdtb.ap[:, m:m + 1],
                  extra_reads=[negdtb])
            p.act(xc[m], cscr, AF.Ln, oap=xc[m].ap[:, 0:T])
            p.tt(du[m], u[m], xc[m], ALU.mult, bap=xc[m].ap[:, 0:T])
        for s in range(NS):
            s4 = s % 2
            if s4 == 0:
                h0 = s
                p.op(p.qgp, lambda e, h0=h0: e.dma_start(
                    out=flatB.ap,
                    in_=ddr_bc.ap[h0:h0 + 2, :].rearrange("a b -> (a b)")),
                    reads=[ddr_bc], writes=[flatB])
                p.op(p.qgp, lambda e, h0=h0: e.dma_start(
                    out=flatC.ap,
                    in_=ddr_bc.ap[16 + h0:16 + h0 + 2, :].rearrange("a b -> (a b)")),
                    reads=[ddr_bc], writes=[flatC])
            psb = p.pbank(2)
            psc = p.pbank(3)
            p.mm(psb, neg_ones_bf, flatB, start=True, stop=True,
                 rap=flatB.ap[0:1, s4 * T:(s4 + 1) * T])
            p.mm(psc, sh.ones_row_bf, flatC, start=True, stop=True,
                 rap=flatC.ap[0:1, s4 * T:(s4 + 1) * T])
            pb_s, pc_s = psb_sb[0], psc_sb[0]
            p.copy(pb_s, psb, eng=p.scal)
            p.copy(pc_s, psc, eng=p.scal)
            for m in range(16):
                j = m % 3
                ea = p.pool if m % 2 == 0 else p.vect
                eb = p.vect if m % 2 == 0 else p.pool
                p.act(dA[j], xc[m], AF.Exp, scale=negA.ap[:, m * NS + s:m * NS + s + 1],
                      aap=xc[m].ap[:, 0:T], extra_reads=[negA])
                p.tt(dBu[j], du[m], pb_s, ALU.mult, eng=ea)
                p.scan(hs[j], dA[j], dBu[j],
                       initial=h_sb[m].ap[:, s:s + 1],
                       extra_reads=[h_sb[m]])
                p.copy(h_sb[m], hs[j], oap=h_sb[m].ap[:, s:s + 1],
                       aap=hs[j].ap[:, T - 1:T], eng=p.scal)
                if s == 0:
                    p.tt(y[m], hs[j], pc_s, ALU.mult, eng=ea)
                else:
                    p.tt(cscr2[m % 2], hs[j], pc_s, ALU.mult, eng=ea)
                    p.tt(y[m], y[m], cscr2[m % 2], ALU.add, eng=eb)
        for m in range(16):
            p.stt(y[m], u[m], Dpar.ap[:, m:m + 1], y[m], ALU.mult, ALU.add,
                  extra_reads=[Dpar])
            p.act(sz, z[m], AF.Silu)
            p.tt(z[m], y[m], sz, ALU.mult)
        ln.bcast_single(p.pbank(2), p.pbank(3))
        for m in range(8):
            ps = p.pbank(m % 2)
            ar = ar2[m % 2]
            p.op(p.qsync, lambda e, ar=ar, m=m: e.dma_start(
                out=ar.ap, in_=strip_batch_ap(opw.ap, 0, 16, m * 128, 128, 1024)),
                reads=[opw], writes=[ar])
            s0 = ln.scr[m % 2]
            p.tt(s0, xb[m], p.pbank(2), ALU.mult)
            p.tt(z1s[m % 2], s0, p.pbank(3), ALU.subtract)
            for k in range(16):
                p.mm(ps, ar, z[k], start=(k == 0), stop=False,
                     lap=ar.ap[:, k * 128:(k + 1) * 128])
            p.mm(ps, sh.ident_bf, z1s[m % 2], start=False, stop=True)
            p.ts(yo[m], ps, opb.ap[:, m:m + 1], ALU.add, extra_reads=[opb])
        ln.stats(yo, double=False)
        ln.apply_add(yo, xf, xf)
        store_x(p, xf, x_out, tb, T)



def emit_l1attn(p: Prog, sh: Shared, ln: "LNArena", xs: "XStream",
                x_in, x_out, wc: WCache, T=512):
    nt = SEQ // T
    NH, PD, NSt, CH = 32, 64, 64, 64
    ipw = wc.get("l1a_ipT", p.hostw["L1_attn_in_projT"])
    opw = wc.get("l1a_opT", p.hostw["L1_attn_out_projT"])

    def colload(name, arr2d, mw):
        c = p.const(name, np.ascontiguousarray(arr2d.astype(np.float32)))
        nk = arr2d.shape[0] // 128
        b = p.sbuf(name + "_sb", (128, nk * mw))
        p.op(p.qsync, lambda e: e.dma_start(
            out=b.ap, in_=strip_batch_ap(c, 0, nk, 0, mw, mw),
            allow_slow_non_contiguous=True), writes=[b])
        return b

    convw = colload("l1a_convw", p.hostw["L1_attn_convw17"], 4)
    cvb = colload("l1a_cvb", p.hostw["L1_attn_conv_b17"], 1)
    nrw = colload("l1a_nrw", p.hostw["L1_attn_norm_w2"], 1)
    dtb_row = p.sbuf("l1a_dtbr", (1, 32))
    p.dma(p.qsync, dtb_row, p.dram_in("L1_attn_dt_bias", (32,)).rearrange("(o n) -> o n", o=1))
    alog_row = p.sbuf("l1a_alogr", (1, 32))
    p.dma(p.qsync, alog_row, p.dram_in("L1_attn_A_log", (32,)).rearrange("(o n) -> o n", o=1))
    D_row = p.sbuf("l1a_Dr", (1, 32))
    p.dma(p.qsync, D_row, p.dram_in("L1_attn_D", (32,)).rearrange("(o n) -> o n", o=1))
    ps4 = p.pbank(4)
    def row_to_col(row, name):
        p.op(p.tens, lambda e: e.transpose(
            ps4.ap[0:32, 0:1], row.ap, sh.ident_f32.ap[0:1, 0:1]),
            reads=[row, sh.ident_f32], writes=[ps4])
        c = p.sbuf(name, (32, 1))
        p.copy(c, ps4, aap=ps4.ap[0:32, 0:1], eng=p.scal)
        return c
    dtb_col = row_to_col(dtb_row, "l1a_dtbc")
    alog_col = row_to_col(alog_row, "l1a_alogc")
    negdtb = p.sbuf("l1a_negdtb", (32, 1))
    p.ts(negdtb, dtb_col, -1.0, ALU.mult)
    negA2 = p.sbuf("l1a_negA2", (32, 1))
    p.act(negA2, alog_col, AF.Exp)
    p.mm(ps4, sh.ones_row_f32, D_row, start=True, stop=True,
         lap=sh.ones_row_f32.ap[0:1, 0:64], oap=ps4.ap[0:64, 0:32])
    Dbc = p.sbuf("l1a_Dbc", (64, 32))
    p.copy(Dbc, ps4, aap=ps4.ap[0:64, 0:32], eng=p.scal)

    t64 = np.zeros((64, 64), np.float32)
    for s_ in range(64):
        t64[s_, s_:] = 1.0
    TRI64 = p.sbuf("l1a_TRI64", (64, 64))
    p.dma(p.qsync, TRI64, p.const("c_l1a_TRI64", t64))
    BD64 = p.sbuf("l1a_BD64", (64, 64))
    p.dma(p.qsync, BD64, p.const("c_l1a_BD64", np.ones((64, 64), np.float32)))
    sel01 = np.zeros((32, 2048), np.float32)
    for h in range(32):
        sel01[h, h * 64:(h + 1) * 64] = 1.0
    SEL01 = p.sbuf("l1a_SEL01", (32, 2048))
    p.dma(p.qsync, SEL01, p.const("c_l1a_SEL01", sel01))
    mbd = np.zeros((64, 64), np.float32)
    for s_ in range(64):
        for l_ in range(64):
            if s_ > l_:
                mbd[s_, l_] = -1e30
    MBD = p.sbuf("l1a_MBD", (64, 64))
    p.dma(p.qsync, MBD, p.const("c_l1a_MBD", mbd))

    S0 = np.transpose(np.asarray(p.hostw["L1_attn_init_states"], np.float32),
                      (2, 0, 1)).reshape(64, 2048)
    S_T = p.sbuf("l1a_ST", (64, 2048))
    p.dma(p.qsync, S_T, p.const("c_l1a_S0", np.ascontiguousarray(S0)))
    S_bf = p.sbuf("l1a_STbf", (64, 2048), BF16)
    p.copy(S_bf, S_T, eng=p.scal)

    xn = [p.sbuf("l1a_xn", (128, T), BF16) for _ in range(DT)]
    z_t = [p.sbuf("l1a_z", (128, T), BF16) for _ in range(16)]
    xc = [p.sbuf("l1a_xc", (128, T + 3), BF16) for _ in range(17)]
    u_x = [p.sbuf("l1a_ux", (128, T), BF16) for _ in range(16)]
    BCs = p.sbuf("l1a_BCs", (128, T), BF16)
    tail = p.sbuf("l1a_tail", (128, 17 * 3), BF16)
    p.memset(tail, 0.0)
    Ldt = p.sbuf("l1a_Ldt", (32, T))
    adt = p.sbuf("l1a_adt", (32, T))
    cscr = p.sbuf("l1a_cscr", (128, T))
    ari = [p.sbuf("l1a_ari", (128, 2048), BF16) for _ in range(2)]
    acsT = p.sbuf("l1a_acsT", (32, 64))
    adt_c = p.sbuf("l1a_adtc", (64, 32))
    acs_c = p.sbuf("l1a_acsc", (64, 32))
    acsF_c = p.sbuf("l1a_acsFc", (64, 32))
    sdo_c = p.sbuf("l1a_sdoc", (64, 32))
    ds_c = p.sbuf("l1a_dsc", (64, 32))
    dt_c = p.sbuf("l1a_dtc", (64, 32))
    X_c = p.sbuf("l1a_Xc", (64, 2048), BF16)
    Xds_c = p.sbuf("l1a_Xdsc", (64, 2048), BF16)
    x_c = p.sbuf("l1a_xc_tok", (64, 2048), BF16)
    B_c = p.sbuf("l1a_Bc", (64, 64), BF16)
    acl_fm = p.sbuf("l1a_acl", (32, T // CH))
    R1 = p.sbuf("l1a_R1", (32, 2048))
    CBs = p.sbuf("l1a_CBs", (64, 64))
    pa_u = [p.sbuf("l1a_pau", (64, 512)) for _ in range(2)]
    pa_e = [p.sbuf("l1a_pae", (64, 512)) for _ in range(2)]

    mt_all = p.sbuf("l1a_mt", (64, 2048), BF16)
    dec_row = p.sbuf("l1a_decrow", (1, 2048))
    Y_c = p.sbuf("l1a_Yc", (64, 2048))
    C_fm = p.sbuf("l1a_Cfm", (64, T), BF16)
    f32scr = p.sbuf("l1a_f32scr", (128, 128))
    f32scr2 = [f32scr, p.sbuf("l1a_f32scrB", (128, 128))]
    y_fm = [p.sbuf("l1a_yfm", (128, T), BF16) for _ in range(16)]
    for _b in y_fm:
        p.memset(_b, 0.0)
    sq16 = [p.sbuf("l1a_sq16", (128, T), BF16)] * 2
    yo = xn
    nch2 = T // 128

    for tb in range(nt):
        t0 = tb * T
        xs.load(x_in, tb, buf=0)
        xf, xb = xs.x[0], xs.xb[0]
        ln.stats(xb, double=False)
        ln.apply(xb, xn)
        for m in range(17):
            if tb == 0:
                p.memset(xc[m], 0.0, oap=xc[m].ap[:, 0:3])
            else:
                p.copy(xc[m], tail, oap=xc[m].ap[:, 0:3],
                       aap=tail.ap[:, m * 3:(m + 1) * 3])
        for m in range(34):
            ps = p.pbank(m % 2)
            ar = ari[m % 2]
            mw = 128 if m < 33 else 32
            p.op(p.qsync, lambda e, ar=ar, m=m, mw=mw: e.dma_start(
                out=ar.ap[:, 0:8 * mw],
                in_=strip_batch_ap(ipw.ap, 0, 8, m * 128, mw, 4256)),
                reads=[ipw], writes=[ar])
            for k in range(DT):
                p.mm(ps, ar, xn[k], start=(k == 0), stop=(k == 7),
                     lap=ar.ap[:, k * mw:(k + 1) * mw],
                     oap=ps.ap[0:mw, :] if mw != 128 else None)
            if m < 16:
                p.copy(z_t[m], ps, eng=p.scal)
            elif m < 32:
                p.copy(xc[m - 16], ps, oap=xc[m - 16].ap[:, 3:T + 3], eng=p.scal)
            elif m == 32:
                p.copy(xc[16], ps, oap=xc[16].ap[:, 3:T + 3], eng=p.scal)
            else:
                p.act(cscr, ps, AF.Sigmoid, scale=-1.0,
                      bias=negdtb.ap[:, 0:1], aap=ps.ap[0:32, :],
                      oap=cscr.ap[0:32, :], extra_reads=[negdtb])
                p.act(Ldt, cscr, AF.Ln, aap=cscr.ap[0:32, :])
                p.ts(adt, Ldt, negA2.ap[:, 0:1], ALU.mult, extra_reads=[negA2])
        for m in range(17):
            p.ts(cscr, xc[m], convw.ap[:, m * 4:m * 4 + 1], ALU.mult,
                 aap=xc[m].ap[:, 0:T], extra_reads=[convw])
            for tap in range(1, 4):
                p.stt(cscr, xc[m], convw.ap[:, m * 4 + tap:m * 4 + tap + 1],
                      cscr, ALU.mult, ALU.add, aap=xc[m].ap[:, tap:tap + T],
                      extra_reads=[convw])
            dst = u_x[m] if m < 16 else BCs
            p.act(dst, cscr, AF.Silu, bias=cvb.ap[:, m:m + 1],
                  extra_reads=[cvb])
            p.copy(tail, xc[m], oap=tail.ap[:, m * 3:(m + 1) * 3],
                   aap=xc[m].ap[:, T:T + 3])
        p.reduce(acl_fm, adt, ALU.add,
                 aap=adt.ap.rearrange("h (c l) -> h c l", l=CH))
        p.copy(C_fm, BCs, aap=BCs.ap[64:128, :])
        for c in range(T // CH):
            ct0 = c * CH
            ps6 = p.pbank(6)
            ps5 = p.pbank(5)
            p.op(p.tens, lambda e, ct0=ct0: e.transpose(
                ps6.ap[0:64, 0:32], adt.ap[0:32, ct0:ct0 + CH],
                sh.ident_f32.ap[0:32, 0:32]), reads=[adt, sh.ident_f32],
                writes=[ps6])
            p.copy(adt_c, ps6, aap=ps6.ap[0:64, 0:32])
            p.mm(ps5, TRI64, adt_c, start=True, stop=True, oap=ps5.ap[0:64, 0:32])
            p.copy(acs_c, ps5, aap=ps5.ap[0:64, 0:32])
            p.mm(ps5, BD64, adt_c, start=True, stop=True, oap=ps5.ap[0:64, 0:32])
            p.copy(acsF_c, ps5, aap=ps5.ap[0:64, 0:32])
            p.act(sdo_c, acs_c, AF.Exp)
            p.tt(ds_c, acsF_c, acs_c, ALU.subtract)
            p.act(ds_c, ds_c, AF.Exp)
            p.op(p.tens, lambda e: e.transpose(
                ps6.ap[0:32, 0:64], acs_c.ap, sh.ident_f32.ap[0:64, 0:64]),
                reads=[acs_c, sh.ident_f32], writes=[ps6])
            p.copy(acsT, ps6, aap=ps6.ap[0:32, 0:64], eng=p.scal)
            p.op(p.tens, lambda e, ct0=ct0: e.transpose(
                ps6.ap[0:64, 0:32], Ldt.ap[0:32, ct0:ct0 + CH],
                sh.ident_f32.ap[0:32, 0:32]), reads=[Ldt, sh.ident_f32],
                writes=[ps6])
            p.ts(dt_c, ps6, -1.0, ALU.mult, aap=ps6.ap[0:64, 0:32])
            for m in range(16):
                fs = f32scr2[m % 2]
                pst = p.pbank(5 if m % 2 else 6)
                p.copy(fs, u_x[m], aap=u_x[m].ap[:, ct0:ct0 + CH],
                       oap=fs.ap[:, 0:64])
                p.op(p.tens, lambda e, m=m, pst=pst, fs=fs: e.transpose(
                    pst.ap[0:64, 0:128], fs.ap[:, 0:64], sh.ident_f32.ap),
                    reads=[fs, sh.ident_f32], writes=[pst])
                p.copy(x_c, pst, oap=x_c.ap[:, m * 128:(m + 1) * 128],
                       aap=pst.ap[0:64, 0:128],
                       eng=(p.scal if m % 2 else p.vect))
            x3 = x_c.ap.rearrange("p (h s) -> p h s", h=32)
            p.tt(X_c, x_c, dt_c, ALU.mult,
                 oap=X_c.ap.rearrange("p (h s) -> p h s", h=32), aap=x3,
                 bap=dt_c.ap.unsqueeze(2).broadcast_to((64, 32, 64)))
            p.tt(Xds_c, X_c, ds_c, ALU.mult,
                 oap=Xds_c.ap.rearrange("p (h s) -> p h s", h=32),
                 aap=X_c.ap.rearrange("p (h s) -> p h s", h=32),
                 bap=ds_c.ap.unsqueeze(2).broadcast_to((64, 32, 64)))
            p.copy(f32scr, BCs, aap=BCs.ap[0:64, ct0:ct0 + CH],
                   oap=f32scr.ap[0:64, 0:64])
            p.op(p.tens, lambda e: e.transpose(
                ps6.ap[0:64, 0:64], f32scr.ap[0:64, 0:64],
                sh.ident_f32.ap[0:64, 0:64]),
                reads=[f32scr, sh.ident_f32], writes=[ps6])
            p.copy(B_c, ps6, aap=ps6.ap[0:64, 0:64])
            ps4b = p.pbank(4)
            p.mm(ps4b, BCs, C_fm, start=True, stop=True,
                 lap=BCs.ap[0:64, ct0:ct0 + CH],
                 rap=C_fm.ap[:, ct0:ct0 + CH],
                 oap=ps4b.ap[0:64, 0:64])
            p.copy(CBs, ps4b, aap=ps4b.ap[0:64, 0:64], eng=p.scal)
            p.tt(R1, SEL01, acsT, ALU.mult,
                 oap=R1.ap.rearrange("p (h s) -> p h s", h=32),
                 aap=SEL01.ap.rearrange("p (h s) -> p h s", h=32),
                 bap=acsT.ap.unsqueeze(1).broadcast_to((32, 32, 64)))
            for q in range(4):
                bq = p.pbank(q)
                p.mm(bq, BD64, R1, start=True, stop=True,
                     lap=BD64.ap[0:32, 0:64],
                     rap=R1.ap[:, q * 512:(q + 1) * 512],
                     oap=bq.ap[0:64, :])
                u0 = pa_u[q % 2]
                p.tt(u0, bq, acs_c, ALU.subtract,
                     oap=u0.ap.rearrange("p (h s) -> p h s", h=8),
                     aap=bq.ap[0:64, :].rearrange("p (h s) -> p h s", h=8),
                     bap=acs_c.ap[:, q * 8:(q + 1) * 8].unsqueeze(2)
                     .broadcast_to((64, 8, 64)))
                p.tt(u0, u0, MBD, ALU.add,
                     oap=u0.ap.rearrange("p (h s) -> p h s", h=8),
                     aap=u0.ap.rearrange("p (h s) -> p h s", h=8),
                     bap=MBD.ap.unsqueeze(1).broadcast_to((64, 8, 64)))
                u1 = pa_e[q % 2]
                p.act(u1, u0, AF.Exp)
                p.tt(mt_all, u1, CBs, ALU.mult,
                     oap=mt_all.ap[:, q * 512:(q + 1) * 512]
                     .rearrange("p (h s) -> p h s", h=8),
                     aap=u1.ap.rearrange("p (h s) -> p h s", h=8),
                     bap=CBs.ap.unsqueeze(1).broadcast_to((64, 8, 64)))
            ybanks = [p.pbank(i) for i in range(4)]
            dbanks = [p.pbank(4 + i) for i in range(4)]
            for i in range(4):
                p.mm(ybanks[i], C_fm, S_bf, start=True, stop=True,
                     lap=C_fm.ap[:, ct0:ct0 + CH],
                     rap=S_bf.ap[:, i * 512:(i + 1) * 512],
                     oap=ybanks[i].ap[0:64, :])
            for h in range(NH):
                p.mm(dbanks[h // 8], mt_all, X_c,
                     start=(h % 8 == 0), stop=(h % 8 == 7),
                     lap=mt_all.ap[0:64, h * 64:(h + 1) * 64],
                     rap=X_c.ap[:, h * 64:(h + 1) * 64],
                     oap=dbanks[h // 8].ap[0:64, (h % 8) * 64:(h % 8) * 64 + 64],
                     inc=True)
            for q in range(4):
                u0 = pa_u[q % 2]
                u1 = pa_e[q % 2]
                sl = slice(q * 512, (q + 1) * 512)
                p.tt(u0, ybanks[q], sdo_c, ALU.mult,
                     oap=u0.ap.rearrange("p (h s) -> p h s", h=8),
                     aap=ybanks[q].ap[0:64, :].rearrange("p (h s) -> p h s", h=8),
                     bap=sdo_c.ap[:, q * 8:(q + 1) * 8].unsqueeze(2)
                     .broadcast_to((64, 8, 64)))
                p.tt(u0, u0, dbanks[q], ALU.add,
                     bap=dbanks[q].ap[0:64, :])
                p.tt(u1, x_c, Dbc, ALU.mult,
                     oap=u1.ap.rearrange("p (h s) -> p h s", h=8),
                     aap=x_c.ap[:, sl].rearrange("p (h s) -> p h s", h=8),
                     bap=Dbc.ap[:, q * 8:(q + 1) * 8].unsqueeze(2)
                     .broadcast_to((64, 8, 64)))
                p.tt(Y_c, u0, u1, ALU.add, oap=Y_c.ap[:, sl])
            for i in range(4):
                ps5c = p.pbank(5)
                p.mm(ps5c, acl_fm, SEL01, start=True, stop=True,
                     lap=acl_fm.ap[:, c:c + 1],
                     rap=SEL01.ap[:, i * 512:(i + 1) * 512],
                     oap=ps5c.ap[0:1, 0:512])
                p.act(dec_row, ps5c, AF.Exp, aap=ps5c.ap[0:1, 0:512],
                      oap=dec_row.ap[0:1, i * 512:(i + 1) * 512])
                ps5e = p.pbank(5)
                p.mm(ps5e, sh.ones_row_f32, dec_row, start=True, stop=True,
                     lap=sh.ones_row_f32.ap[0:1, 0:64],
                     oap=ps5e.ap[0:64, 0:512],
                     rap=dec_row.ap[0:1, i * 512:(i + 1) * 512])
                p.tt(S_T, S_T, ps5e, ALU.mult,
                     aap=S_T.ap[:, i * 512:(i + 1) * 512],
                     oap=S_T.ap[:, i * 512:(i + 1) * 512],
                     bap=ps5e.ap[0:64, 0:512])
            for i in range(4):
                sb_ = p.pbank(4 + (i % 2))
                p.mm(sb_, B_c, Xds_c, start=True, stop=True,
                     lap=B_c.ap[:, 0:64],
                     rap=Xds_c.ap[:, i * 512:(i + 1) * 512],
                     oap=sb_.ap[0:64, :])
                p.tt(S_T, S_T, sb_, ALU.add,
                     aap=S_T.ap[:, i * 512:(i + 1) * 512],
                     oap=S_T.ap[:, i * 512:(i + 1) * 512],
                     bap=sb_.ap[0:64, :])
            p.copy(S_bf, S_T, eng=p.scal)
            for m in range(16):
                pst = p.pbank(5 if m % 2 else 6)
                p.op(p.tens, lambda e, m=m, pst=pst: e.transpose(
                    pst.ap[:, 0:64], Y_c.ap[:, m * 128:(m + 1) * 128],
                    sh.ident_f32.ap[0:64, 0:64]),
                    reads=[Y_c, sh.ident_f32], writes=[pst])
                p.copy(y_fm[m], pst, oap=y_fm[m].ap[:, ct0:ct0 + CH],
                       aap=pst.ap[:, 0:64],
                       eng=(p.scal if m % 2 else p.vect))
        for m in range(16):
            p.act(sq16[m % 2], z_t[m], AF.Silu)
            p.tt(y_fm[m], y_fm[m], sq16[m % 2], ALU.mult)
        for m in range(16):
            p.act(sq16[m % 2], y_fm[m], AF.Square)
            p.mm(p.pbank(5), sh.c1024_col, sq16[m % 2], start=(m == 0),
                 stop=(m == 15), oap=p.pbank(5).ap[0:1, :], inc=True)
        rrow = ln.row_var
        p.ts(rrow, p.pbank(5), 1024.0 / 2048.0, ALU.mult,
             scalar2=EPS, op1=ALU.add, aap=p.pbank(5).ap[0:1, :])
        p.act(rrow, rrow, AF.Ln)
        p.act(rrow, rrow, AF.Exp, scale=-0.5)
        p.mm(p.pbank(6), sh.ones_row_f32, rrow, start=True, stop=True)
        for m in range(16):
            p.tt(cscr, y_fm[m], p.pbank(6), ALU.mult)
            p.ts(y_fm[m], cscr, nrw.ap[:, m:m + 1], ALU.mult, extra_reads=[nrw])
        for m in range(8):
            ps = p.pbank(m % 2)
            ar = ari[m % 2]
            p.op(p.qsync, lambda e, ar=ar, m=m: e.dma_start(
                out=ar.ap[:, 0:2048],
                in_=strip_batch_ap(opw.ap, 0, 16, m * 128, 128, 1024)),
                reads=[opw], writes=[ar])
            for k in range(16):
                p.mm(ps, ar, y_fm[k], start=(k == 0), stop=(k == 15),
                     lap=ar.ap[:, k * 128:(k + 1) * 128])
            p.copy(yo[m], ps, eng=p.scal)
        ln.stats(yo, double=False)
        ln.apply_add(yo, xf, xf)
        store_x(p, xf, x_out, tb, T)


def _build_fused(hostw, quiet_inputs=()):
    p = Prog()
    p.hostw = hostw
    sh = Shared(p)
    x_in = p.dram_in("x_tm_in", (SEQ, D))
    x_out_final = p.dram_out("x_tm_out", (SEQ, D))
    sA = p.dram_tmp("x_scrA", (D, SEQ))
    sB = p.dram_tmp("x_scrB", (D, SEQ))
    sh.setup(need_ident_bf=True, need_ident_f32=True)
    ln = LNArena(p, sh, 512)
    xs = XStream(p, 512)
    wc = WCache(p)
    qb = p.sbuf("quiet_all", (1, 16), F32)
    for nm, shape in quiet_inputs:
        apq = p.dram_in(nm, shape)
        flat = ("a -> (a)" if len(shape) == 1 else
                "a b -> (a b)" if len(shape) == 2 else "a b c -> (a b c)")
        v = apq.rearrange(flat)[0:16].rearrange("(o n) -> o n", o=1)
        p.op(p.qsync, lambda e, v=v, qb=qb: e.dma_start(out=qb.ap, in_=v),
             writes=[qb])

    p.open_scope()
    emit_entry_transpose(p, sh, x_in, sA)
    p.close_scope()
    cur, nxt = sA, sB
    for name, em, wc_ in BLOCK_EMITTERS:
        p.open_scope()
        emitter = globals()[em]
        kw = {"wc": wc} if wc_ else {}
        emitter(p, sh, ln, xs, cur, nxt, **kw)
        p.close_scope()
        cur, nxt = nxt, cur
    p.open_scope()
    emit_exit_transpose(p, sh, cur, x_out_final)
    p.close_scope()
    _finalize_waits(p)
    return p


_FUSED = {}


def kernel_fused(**inputs):
    hostw = host_weights(inputs)
    x_full = np.asarray(inputs["L0_attn_x"], np.float32)
    quiet = [("L2_ffn_Wg", (D, 16)), ("L2_ffn_Wq", (16, D, 64)),
             ("L2_ffn_bq", (16, 64)), ("L2_ffn_Wkv", (16, D, 128)),
             ("L2_ffn_bkv", (16, 128)), ("L2_ffn_Wo", (16, 64, D)),
             ("L2_ffn_bo", (D,))]
    p = _build_fused(hostw, quiet)
    extra = [{"x_tm_in": np.ascontiguousarray(x_full[c])} for c in range(NCORES)]
    res = _run_prog(p, extra, inputs)
    return np.stack([np.asarray(r["x_tm_out"]) for r in res], 0)



BLOCK_EMITTERS = [
    ("L0_attn", "emit_l0attn", True),
    ("L0_ffn", "emit_l0ffn", True),
    ("L1_attn", "emit_l1attn", True),
    ("L1_ffn", "emit_l1ffn", False),
    ("L2_attn", "emit_l2attn", True),
    ("L2_ffn", "emit_l2ffn", False),
    ("L3_attn", "emit_l3attn", True),
    ("L3_ffn", "emit_l3ffn", True),
]


def _finalize_waits(p):
    def fin(e, p=p):
        e.wait_ge(p.qsync.sem, p.qsync.n)
        e.wait_ge(p.qgp.sem, p.qgp.n)
    p.qsync.push(fin)


def _build_entry():
    p = Prog()
    sh = Shared(p)
    x_in = p.dram_in("x_tm_in", (SEQ, D))
    x_out = p.dram_out("x_res_out", (D, SEQ))
    sh.setup(need_ident_f32=True)
    emit_entry_transpose(p, sh, x_in, x_out)
    _finalize_waits(p)
    return p


def _build_exit():
    p = Prog()
    sh = Shared(p)
    x_in = p.dram_in("x_res_in", (D, SEQ))
    x_out = p.dram_out("x_tm_out", (SEQ, D))
    sh.setup(need_ident_f32=True)
    emit_exit_transpose(p, sh, x_in, x_out)
    _finalize_waits(p)
    return p


def _build_block(name, emitter_name, needs_wc, hostw):
    p = Prog()
    p.hostw = hostw
    sh = Shared(p)
    x_in = p.dram_in("x_res_in", (D, SEQ))
    x_out = p.dram_out("x_res_out", (D, SEQ))
    sh.setup(need_ident_bf=True, need_ident_f32=True)
    ln = LNArena(p, sh, 512)
    xs = XStream(p, 512)
    emitter = globals()[emitter_name]
    kw = {"wc": WCache(p)} if needs_wc else {}
    emitter(p, sh, ln, xs, x_in, x_out, **kw)
    _finalize_waits(p)
    return p


_PROG_CACHE = {}


def _run_prog(p, per_core_extra, inputs):
    nc = p.finish() if not getattr(p, "_finished", False) else p.nc
    p._finished = True
    in_maps = []
    for c in range(NCORES):
        im = {}
        im.update(p.consts)
        for n in getattr(p, "extern_names", []):
            if n in per_core_extra[c]:
                im[n] = per_core_extra[c][n]
            elif n in inputs:
                im[n] = np.ascontiguousarray(np.asarray(inputs[n], np.float32))
            else:
                raise KeyError(n)
        in_maps.append(im)
    res = run_bass_kernel_spmd(nc, in_maps, core_ids=list(range(NCORES)))
    return res.results


def kernel(**inputs):
    return kernel_fused(**inputs)
    hostw = host_weights(inputs)
    x_full = np.asarray(inputs["L0_attn_x"], np.float32)

    p = _build_entry()
    res = _run_prog(p, [{"x_tm_in": np.ascontiguousarray(x_full[c])}
                        for c in range(NCORES)], inputs)
    xres = [np.asarray(r["x_res_out"]) for r in res]

    for name, em, wc_ in BLOCK_EMITTERS:
        p = _build_block(name, em, wc_, hostw)
        res = _run_prog(p, [{"x_res_in": xres[c]} for c in range(NCORES)], inputs)
        xres = [np.asarray(r["x_res_out"]) for r in res]

    p = _build_exit()
    res = _run_prog(p, [{"x_res_in": xres[c]} for c in range(NCORES)], inputs)
    out = np.stack([np.asarray(r["x_tm_out"]) for r in res], 0)
    return out




# version 12
# speedup vs baseline: 1.1747x; 1.1747x over previous
import contextlib
import math
import numpy as np
import ml_dtypes
import concourse.bass as bass
import concourse.mybir as mybir
from concourse.bass_utils import run_bass_kernel_spmd

F32 = mybir.dt.float32
BF16 = mybir.dt.bfloat16
AF = mybir.ActivationFunctionType
ALU = mybir.AluOpType
AX = mybir.AxisListType
BF_NP = ml_dtypes.bfloat16

NCORES = 8
D = 1024
SEQ = 4096
DT = 8
EPS = 1e-5



class Buf:
    __slots__ = ("ap", "name", "wtok", "rtoks")

    def __init__(self, ap, name):
        self.ap = ap
        self.name = name
        self.wtok = None
        self.rtoks = []

    def __getitem__(self, idx):
        return self.ap[idx]


class Eng:
    def __init__(self, prog, name, step):
        self.prog = prog
        self.name = name
        self.sem = None
        self.n = 0
        self.step = step
        self.waited = {}
        self.queue = []
        self.pending = []
        self.pos_of_cnt = {}
        self.flushes = set()

    def push(self, fn):
        self.prog.seqctr += 1
        self.queue.append((float(self.prog.seqctr), fn))

    def ensure_flushed(self, cnt):
        if cnt in self.flushes:
            return
        pos = self.pos_of_cnt[cnt] + 1
        seq = self.queue[pos - 1][0] + 0.5
        self.queue.insert(pos, (seq, lambda e, s=self, cnt=cnt: e.wait_ge(s.sem, cnt)))
        for c, i in self.pos_of_cnt.items():
            if i >= pos:
                self.pos_of_cnt[c] = i + 1
        self.flushes.add(cnt)

    def wait_tok(self, tok):
        if tok is None:
            return
        eng, cnt = tok
        if eng is self and cnt > self.n:
            return
        if self.waited.get(eng.name, 0) >= cnt:
            return
        if eng.step == 16:
            eng.ensure_flushed(cnt)
        self.push(lambda e, eng=eng, cnt=cnt: e.wait_ge(eng.sem, cnt))
        self.waited[eng.name] = cnt


class Prog:

    def __init__(self, name="prog"):
        self.nc = bass.Bass("TRN2", target_bir_lowering=False)
        self.es = contextlib.ExitStack()
        self.consts = {}
        self.const_aps = {}
        self._uid = 0
        self.seqctr = 0
        self.tens = Eng(self, "tens", 1)
        self.vect = Eng(self, "vect", 1)
        self.scal = Eng(self, "scal", 1)
        self.pool = Eng(self, "pool", 1)
        self.qsync = Eng(self, "qsync", 16)
        self.qgp = Eng(self, "qgp", 16)
        self.engines = [self.tens, self.vect, self.scal, self.pool,
                        self.qsync, self.qgp]
        self.sbuf_bytes = 0
        self.hostw = {}

    def uid(self):
        self._uid += 1
        return self._uid

    def dram_in(self, name, shape, dtype=F32):
        if not hasattr(self, "extern_names"):
            self.extern_names = []
        self.extern_names.append(name)
        return self.nc.dram_tensor(name, list(shape), dtype, kind="ExternalInput").ap()

    def dram_out(self, name, shape, dtype=F32):
        return self.nc.dram_tensor(name, list(shape), dtype, kind="ExternalOutput").ap()

    def dram_tmp(self, name, shape, dtype=F32):
        return self.nc.dram_tensor(name, list(shape), dtype, kind="Internal").ap()

    def dram_tmp_buf(self, name, shape, dtype=F32):
        return Buf(self.dram_tmp(name, shape, dtype), name)

    def const(self, name, arr):
        if name in self.const_aps:
            return self.const_aps[name]
        arr = np.ascontiguousarray(arr)
        dt = {np.dtype(np.float32): F32, np.dtype(BF_NP): BF16}[arr.dtype]
        ap = self.nc.dram_tensor(name, list(arr.shape), dt, kind="ExternalInput").ap()
        self.consts[name] = arr
        self.const_aps[name] = ap
        return ap

    def sbuf(self, name, shape, dtype=F32):
        uname = f"{name}_{self.uid()}"
        es = getattr(self, "_scope_es", None) or self.es
        ap = es.enter_context(self.nc.sbuf_tensor(uname, list(shape), dtype))
        nbytes = int(np.prod(shape[1:])) * (2 if dtype == BF16 else 4)
        self.sbuf_bytes += nbytes
        return Buf(ap[:], uname)

    def open_scope(self):
        import contextlib as _ct
        assert getattr(self, "_scope_es", None) is None
        self._scope_es = _ct.ExitStack()

    def close_scope(self):
        self.barrier()
        self._scope_es.close()
        self._scope_es = None

    def barrier(self):
        for q in (self.qsync, self.qgp):
            if q.n > 0:
                q.ensure_flushed(q.n)
        snap = [(e, e.n) for e in self.engines]
        for e in self.engines:
            for oe, n in snap:
                if n > 0:
                    e.wait_tok((oe, n))
        if hasattr(self, "_pool"):
            for b in self._pool.values():
                b.wtok = None
                b.rtoks = []

    def pbank(self, i):
        if not hasattr(self, "_pool"):
            self._pool = {}
        if i not in self._pool:
            self._pool[i] = self.psum(f"bank{i}")
        return self._pool[i]

    def psum(self, name, shape=(128, 512), dtype=F32):
        uname = f"{name}_{self.uid()}"
        ap = self.es.enter_context(self.nc.psum_tensor(uname, list(shape), dtype))
        return Buf(ap[:], uname)

    def op(self, eng, fn, reads=(), writes=(), inc=True):
        for b in reads:
            if b is not None:
                eng.wait_tok(b.wtok)
        for b in writes:
            if b is not None:
                eng.wait_tok(b.wtok)
                for t in b.rtoks:
                    eng.wait_tok(t)
        if inc:
            def emit(e, fn=fn, eng=eng):
                fn(e).then_inc(eng.sem, eng.step)
            eng.push(emit)
            eng.n += eng.step
            if eng.step == 16:
                eng.pos_of_cnt[eng.n] = len(eng.queue) - 1
            tok = (eng, eng.n)
            eng.pending = []
            for b in reads:
                if b is not None:
                    b.rtoks.append(tok)
            for b in writes:
                if b is not None:
                    b.wtok = tok
                    b.rtoks = []
            return tok
        else:
            eng.push(lambda e, fn=fn: fn(e))
            ftok = (eng, eng.n + eng.step)
            eng.pending.append(ftok)
            for b in reads:
                if b is not None:
                    b.rtoks.append(ftok)
            for b in writes:
                if b is not None:
                    b.wtok = ftok
            return None

    def dma(self, q, out, in_, reads=(), writes=()):
        out_ap = out.ap if isinstance(out, Buf) else out
        in_ap = in_.ap if isinstance(in_, Buf) else in_
        reads = list(reads) + ([in_] if isinstance(in_, Buf) else [])
        writes = list(writes) + ([out] if isinstance(out, Buf) else [])
        return self.op(q, lambda e: e.dma_start(out=out_ap, in_=in_ap),
                       reads=reads, writes=writes)

    def tt(self, o, a, b, op, oap=None, aap=None, bap=None, eng=None, extra_reads=()):
        eng = eng or self.vect
        return self.op(eng, lambda e: e.tensor_tensor(
            out=oap if oap is not None else o.ap,
            in0=aap if aap is not None else a.ap,
            in1=bap if bap is not None else b.ap, op=op),
            reads=[a, b] + list(extra_reads), writes=[o])

    def ts(self, o, a, scalar1, op0, scalar2=None, op1=None, oap=None, aap=None,
           extra_reads=(), eng=None):
        eng = eng or self.vect
        if op1 is not None:
            fn = lambda e: e.tensor_scalar(
                out=oap if oap is not None else o.ap,
                in0=aap if aap is not None else a.ap,
                scalar1=scalar1, scalar2=scalar2, op0=op0, op1=op1)
        else:
            fn = lambda e: e.tensor_scalar(
                out=oap if oap is not None else o.ap,
                in0=aap if aap is not None else a.ap,
                scalar1=scalar1, scalar2=None, op0=op0)
        return self.op(eng, fn, reads=[a] + list(extra_reads), writes=[o])

    def stt(self, o, a, scalar, b, op0, op1, oap=None, aap=None, bap=None,
            extra_reads=(), eng=None):
        eng = eng or self.vect
        return self.op(eng, lambda e: e.scalar_tensor_tensor(
            out=oap if oap is not None else o.ap,
            in0=aap if aap is not None else a.ap, scalar=scalar,
            in1=bap if bap is not None else b.ap, op0=op0, op1=op1),
            reads=[a, b] + list(extra_reads), writes=[o])

    def act(self, o, a, func, bias=0.0, scale=1.0, accum=None, accum_ap=None,
            oap=None, aap=None, extra_reads=()):
        kw = {}
        if accum is not None:
            kw["accum_out"] = accum_ap if accum_ap is not None else accum.ap
        return self.op(self.scal, lambda e: e.activation(
            out=oap if oap is not None else o.ap,
            in_=aap if aap is not None else a.ap,
            func=func, bias=bias, scale=scale, **kw),
            reads=[a] + list(extra_reads),
            writes=[o] + ([accum] if accum is not None else []))

    def copy(self, o, a, oap=None, aap=None, eng=None):
        eng = eng or self.vect
        if eng is self.scal:
            return self.op(eng, lambda e: e.copy(
                out=oap if oap is not None else o.ap,
                in_=aap if aap is not None else a.ap), reads=[a], writes=[o])
        return self.op(eng, lambda e: e.tensor_copy(
            oap if oap is not None else o.ap,
            aap if aap is not None else a.ap), reads=[a], writes=[o])

    def memset(self, o, val, oap=None):
        return self.op(self.vect, lambda e: e.memset(
            oap if oap is not None else o.ap, val), writes=[o])

    def mm(self, o, lhsT, rhs, start, stop, oap=None, lap=None, rap=None,
           extra_reads=(), inc=None):
        if inc is None:
            inc = stop
        _l = lap if lap is not None else lhsT.ap
        _o = oap if oap is not None else o.ap
        _r = rap if rap is not None else rhs.ap
        assert _l.shape[-1] == _o.shape[0], \
            f"mm M mismatch: lhsT {_l.shape} out {_o.shape}"
        assert _l.shape[0] == _r.shape[0], \
            f"mm K mismatch: lhsT {_l.shape} rhs {_r.shape}"
        return self.op(self.tens, lambda e: e.matmul(
            oap if oap is not None else o.ap,
            lap if lap is not None else lhsT.ap,
            rap if rap is not None else rhs.ap, start=start, stop=stop),
            reads=[lhsT, rhs] + list(extra_reads), writes=[o], inc=inc)

    def transpose(self, o, a, ident, oap=None, aap=None):
        return self.op(self.tens, lambda e: e.transpose(
            oap if oap is not None else o.ap,
            aap if aap is not None else a.ap, ident.ap),
            reads=[a, ident], writes=[o])

    def reduce(self, o, a, op, negate=False, oap=None, aap=None, eng=None):
        eng = eng or self.vect
        return self.op(eng, lambda e: e.tensor_reduce(
            out=oap if oap is not None else o.ap,
            in_=aap if aap is not None else a.ap,
            axis=AX.X, op=op, negate=negate),
            reads=[a], writes=[o])

    def scan(self, o, a0, a1, initial, oap=None, a0ap=None, a1ap=None,
             extra_reads=()):
        return self.op(self.vect, lambda e: e.tensor_tensor_scan(
            out=oap if oap is not None else o.ap,
            data0=a0ap if a0ap is not None else a0.ap,
            data1=a1ap if a1ap is not None else a1.ap,
            initial=initial, op0=ALU.mult, op1=ALU.add),
            reads=[a0, a1] + list(extra_reads), writes=[o])

    def finish(self):
        for eng in self.engines:
            eng.sem = self.es.enter_context(self.nc.semaphore(f"sem_{eng.name}"))
        for eng in self.engines:
            assert not eng.pending, f"unclosed matmul group on {eng.name}"
        with self.es, self.nc.Block() as block:
            def run(entries):
                def body(e):
                    for _, fn in entries:
                        fn(e)
                return body
            merged = sorted(self.qgp.queue + self.pool.queue, key=lambda t: t[0])
            block.sync(run(self.qsync.queue))
            block.gpsimd(run(merged))
            block.tensor(run(self.tens.queue))
            block.vector(run(self.vect.queue))
            block.scalar(run(self.scal.queue))
        return self.nc


def wT(a):
    return np.ascontiguousarray(np.asarray(a, np.float32).T)



class Shared:

    def __init__(self, p: Prog):
        self.p = p
        self.ones_row_f32 = None
        self.c1024_col = None
        self.ident_bf = None
        self.ident_f32 = None

    def setup(self, need_ident_bf=False, need_ident_f32=False):
        p = self.p
        self.ones_row_f32 = p.sbuf("ones_row", (1, 128), F32)
        p.dma(p.qsync, self.ones_row_f32, p.const("c_ones_row", np.ones((1, 128), np.float32)))
        self.c1024_col = p.sbuf("c1024", (128, 1), BF16)
        p.dma(p.qsync, self.c1024_col,
              p.const("c_rcpD", np.full((128, 1), 1.0 / 1024.0, BF_NP)))
        if need_ident_bf:
            self.ident_bf = p.sbuf("identb", (128, 128), BF16)
            p.dma(p.qsync, self.ident_bf, p.const("c_ident_b", np.eye(128, dtype=BF_NP)))
        if need_ident_f32:
            self.ident_f32 = p.sbuf("identf", (128, 128), F32)
            p.dma(p.qsync, self.ident_f32, p.const("c_ident_f", np.eye(128, dtype=np.float32)))
        self.ones_row_bf = p.sbuf("ones_row_bf", (1, 128), BF16)
        p.dma(p.qsync, self.ones_row_bf, p.const("c_ones_row_bf", np.ones((1, 128), BF_NP)))
        self.ones_col_f32 = p.sbuf("ones_col_f32", (128, 1), F32)
        p.dma(p.qsync, self.ones_col_f32, p.const("c_ones_col_f32", np.ones((128, 1), np.float32)))


class LNArena:

    def __init__(self, p: Prog, sh: Shared, T, nk=8):
        self.p, self.sh, self.T, self.nk = p, sh, T, nk
        assert T <= 512
        self.ps_stats = p.pbank(5)
        self.ps_BA = p.pbank(6)
        self.ps_BB = p.pbank(7)
        self.sq = [p.sbuf("ln_sq", (128, T), BF16) for _ in range(2)]
        self.row_var = p.sbuf("ln_var", (1, T), F32)
        self.row_r = p.sbuf("ln_r", (1, T), F32)
        self.row_mr = p.sbuf("ln_mr", (1, T), F32)
        self.row_r1 = p.sbuf("ln_r1", (1, T), F32)
        self.row_mr1 = p.sbuf("ln_mr1", (1, T), F32)
        self.scr = [p.sbuf("ln_scr", (128, T), F32) for _ in range(2)]

    def stats(self, src_bf, n_feat=None, double=False, srcaps=None,
              also_single=False):
        p, sh, T = self.p, self.sh, self.T
        nk = len(src_bf)
        n_feat = n_feat or (128 * nk)
        rcp = 1.0 / n_feat
        for k in range(nk):
            ap = srcaps[k] if srcaps is not None else None
            p.mm(self.ps_stats, sh.c1024_col, src_bf[k], start=(k == 0),
                 stop=(k == nk - 1), oap=self.ps_stats.ap[0:1, :], rap=ap)
        for k in range(nk):
            ap = srcaps[k] if srcaps is not None else None
            p.act(self.sq[k % 2], src_bf[k], AF.Square, aap=ap)
            p.mm(self.ps_stats, sh.c1024_col, self.sq[k % 2], start=(k == 0),
                 stop=(k == nk - 1), oap=self.ps_stats.ap[64:65, :], inc=True)
        corr = (128.0 * nk) / 1024.0 / n_feat
        scale_m = 1024.0 * rcp
        p.act(self.row_var, self.ps_stats, AF.Square, scale=scale_m,
              aap=self.ps_stats.ap[0:1, :])
        p.stt(self.row_var, self.ps_stats, scale_m, self.row_var,
              ALU.mult, ALU.subtract, aap=self.ps_stats.ap[64:65, :])
        if also_single:
            p.ts(self.row_r1, self.row_var, EPS, ALU.add)
            p.act(self.row_r1, self.row_r1, AF.Ln)
            p.act(self.row_r1, self.row_r1, AF.Exp, scale=-0.5)
            p.stt(self.row_mr1, self.ps_stats, scale_m, self.row_r1,
                  ALU.mult, ALU.mult, aap=self.ps_stats.ap[0:1, :])
        if double:
            p.ts(self.row_var, self.row_var, 1.0 + EPS, ALU.mult,
                 scalar2=EPS * (1.0 + EPS), op1=ALU.add)
        else:
            p.ts(self.row_var, self.row_var, EPS, ALU.add)
        p.act(self.row_r, self.row_var, AF.Ln)
        p.act(self.row_r, self.row_r, AF.Exp, scale=-0.5)
        p.stt(self.row_mr, self.ps_stats, scale_m, self.row_r,
              ALU.mult, ALU.mult, aap=self.ps_stats.ap[0:1, :])
        p.mm(self.ps_BA, sh.ones_row_f32, self.row_r, start=True, stop=True)
        p.mm(self.ps_BB, sh.ones_row_f32, self.row_mr, start=True, stop=True)

    def apply(self, src, outs, srcaps=None, outaps=None):
        p = self.p
        for k in range(len(src)):
            sa = srcaps[k] if srcaps is not None else None
            oa = outaps[k] if outaps is not None else None
            scr = self.scr[k % 2]
            p.tt(scr, src[k], self.ps_BA, ALU.mult, aap=sa)
            p.tt(outs[k], scr, self.ps_BB, ALU.subtract, oap=oa)

    def bcast_single(self, pA, pB):
        p, sh = self.p, self.sh
        p.mm(pA, sh.ones_row_f32, self.row_r1, start=True, stop=True)
        p.mm(pB, sh.ones_row_f32, self.row_mr1, start=True, stop=True)

    def apply_single(self, src, outs, pA, pB):
        p = self.p
        for k in range(len(src)):
            scr = self.scr[k % 2]
            p.tt(scr, src[k], pA, ALU.mult)
            p.tt(outs[k], scr, pB, ALU.subtract)

    def apply_add(self, src, res, outs, srcaps=None, resaps=None, outaps=None):
        p = self.p
        for k in range(len(src)):
            sa = srcaps[k] if srcaps is not None else None
            ra = resaps[k] if resaps is not None else None
            oa = outaps[k] if outaps is not None else None
            s0, s1 = self.scr[0], self.scr[1]
            p.tt(s0, src[k], self.ps_BA, ALU.mult, aap=sa)
            p.tt(s1, s0, self.ps_BB, ALU.subtract)
            p.tt(outs[k], s1, res[k], ALU.add, bap=ra, oap=oa)


class WLoader:

    def __init__(self, p: Prog, name, wt_np, mwidth=None):
        self.p = p
        K, M = wt_np.shape
        assert K % 128 == 0
        self.K, self.M = K, M
        self.nk = K // 128
        self.dram = p.const(name, np.ascontiguousarray(wt_np.astype(np.float32)))
        self.mw = mwidth or M
        self.strips = [p.sbuf(f"{name}_s{k}", (128, self.mw), BF16)
                       for k in range(self.nk)]
        self.loaded_m0 = [None] * self.nk

    def load(self, m0=0):
        p = self.p
        for k in range(self.nk):
            if self.loaded_m0[k] == m0:
                continue
            p.dma(p.qgp, self.strips[k],
                  self.dram[k * 128:(k + 1) * 128, m0:m0 + self.mw])
            self.loaded_m0[k] = m0

    def tile(self, k, m):
        return self.strips[k].ap[:, m * 128:(m + 1) * 128]


def load_cols(p: Prog, sh: Shared, dram_1d_ap, n, name, dtype=F32):
    nk = (n + 127) // 128
    row = p.sbuf(f"{name}_row", (1, nk * 128), F32)
    src2 = dram_1d_ap.rearrange("(o n) -> o n", o=1)
    if n % 128 == 0:
        p.dma(p.qsync, row, src2)
    else:
        p.memset(row, 0.0)
        p.op(p.qsync, lambda e: e.dma_start(out=row.ap[0:1, 0:n], in_=src2),
             writes=[row])
    ps = p.pbank(4)
    cols = []
    for k in range(nk):
        p.op(p.tens, lambda e, k=k: e.transpose(
            ps.ap[:, 0:1], row.ap[0:1, k * 128:(k + 1) * 128],
            sh.ident_f32.ap[0:1, 0:1]), reads=[row, sh.ident_f32], writes=[ps])
        c = p.sbuf(f"{name}_c{k}", (128, 1), F32)
        p.copy(c, ps, aap=ps.ap[:, 0:1], eng=p.scal)
        cols.append(c)
    return cols


def emit_entry_transpose(p: Prog, sh: Shared, x_in, x_res):
    tm = [p.sbuf("ent_tm", (128, 1024), F32) for _ in range(2)]
    st = [p.sbuf("ent_st", (128, 128), F32) for _ in range(2)]
    ps = [p.pbank(3), p.pbank(4)]
    for tt in range(32):
        tmb = tm[tt % 2]
        p.dma(p.qsync, tmb, x_in[tt * 128:(tt + 1) * 128, :])
        for dd in range(DT):
            j = (tt * DT + dd) % 2
            p.transpose(ps[j], tmb, sh.ident_f32,
                        aap=tmb.ap[:, dd * 128:(dd + 1) * 128],
                        oap=ps[j].ap[:, 0:128])
            p.copy(st[j], ps[j], eng=p.scal, aap=ps[j].ap[:, 0:128])
            p.dma(p.qgp, x_res[dd * 128:(dd + 1) * 128, tt * 128:(tt + 1) * 128],
                  st[j])


def emit_exit_transpose(p: Prog, sh: Shared, x_res, x_out):
    fm = [p.sbuf("ext_fm", (128, 4096), F32) for _ in range(2)]
    st = [p.sbuf("ext_st", (128, 128), F32) for _ in range(2)]
    ps = [p.pbank(3), p.pbank(4)]
    for dd in range(DT):
        fmb = fm[dd % 2]
        p.dma(p.qsync, fmb, x_res[dd * 128:(dd + 1) * 128, :])
        for tt in range(32):
            j = (dd * 32 + tt) % 2
            p.transpose(ps[j], fmb, sh.ident_f32,
                        aap=fmb.ap[:, tt * 128:(tt + 1) * 128],
                        oap=ps[j].ap[:, 0:128])
            p.copy(st[j], ps[j], eng=p.scal, aap=ps[j].ap[:, 0:128])
            p.dma(p.qgp, x_out[tt * 128:(tt + 1) * 128, dd * 128:(dd + 1) * 128],
                  st[j])


class XStream:

    def __init__(self, p: Prog, T, double=False):
        n = 2 if double else 1
        self.p, self.T = p, T
        self.x = [[p.sbuf("xs_f", (128, T), F32) for _ in range(DT)] for _ in range(n)]
        self.xb = [[p.sbuf("xs_b", (128, T), BF16) for _ in range(DT)] for _ in range(n)]

    def load(self, x_res, tb, buf=0, need_f32=True):
        p, T = self.p, self.T
        t0 = tb * T
        for k in range(DT):
            if need_f32:
                p.dma(p.qsync, self.x[buf][k], x_res[k * 128:(k + 1) * 128, t0:t0 + T])
            p.dma(p.qgp, self.xb[buf][k], x_res[k * 128:(k + 1) * 128, t0:t0 + T])


def store_x(p: Prog, xnew, x_out, tb, T):
    t0 = tb * T
    for k in range(DT):
        p.dma(p.qsync, x_out[k * 128:(k + 1) * 128, t0:t0 + T], xnew[k])



def emit_l2ffn(p: Prog, sh: Shared, ln: "LNArena", xs: "XStream",
               x_in, x_out, T=512):
    g_cols = load_cols(p, sh, p.dram_in("L2_ffn_ln_g", (D,)), D, "l2f_g")
    b_cols = load_cols(p, sh, p.dram_in("L2_ffn_ln_b", (D,)), D, "l2f_b")
    z2 = [p.sbuf("l2f_z2", (128, T), F32) for _ in range(DT)]
    yn = [p.sbuf("l2f_yn", (128, T), BF16) for _ in range(DT)]
    xnew = [p.sbuf("l2f_xn", (128, T), F32) for _ in range(DT)]
    nt = SEQ // T
    for tb in range(nt):
        xs.load(x_in, tb, buf=0)
        xf, xb = xs.x[0], xs.xb[0]
        ln.stats(xb, double=True)
        ln.apply(xb, z2)
        for k in range(DT):
            p.ts(yn[k], z2[k], g_cols[k].ap[:, 0:1], ALU.mult,
                 scalar2=b_cols[k].ap[:, 0:1], op1=ALU.add,
                 extra_reads=[g_cols[k], b_cols[k]])
        ln.stats(yn, double=False)
        ln.apply_add(yn, xf, xnew)
        store_x(p, xnew, x_out, tb, T)



def emit_l1ffn(p: Prog, sh: Shared, ln: "LNArena", xs: "XStream",
               x_in, x_out, T=512):
    G1R = p.dram_in("L1_ffn_G1_R", (1, 32, 4, 32))
    G2R = p.dram_in("L1_ffn_G2_R", (4, 16, 1, 16))
    G1I = p.dram_in("L1_ffn_G1_I", (1, 32, 4, 32))
    G2I = p.dram_in("L1_ffn_G2_I", (4, 16, 1, 16))
    br_cols = load_cols(p, sh, p.dram_in("L1_ffn_bias_real", (512,)), 512, "l1f_br")
    bi_cols = load_cols(p, sh, p.dram_in("L1_ffn_bias_imag", (512,)), 512, "l1f_bi")

    j = np.arange(512)
    Eb = (j[None, :] // 16 == np.arange(32)[:, None]).astype(np.float32)
    Ee = (j[None, :] % 16 == np.arange(16)[:, None]).astype(np.float32)
    Ed = (j[None, :] // 16 == np.arange(32)[:, None]).astype(np.float32)
    Eg = (j[None, :] % 16 == np.arange(16)[:, None]).astype(np.float32)
    EbC = p.sbuf("l1f_Eb", (32, 512)); p.dma(p.qsync, EbC, p.const("c_l1f_Eb", Eb))
    EeC = p.sbuf("l1f_Ee", (16, 512)); p.dma(p.qsync, EeC, p.const("c_l1f_Ee", Ee))
    EdC = p.sbuf("l1f_Ed", (32, 512)); p.dma(p.qsync, EdC, p.const("c_l1f_Ed", Ed))
    EgC = p.sbuf("l1f_Eg", (16, 512)); p.dma(p.qsync, EgC, p.const("c_l1f_Eg", Eg))

    g1 = {"R": p.sbuf("l1f_g1r", (32, 128)), "I": p.sbuf("l1f_g1i", (32, 128))}
    g2 = {"R": p.sbuf("l1f_g2r", (16, 64)), "I": p.sbuf("l1f_g2i", (16, 64))}
    p.dma(p.qsync, g1["R"], G1R.rearrange("o b c d -> (o b) (c d)"))
    p.dma(p.qsync, g1["I"], G1I.rearrange("o b c d -> (o b) (c d)"))
    p.op(p.qsync, lambda e: e.dma_start(
        out=g2["R"].ap.rearrange("e (c g) -> e c g", c=4),
        in_=G2R.rearrange("c e o g -> (e o) c g")), writes=[g2["R"]])
    p.op(p.qsync, lambda e: e.dma_start(
        out=g2["I"].ap.rearrange("e (c g) -> e c g", c=4),
        in_=G2I.rearrange("c e o g -> (e o) c g")), writes=[g2["I"]])

    ps_t = p.pbank(0)
    ps_A = p.pbank(1)
    ps_B = p.pbank(2)
    t1s = p.sbuf("l1f_t1", (32, 512))
    t2s = p.sbuf("l1f_t2", (16, 512))
    m1s = p.sbuf("l1f_m1", (128, 512))
    wtmp = p.sbuf("l1f_wtmp", (128, 512))
    Wbf = {}
    for RI in ("R", "I"):
        Wacc = [p.sbuf(f"l1f_Wac{RI}{m}", (128, 512)) for m in range(4)]
        for c in range(4):
            p.mm(ps_t, g1[RI], EbC, start=True, stop=True,
                 lap=g1[RI].ap[:, c * 32:(c + 1) * 32], oap=ps_t.ap[0:32, :])
            p.copy(t1s, ps_t, aap=ps_t.ap[0:32, :], eng=p.scal)
            p.mm(ps_t, g2[RI], EeC, start=True, stop=True,
                 lap=g2[RI].ap[:, c * 16:(c + 1) * 16], oap=ps_t.ap[0:16, :])
            p.copy(t2s, ps_t, aap=ps_t.ap[0:16, :], eng=p.scal)
            for m in range(4):
                p.mm(ps_A, EdC, t1s, start=True, stop=True,
                     lap=EdC.ap[:, m * 128:(m + 1) * 128])
                p.mm(ps_B, EgC, t2s, start=True, stop=True,
                     lap=EgC.ap[:, m * 128:(m + 1) * 128])
                p.copy(m1s, ps_A)
                if c == 0:
                    p.tt(Wacc[m], m1s, ps_B, ALU.mult)
                else:
                    p.tt(wtmp, m1s, ps_B, ALU.mult)
                    p.tt(Wacc[m], Wacc[m], wtmp, ALU.add)
        for k in range(4):
            b = p.sbuf(f"l1f_W{RI}b{k}", (128, 512), BF16)
            p.copy(b, Wacc[k], eng=p.scal)
            Wbf[(RI, k)] = b
    negWi = []
    for k in range(4):
        b = p.sbuf(f"l1f_nWib{k}", (128, 512), BF16)
        p.ts(b, Wbf[("I", k)], -1.0, ALU.mult)
        negWi.append(b)

    xn = [p.sbuf("l1f_xn", (128, T), BF16) for _ in range(DT)]
    yb = [p.sbuf("l1f_y", (128, T), BF16) for _ in range(DT)]
    xnew = [p.sbuf("l1f_xo", (128, T), F32) for _ in range(DT)]
    ps_mm = [p.pbank(0), p.pbank(1)]
    nt = SEQ // T
    for tb in range(nt):
        xs.load(x_in, tb, buf=0)
        xf, xb = xs.x[0], xs.xb[0]
        ln.stats(xb, double=False)
        ln.apply(xb, xn)
        for m in range(8):
            ps = ps_mm[m % 2]
            if m < 4:
                pairs = [(Wbf[("R", k)], xn[k]) for k in range(4)] + \
                        [(negWi[k], xn[4 + k]) for k in range(4)]
                bcol = br_cols[m]
            else:
                pairs = [(Wbf[("I", k)], xn[k]) for k in range(4)] + \
                        [(Wbf[("R", k)], xn[4 + k]) for k in range(4)]
                bcol = bi_cols[m - 4]
            mm0 = m % 4
            for i, (w, a) in enumerate(pairs):
                p.mm(ps, w, a, start=(i == 0), stop=(i == 7),
                     lap=w.ap[:, mm0 * 128:(mm0 + 1) * 128])
            p.ts(yb[m], ps, bcol.ap[:, 0:1], ALU.add, extra_reads=[bcol])
        ln.stats(yb, double=False)
        ln.apply_add(yb, xf, xnew)
        store_x(p, xnew, x_out, tb, T)


class WCache:

    def __init__(self, p: Prog):
        self.p = p
        self.done = {}

    def get(self, name, arr_np):
        if name in self.done:
            return self.done[name]
        p = self.p
        src = p.const(name, np.ascontiguousarray(arr_np.astype(np.float32)))
        dst_t = p.nc.dram_tensor(name + "_bf", list(arr_np.shape), BF16, kind="Internal")
        dst = dst_t.ap()
        buf = Buf(dst, name + "_bf")
        p.dma(p.qgp, buf, src)
        self.done[name] = buf
        return buf


def strip_batch_ap(dram_ap, k0, nk, m0, mw, M):
    v = dram_ap[k0 * 128:(k0 + nk) * 128, m0:m0 + mw]
    return v.rearrange("(k p) m -> p k m", p=128)



def emit_l0ffn(p: Prog, sh: Shared, ln: "LNArena", xs: "XStream",
               x_in, x_out, wc: WCache, T=512):
    nt = SEQ // T
    nch = T // 128

    W1T = p.hostw["L0_ffn_w1T"]
    W2T = p.hostw["L0_ffn_w2T"]
    S1T = p.hostw["L0_ffn_sfc1T"]
    S2T = p.hostw["L0_ffn_sfc2T"]
    S3T = p.hostw["L0_ffn_sfc3T"]
    w1d = wc.get("l0f_w1T", W1T)
    w2d = wc.get("l0f_w2T", W2T)
    s1d = wc.get("l0f_s1T", S1T)
    s2d = wc.get("l0f_s2T", S2T)
    s3d = wc.get("l0f_s3T", S3T)

    gwT = p.hostw["L0_ffn_gate_wT"]
    gwc = p.const("l0f_gwT", gwT)
    gstr = [p.sbuf("l0f_gw", (128, 16), F32) for _ in range(8)]
    for k in range(8):
        p.dma(p.qsync, gstr[k], gwc[k * 128:(k + 1) * 128, :])
    ps4 = p.pbank(4)
    for k in range(8):
        p.mm(ps4, sh.ones_col_f32, gstr[k], start=(k == 0), stop=(k == 7),
             oap=ps4.ap[0:1, 0:16])
    gsum_row = p.sbuf("l0f_gsum", (1, 16))
    p.copy(gsum_row, ps4, aap=ps4.ap[0:1, 0:16], eng=p.scal)
    gsum_col = p.sbuf("l0f_gsumc", (16, 1))
    p.op(p.tens, lambda e: e.transpose(
        ps4.ap[0:16, 0:1], gsum_row.ap, sh.ident_f32.ap[0:1, 0:1]),
        reads=[gsum_row, sh.ident_f32], writes=[ps4])
    p.copy(gsum_col, ps4, aap=ps4.ap[0:16, 0:1], eng=p.scal)
    glog_t1 = p.sbuf("l0f_glt1", (16, T))
    glog_t2 = p.sbuf("l0f_glt2", (16, T))
    glog_t0 = p.sbuf("l0f_glt0", (16, T))

    ddr_disp = p.dram_tmp_buf("l0f_dispddr", (16, T))
    xn = [p.sbuf("l0f_xn", (128, T), BF16) for _ in range(DT)]
    glog = p.sbuf("l0f_glog", (16, T))
    lgt = p.sbuf("l0f_lgt", (128, 16))
    m1n = p.sbuf("l0f_m1n", (128, 1))
    m2 = p.sbuf("l0f_m2", (128, 1))
    e2 = p.sbuf("l0f_e2", (128, 1))
    winv = p.sbuf("l0f_winv", (128, 1))
    w2v = p.sbuf("l0f_w2v", (128, 1))
    eq1 = p.sbuf("l0f_eq1", (128, 16))
    eq2 = p.sbuf("l0f_eq2", (128, 16))
    dtok = p.sbuf("l0f_dtok", (128, 16))
    disp_fm = p.sbuf("l0f_dispfm", (16, T))
    disp_flat = p.sbuf("l0f_dispflat", (1, 16 * T), BF16)
    gg = [p.sbuf("l0f_gg", (128, T), BF16) for _ in range(16)]
    scaled = [p.sbuf("l0f_sc", (128, T), BF16) for _ in range(64)]
    sfc_ar = [p.sbuf("l0f_sfar", (128, 8 * 128), BF16) for _ in range(4)]
    w1_ar = [p.sbuf("l0f_w1ar", (128, 8 * 512), BF16) for _ in range(2)]
    kstrip = [p.sbuf("l0f_kstr", (128, 1024), BF16) for _ in range(2)]
    hscr = [p.sbuf("l0f_h", (128, T), BF16) for _ in range(2)]
    g2scr = [p.sbuf("l0f_g2", (128, T), BF16) for _ in range(2)]
    yb = [p.sbuf("l0f_y", (128, T), BF16) for _ in range(DT)]

    for tb in range(nt):
        t0 = tb * T
        xs.load(x_in, tb, buf=0)
        xf, xb = xs.x[0], xs.xb[0]
        ln.stats(xb, double=True)
        ln.apply(xb, xn)
        psg = p.pbank(0)
        for k in range(8):
            p.mm(psg, gstr[k], xf[k], start=(k == 0), stop=(k == 7),
                 oap=psg.ap[0:16, :])
        p.copy(glog_t0, psg, aap=psg.ap[0:16, :], eng=p.scal)
        p.tt(glog_t1, glog_t0, ln.ps_BA, ALU.mult, bap=ln.ps_BA.ap[0:16, :])
        p.ts(glog_t2, ln.ps_BB, gsum_col.ap[:, 0:1], ALU.mult,
             aap=ln.ps_BB.ap[0:16, :], extra_reads=[gsum_col])
        p.tt(glog, glog_t1, glog_t2, ALU.subtract)
        ps3 = p.pbank(3)
        for c in range(nch):
            p.op(p.tens, lambda e, c=c: e.transpose(
                ps3.ap[:, 0:16], glog.ap[:, c * 128:(c + 1) * 128],
                sh.ident_f32.ap[0:16, 0:16]), reads=[glog, sh.ident_f32], writes=[ps3])
            p.copy(lgt, ps3, aap=ps3.ap[:, 0:16])
            p.reduce(m1n, lgt, ALU.max, negate=True)
            p.ts(eq1, lgt, m1n.ap[:, 0:1], ALU.add, extra_reads=[m1n])
            p.ts(eq1, eq1, 0.0, ALU.is_equal)
            p.stt(eq2, eq1, -1e30, lgt, ALU.mult, ALU.add)
            p.reduce(m2, eq2, ALU.max)
            p.act(e2, m2, AF.Exp, bias=m1n.ap[:, 0:1], extra_reads=[m1n])
            p.ts(winv, e2, 1.0, ALU.add)
            p.op(p.vect, lambda e: e.reciprocal(out=winv.ap, in_=winv.ap),
                 reads=[winv], writes=[winv])
            p.tt(w2v, e2, winv, ALU.mult)
            p.ts(eq2, lgt, m2.ap[:, 0:1], ALU.is_equal, extra_reads=[m2])
            p.ts(dtok, eq1, winv.ap[:, 0:1], ALU.mult, extra_reads=[winv])
            p.stt(dtok, eq2, w2v.ap[:, 0:1], dtok, ALU.mult, ALU.add,
                  extra_reads=[w2v])
            p.op(p.tens, lambda e, c=c: e.transpose(
                ps3.ap[0:16, 0:128], dtok.ap, sh.ident_f32.ap),
                reads=[dtok, sh.ident_f32], writes=[ps3])
            p.copy(disp_fm, ps3, oap=disp_fm.ap[:, c * 128:(c + 1) * 128],
                   aap=ps3.ap[0:16, 0:128], eng=p.scal)
        p.dma(p.qgp, ddr_disp, disp_fm)
        p.dma(p.qgp, disp_flat, ddr_disp.ap.rearrange("a b -> (a b)"),
              reads=[ddr_disp])
        for m in range(16):
            ar1 = sfc_ar[(m % 2) * 2]
            ar2s = sfc_ar[(m % 2) * 2 + 1]
            ps0 = p.pbank((m % 2) * 2)
            ps1b = p.pbank((m % 2) * 2 + 1)
            p.dma(p.qsync, ar1, strip_batch_ap(s1d.ap, 0, 8, m * 128, 128, 2048), reads=[s1d])
            p.dma(p.qsync, ar2s, strip_batch_ap(s2d.ap, 0, 8, m * 128, 128, 2048), reads=[s2d])
            for k in range(8):
                p.mm(ps0, ar1, xn[k], start=(k == 0), stop=(k == 7),
                     lap=ar1.ap[:, k * 128:(k + 1) * 128])
            p.act(gg[m], ps0, AF.Silu)
            for k in range(8):
                p.mm(ps1b, ar2s, xn[k], start=(k == 0), stop=(k == 7),
                     lap=ar2s.ap[:, k * 128:(k + 1) * 128])
            p.copy(g2scr[m % 2], ps1b, eng=p.scal)
            p.tt(gg[m], gg[m], g2scr[m % 2], ALU.mult)
        ps_db = p.pbank(2)
        for e_i in range(16):
            ar = w1_ar[e_i % 2]
            p.dma(p.qsync, ar, strip_batch_ap(
                w1d.ap[e_i * 1024:(e_i + 1) * 1024, :], 0, 8, 0, 512, 512),
                reads=[w1d])
            p.mm(ps_db, sh.ones_row_bf, disp_flat, start=True, stop=True,
                 rap=disp_flat.ap[0:1, e_i * T:(e_i + 1) * T])
            for mh in range(4):
                ps = p.pbank(mh % 2)
                for k in range(8):
                    p.mm(ps, ar, xn[k], start=(k == 0), stop=(k == 7),
                         lap=ar.ap[:, k * 512 + mh * 128:k * 512 + (mh + 1) * 128])
                hb = hscr[mh % 2]
                p.act(hb, ps, AF.Silu)
                p.tt(scaled[e_i * 4 + mh], hb, ps_db, ALU.mult)
        banks = [p.pbank(i) for i in range(8)]
        for k in range(64):
            ks = kstrip[k % 2]
            p.dma(p.qsync, ks, w2d.ap[k * 128:(k + 1) * 128, :], reads=[w2d])
            for m in range(8):
                p.mm(banks[m], ks, scaled[k], start=(k == 0), stop=False,
                     lap=ks.ap[:, m * 128:(m + 1) * 128], inc=(m == 7))
        for k in range(16):
            ks = kstrip[k % 2]
            p.dma(p.qsync, ks, s3d.ap[k * 128:(k + 1) * 128, :], reads=[s3d])
            for m in range(8):
                p.mm(banks[m], ks, gg[k], start=False, stop=(k == 15),
                     lap=ks.ap[:, m * 128:(m + 1) * 128], inc=(m == 7))
        for m in range(8):
            p.copy(yb[m], banks[m], eng=p.scal)
        ln.stats(yb, double=False)
        ln.apply_add(yb, xf, xf)
        store_x(p, xf, x_out, tb, T)


def emit_l3attn(p: Prog, sh: Shared, ln: "LNArena", xs: "XStream",
                x_in, x_out, wc: WCache, T=512):
    nt = SEQ // T
    wq = wc.get("l3a_Wq", p.hostw["L3_attn_Wq"])
    wk = wc.get("l3a_Wk", p.hostw["L3_attn_Wk"])
    wv = wc.get("l3a_Wv", p.hostw["L3_attn_Wv"])
    wo = wc.get("l3a_Wo", p.hostw["L3_attn_Wo"])

    Dh = 128
    inv = 1.0 / (10000.0 ** (np.arange(0, Dh, 2, dtype=np.float64) / Dh))
    tpos = np.arange(SEQ, dtype=np.float64)
    fr = np.outer(tpos, inv)
    emb = np.concatenate([fr, fr], axis=1)
    cos = np.cos(emb).T.astype(np.float32)
    sin = np.sin(emb).T.astype(np.float32)
    ssin = sin.copy()
    ssin[0:64] = -ssin[0:64]
    sc = 1.0 / math.sqrt(Dh)
    qcos = p.sbuf("l3a_cos", (128, SEQ), BF16)
    qsin = p.sbuf("l3a_sin", (128, SEQ), BF16)
    kcos, ksin = qcos, qsin
    p.dma(p.qsync, qcos, p.const("c_l3a_cos", cos.astype(BF_NP)))
    p.dma(p.qsync, qsin, p.const("c_l3a_sin", ssin.astype(BF_NP)))
    msks = []
    for r in range(4):
        m = np.full((128, 512), -1e30, np.float32)
        for row in range(128):
            m[row, :r * 128 + row + 1] = 0.0
        mb = p.sbuf(f"l3a_msk{r}", (128, 512))
        p.dma(p.qsync, mb, p.const(f"c_l3a_msk{r}", m))
        msks.append(mb)

    q_ddr = p.dram_tmp_buf("l3a_q", (D, SEQ), BF16)
    v_ddr = p.dram_tmp_buf("l3a_v", (SEQ, 512), BF16)
    k_res = [p.sbuf("l3a_kres", (128, SEQ), BF16) for _ in range(4)]

    xn = [p.sbuf("l3a_xn", (128, T), BF16) for _ in range(DT)]
    wstr = [p.sbuf("l3a_wstr", (128, 4096), BF16) for _ in range(2)]
    ev = [p.sbuf("l3a_ev", (128, 512), BF16) for _ in range(2)]
    rs1v = [p.sbuf("l3a_rs1", (128, 512)) for _ in range(2)]
    rs2v = [p.sbuf("l3a_rs2", (128, 512)) for _ in range(2)]

    def rope(dst_bf, src_ps, cosd, sind, t0, Tw, par=0):
        rs1, rs2 = rs1v[par], rs2v[par]
        p.tt(rs1, src_ps, cosd, ALU.mult, bap=cosd.ap[:, t0:t0 + Tw])
        p.tt(rs2, src_ps, sind, ALU.mult, aap=src_ps.ap[64:128, :],
             bap=sind.ap[0:64, t0:t0 + Tw], oap=rs2.ap[0:64, :])
        p.tt(rs2, src_ps, sind, ALU.mult, aap=src_ps.ap[0:64, :],
             bap=sind.ap[64:128, t0:t0 + Tw], oap=rs2.ap[64:128, :])
        p.tt(dst_bf, rs1, rs2, ALU.add)

    for tb in range(nt):
        t0 = tb * T
        xs.load(x_in, tb, buf=0)
        xf, xb = xs.x[0], xs.xb[0]
        ln.stats(xb, double=False)
        ln.apply(xb, xn)
        for m in range(8):
            ps = p.pbank(m % 2)
            ar = wstr[m % 2]
            p.op(p.qsync, lambda e, ar=ar, m=m: e.dma_start(
                out=ar.ap[:, 0:1024],
                in_=strip_batch_ap(wq.ap, 0, 8, m * 128, 128, D)),
                reads=[wq], writes=[ar])
            for k in range(8):
                p.mm(ps, ar, xn[k], start=(k == 0), stop=(k == 7),
                     lap=ar.ap[:, k * 128:(k + 1) * 128])
            rope(ev[m % 2], ps, qcos, qsin, t0, T, par=m % 2)
            p.dma(p.qgp, q_ddr.ap[m * 128:(m + 1) * 128, t0:t0 + T], ev[m % 2],
                  writes=[q_ddr])
        for m in range(4):
            ps = p.pbank(m % 2)
            ar = wstr[m % 2]
            p.op(p.qsync, lambda e, ar=ar, m=m: e.dma_start(
                out=ar.ap[:, 0:1024],
                in_=strip_batch_ap(wk.ap, 0, 8, m * 128, 128, 512)),
                reads=[wk], writes=[ar])
            for k in range(8):
                p.mm(ps, ar, xn[k], start=(k == 0), stop=(k == 7),
                     lap=ar.ap[:, k * 128:(k + 1) * 128])
            kr = Buf(k_res[m].ap[:, t0:t0 + T], k_res[m].name)
            rope(kr, ps, kcos, ksin, t0, T, par=m % 2)
            k_res[m].wtok = kr.wtok
        for tc in range(T // 128):
            ps = p.pbank(2)
            ar = wstr[tc % 2]
            p.op(p.qsync, lambda e, ar=ar: e.dma_start(
                out=ar.ap[:, 0:4096],
                in_=strip_batch_ap(wv.ap, 0, 8, 0, 512, 512)),
                reads=[wv], writes=[ar])
            for k in range(8):
                p.mm(ps, xn[k], ar, start=(k == 0), stop=(k == 7),
                     lap=xn[k].ap[:, tc * 128:(tc + 1) * 128],
                     rap=ar.ap[:, k * 512:(k + 1) * 512])
            e2 = ev[tc % 2]
            p.copy(e2, ps, eng=p.scal)
            p.dma(p.qgp, v_ddr.ap[t0 + tc * 128:t0 + (tc + 1) * 128, :], e2,
                  writes=[v_ddr])

    q_sb = p.sbuf("l3a_qsb", (128, 8 * 128), BF16)
    v_sb = [p.sbuf("l3a_vsb", (128, 512), BF16) for _ in range(32)]
    p_exp4 = [p.sbuf("l3a_pexp", (128, 1024)) for _ in range(4)]
    pT = [p.sbuf("l3a_pT", (128, 128), BF16) for _ in range(2)]
    nmA = [p.sbuf("l3a_nmA", (128, 1)) for _ in range(2)]
    nmN = [p.sbuf("l3a_nmN", (128, 1)) for _ in range(2)]
    nmS = [p.sbuf("l3a_nmS", (128, 1)) for _ in range(2)]
    lsum = [p.sbuf("l3a_lsum", (128, 1)) for _ in range(2)]
    lB = [p.sbuf("l3a_lB", (128, 1)) for _ in range(2)]
    att_tok = p.sbuf("l3a_atok", (128, 1024))
    att_fm = [p.sbuf("l3a_afm", (128, T), BF16) for _ in range(DT)]
    yb = [p.sbuf("l3a_y", (128, T), BF16) for _ in range(DT)]
    banks = [p.pbank(i) for i in range(8)]

    for qt in range(32):
        p.dma(p.qsync, q_sb,
              q_ddr.ap[:, qt * 128:(qt + 1) * 128].rearrange("(k p) m -> p k m", p=128),
              reads=[q_ddr])
        p.dma(p.qsync, v_sb[qt], v_ddr.ap[qt * 128:(qt + 1) * 128, :], reads=[v_ddr])
        nb = qt // 4 + 1
        lastw = (qt % 4 + 1) * 128
        for qh in range(8):
            kv = qh // 2
            par = qh % 2
            widths = [512 if b < nb - 1 else lastw for b in range(nb)]
            for b in range(nb):
                w = widths[b]
                bk = banks[b % 4]
                p.mm(bk, q_sb, k_res[kv], start=True, stop=True,
                     lap=q_sb.ap[:, qh * 128:(qh + 1) * 128],
                     rap=k_res[kv].ap[:, b * 512:b * 512 + w],
                     oap=bk.ap[:, 0:w])
                if b == nb - 1:
                    p.tt(bk, bk, msks[qt % 4], ALU.add,
                         oap=bk.ap[:, 0:lastw], aap=bk.ap[:, 0:lastw],
                         bap=msks[qt % 4].ap[:, 0:lastw])
                pe = p_exp4[b // 2]
                po = (b % 2) * 512
                p.copy(pe, bk, oap=pe.ap[:, po:po + w],
                       aap=bk.ap[:, 0:w], eng=p.scal)
            for b in range(nb):
                pe = p_exp4[b // 2]
                po = (b % 2) * 512
                dst = nmA[par] if b == 0 else nmN[par]
                p.reduce(dst, pe, ALU.max, negate=True,
                         aap=pe.ap[:, po:po + widths[b]])
                if b > 0:
                    p.tt(nmA[par], nmA[par], nmN[par], ALU.min)
            p.ts(nmS[par], nmA[par], 1.0 / math.sqrt(128.0), ALU.mult)
            for b in range(nb):
                w = widths[b]
                pe = p_exp4[b // 2]
                po = (b % 2) * 512
                p.act(pe, pe, AF.Exp, bias=nmS[par].ap[:, 0:1],
                      scale=1.0 / math.sqrt(128.0),
                      accum=(lsum[par] if b == 0 else lB[par]),
                      accum_ap=(lsum[par] if b == 0 else lB[par]).ap[:, 0:1],
                      oap=pe.ap[:, po:po + w],
                      aap=pe.ap[:, po:po + w],
                      extra_reads=[nmS[par]])
                if b > 0:
                    p.tt(lsum[par], lsum[par], lB[par], ALU.add)
            tbk = banks[4 + par]
            abk = banks[6 + par]
            nsub = (qt + 1)
            first = True
            for s_i in range(nsub):
                pe = p_exp4[s_i // 8]
                so = (s_i % 8) * 128
                p.op(p.tens, lambda e, pe=pe, so=so, tbk=tbk: e.transpose(
                    tbk.ap[:, 0:128],
                    pe.ap[:, so:so + 128],
                    sh.ident_f32.ap), reads=[pe, sh.ident_f32], writes=[tbk])
                p.copy(pT[par], tbk, aap=tbk.ap[:, 0:128])
                p.mm(abk, pT[par], v_sb[s_i],
                     start=first, stop=(s_i == nsub - 1),
                     rap=v_sb[s_i].ap[:, kv * 128:(kv + 1) * 128],
                     oap=abk.ap[:, 0:128], inc=True)
                first = False
            p.op(p.vect, lambda e, par=par: e.reciprocal(
                out=lsum[par].ap, in_=lsum[par].ap),
                reads=[lsum[par]], writes=[lsum[par]])
            p.ts(att_tok, abk, lsum[par].ap[:, 0:1], ALU.mult,
                 aap=abk.ap[:, 0:128],
                 oap=att_tok.ap[:, qh * 128:(qh + 1) * 128],
                 extra_reads=[lsum[par]])
        co = (qt % 4) * 128
        for dd in range(DT):
            p.op(p.tens, lambda e, dd=dd: e.transpose(
                banks[0].ap[:, 0:128], att_tok.ap[:, dd * 128:(dd + 1) * 128],
                sh.ident_f32.ap), reads=[att_tok, sh.ident_f32], writes=[banks[0]])
            p.copy(att_fm[dd], banks[0], oap=att_fm[dd].ap[:, co:co + 128],
                   aap=banks[0].ap[:, 0:128])
        if qt % 4 == 3:
            tb = qt // 4
            xs.load(x_in, tb, buf=0)
            xf = xs.x[0]
            for m in range(8):
                ps = p.pbank(0 if m % 2 == 0 else 1)
                ar = wstr[m % 2]
                p.op(p.qsync, lambda e, ar=ar, m=m: e.dma_start(
                    out=ar.ap[:, 0:1024],
                    in_=strip_batch_ap(wo.ap, 0, 8, m * 128, 128, D)),
                    reads=[wo], writes=[ar])
                for k in range(8):
                    p.mm(ps, ar, att_fm[k], start=(k == 0), stop=(k == 7),
                         lap=ar.ap[:, k * 128:(k + 1) * 128])
                p.copy(yb[m], ps, eng=p.scal)
            ln.stats(yb, double=False)
            ln.apply_add(yb, xf, xf)
            store_x(p, xf, x_out, tb, T)


def host_weights(I):
    H = {}
    H["L3_ffn_WiT"] = wT(I["L3_ffn_Wi"])
    H["L3_ffn_WoT"] = wT(I["L3_ffn_Wo"])
    H["L3_ffn_AT"] = np.ascontiguousarray(
        np.transpose(np.asarray(I["L3_ffn_A"], np.float32), (0, 2, 1)))
    H["L3_ffn_BT"] = np.ascontiguousarray(
        np.transpose(np.asarray(I["L3_ffn_B"], np.float32), (0, 2, 1)))
    H["L3_ffn_WgT"] = wT(I["L3_ffn_Wg"])
    H["L2_attn_WqT"] = wT(I["L2_attn_Wq"])
    H["L2_attn_WkT"] = wT(I["L2_attn_Wk"])
    H["L2_attn_WvT"] = wT(I["L2_attn_Wv"])
    H["L2_attn_WoT"] = wT(I["L2_attn_Wo"])
    for nm in ("Wq", "Wk", "Wv", "Wo"):
        H["L3_attn_" + nm] = np.ascontiguousarray(np.asarray(I["L3_attn_" + nm], np.float32))
    H["L0_attn_in_projT"] = wT(I["L0_attn_in_proj_w"])
    H["L0_attn_x_projT"] = wT(I["L0_attn_x_proj_w"])
    H["L0_attn_dt_projT"] = wT(I["L0_attn_dt_proj_w"])
    H["L0_attn_out_projT"] = wT(I["L0_attn_out_proj_w"])
    H["L0_attn_convw"] = np.ascontiguousarray(
        np.asarray(I["L0_attn_conv_w"], np.float32).reshape(2048, 4))
    H["L0_attn_in_proj_b2"] = np.asarray(I["L0_attn_in_proj_b"], np.float32).reshape(-1, 1)
    H["L0_attn_dt_proj_b2"] = np.asarray(I["L0_attn_dt_proj_b"], np.float32).reshape(-1, 1)
    H["L0_attn_conv_b2"] = np.asarray(I["L0_attn_conv_b"], np.float32).reshape(-1, 1)
    H["L0_attn_out_proj_b2"] = np.asarray(I["L0_attn_out_proj_b"], np.float32).reshape(-1, 1)
    H["L0_attn_D_param2"] = np.asarray(I["L0_attn_D_param"], np.float32).reshape(-1, 1)
    H["L0_attn_norm_w2"] = np.asarray(I["L0_attn_norm_w"], np.float32).reshape(-1, 1)
    H["L0_attn_norm_b2"] = np.asarray(I["L0_attn_norm_b"], np.float32).reshape(-1, 1)
    H["L0_attn_A_log"] = np.ascontiguousarray(np.asarray(I["L0_attn_A_log"], np.float32))
    H["L1_attn_in_projT"] = wT(I["L1_attn_in_proj_w"])
    H["L1_attn_out_projT"] = wT(I["L1_attn_out_proj_w"])
    H["L1_attn_convw17"] = np.ascontiguousarray(np.asarray(I["L1_attn_conv_w"], np.float32))
    H["L1_attn_conv_b17"] = np.asarray(I["L1_attn_conv_b"], np.float32).reshape(-1, 1)
    H["L1_attn_norm_w2"] = np.asarray(I["L1_attn_norm_w"], np.float32).reshape(-1, 1)
    H["L1_attn_init_states"] = np.asarray(I["L1_attn_init_states"], np.float32)
    H["L0_ffn_gate_wT"] = wT(I["L0_ffn_gate_w"])
    H["L0_ffn_w1T"] = np.ascontiguousarray(
        np.transpose(np.asarray(I["L0_ffn_w1"], np.float32), (0, 2, 1)).reshape(16 * 1024, 512))
    H["L0_ffn_w2T"] = np.ascontiguousarray(
        np.transpose(np.asarray(I["L0_ffn_w2"], np.float32), (0, 2, 1)).reshape(16 * 512, 1024))
    H["L0_ffn_sfc1T"] = wT(I["L0_ffn_sfc1"])
    H["L0_ffn_sfc2T"] = wT(I["L0_ffn_sfc2"])
    H["L0_ffn_sfc3T"] = wT(I["L0_ffn_sfc3"])
    return H



def emit_l3ffn(p: Prog, sh: Shared, ln: "LNArena", xs: "XStream",
               x_in, x_out, wc: WCache, T=512):
    nt = SEQ // T
    nch = T // 128
    WiT = p.hostw["L3_ffn_WiT"]
    WoT = p.hostw["L3_ffn_WoT"]
    A_T = p.hostw["L3_ffn_AT"]
    B_T = p.hostw["L3_ffn_BT"]
    WgT = p.hostw["L3_ffn_WgT"]
    wid = wc.get("l3f_WiT", WiT)
    wod = wc.get("l3f_WoT", WoT)

    gwc = p.const("l3f_WgT", WgT)
    gstr = [p.sbuf("l3f_gw", (128, 4), F32) for _ in range(8)]
    for k in range(8):
        p.dma(p.qsync, gstr[k], gwc[k * 128:(k + 1) * 128, :])
    ps4 = p.pbank(4)
    for k in range(8):
        p.mm(ps4, sh.ones_col_f32, gstr[k], start=(k == 0), stop=(k == 7),
             oap=ps4.ap[0:1, 0:4])
    gsum_row = p.sbuf("l3f_gsum", (1, 4))
    p.copy(gsum_row, ps4, aap=ps4.ap[0:1, 0:4], eng=p.scal)
    gsum_col = p.sbuf("l3f_gsumc", (4, 1))
    p.op(p.tens, lambda e: e.transpose(
        ps4.ap[0:4, 0:1], gsum_row.ap, sh.ident_f32.ap[0:1, 0:1]),
        reads=[gsum_row, sh.ident_f32], writes=[ps4])
    p.copy(gsum_col, ps4, aap=ps4.ap[0:4, 0:1], eng=p.scal)
    bg_cols = load_cols(p, sh, p.dram_in("L3_ffn_bg", (4,)), 4, "l3f_bg")

    a_c = p.const("l3f_AT", A_T.reshape(4 * 1024, 4))
    astr = [p.sbuf("l3f_astr", (128, 16), BF16) for _ in range(8)]
    for k in range(8):
        p.op(p.qgp, lambda e, k=k: e.dma_start(
            out=astr[k].ap.rearrange("p (e r) -> p e r", e=4),
            in_=a_c.rearrange("(e d) r -> d e r", e=4)[k * 128:(k + 1) * 128]),
            writes=[astr[k]])
    b_c = p.const("l3f_BT", B_T.reshape(16, 4096))
    bstr = [p.sbuf("l3f_bstr", (4, 4096), BF16) for _ in range(4)]
    for e_i in range(4):
        p.dma(p.qgp, bstr[e_i], b_c[e_i * 4:(e_i + 1) * 4, :])

    ddr_low = p.dram_tmp_buf("l3f_lowddr", (16, T))
    ddr_w = p.dram_tmp_buf("l3f_wddr", (4, T))
    glog = p.sbuf("l3f_glog", (4, T))
    gl_t1 = p.sbuf("l3f_glt1", (4, T))
    gl_t2 = p.sbuf("l3f_glt2", (4, T))
    gl_t0 = p.sbuf("l3f_glt0", (4, T))
    low_all = p.sbuf("l3f_lowall", (16, T))
    low_e = [p.sbuf("l3f_lowe", (4, T), BF16) for _ in range(4)]
    lgt = p.sbuf("l3f_lgt", (128, 4))
    m1n = p.sbuf("l3f_m1n", (128, 1))
    m2 = p.sbuf("l3f_m2", (128, 1))
    ssum = p.sbuf("l3f_ssum", (128, 1))
    stok = p.sbuf("l3f_stok", (128, 4))
    keep = p.sbuf("l3f_keep", (128, 4))
    mask4 = p.sbuf("l3f_mask4", (128, 4))
    w_fm = p.sbuf("l3f_wfm", (4, T))
    w_flat = p.sbuf("l3f_wflat", (1, 4 * T), BF16)
    xn = [p.sbuf("l3f_xn", (128, T), BF16) for _ in range(DT)]
    base_sb2 = [p.sbuf("l3f_base", (128, T), BF16) for _ in range(2)]
    hscr = [p.sbuf("l3f_h", (128, T), BF16) for _ in range(2)]
    htmp = p.sbuf("l3f_ht", (128, T))
    htmp2 = p.sbuf("l3f_ht2", (128, T))
    wbc = [p.sbuf("l3f_wbc", (128, T), BF16) for _ in range(4)]
    eprod = [p.sbuf("l3f_ep", (128, T), BF16) for _ in range(4)]
    hsum = [p.sbuf("l3f_hsum", (128, T), BF16) for _ in range(32)]
    wi_ar = [p.sbuf("l3f_wiar", (128, 1024), BF16) for _ in range(2)]
    wo_ar = [p.sbuf("l3f_woar", (128, 1024), BF16) for _ in range(2)]
    yb = [p.sbuf("l3f_y", (128, T), BF16) for _ in range(DT)]

    for tb in range(nt):
        xs.load(x_in, tb, buf=0)
        xf, xb = xs.x[0], xs.xb[0]
        ln.stats(xb, double=True)
        ln.apply(xb, xn)
        psg = p.pbank(3)
        for k in range(8):
            p.mm(psg, gstr[k], xf[k], start=(k == 0), stop=(k == 7),
                 oap=psg.ap[0:4, :])
        p.copy(gl_t0, psg, aap=psg.ap[0:4, :], eng=p.scal)
        p.tt(gl_t1, gl_t0, ln.ps_BA, ALU.mult, bap=ln.ps_BA.ap[0:4, :])
        p.ts(gl_t2, ln.ps_BB, gsum_col.ap[:, 0:1], ALU.mult,
             aap=ln.ps_BB.ap[0:4, :], extra_reads=[gsum_col])
        p.tt(glog, gl_t1, gl_t2, ALU.subtract)
        p.ts(glog, glog, bg_cols[0].ap[0:4, 0:1], ALU.add,
             extra_reads=[bg_cols[0]])
        psl = p.pbank(3)
        for k in range(8):
            p.mm(psl, astr[k], xn[k], start=(k == 0), stop=(k == 7),
                 oap=psl.ap[0:16, :])
        p.copy(low_all, psl, aap=psl.ap[0:16, :], eng=p.scal)
        p.dma(p.qgp, ddr_low, low_all)
        for e_i in range(4):
            p.dma(p.qgp, low_e[e_i], ddr_low.ap[e_i * 4:(e_i + 1) * 4, :],
                  reads=[ddr_low])
        ps3 = p.pbank(3)
        for c in range(nch):
            p.op(p.tens, lambda e, c=c: e.transpose(
                ps3.ap[:, 0:4], glog.ap[:, c * 128:(c + 1) * 128],
                sh.ident_f32.ap[0:4, 0:4]), reads=[glog, sh.ident_f32], writes=[ps3])
            p.copy(lgt, ps3, aap=ps3.ap[:, 0:4])
            p.reduce(m1n, lgt, ALU.max, negate=True)
            p.act(stok, lgt, AF.Exp, bias=m1n.ap[:, 0:1], extra_reads=[m1n])
            p.reduce(ssum, stok, ALU.add)
            p.op(p.vect, lambda e: e.reciprocal(out=ssum.ap, in_=ssum.ap),
                 reads=[ssum], writes=[ssum])
            p.ts(stok, stok, ssum.ap[:, 0:1], ALU.mult, extra_reads=[ssum])
            p.reduce(m1n, stok, ALU.max, negate=True)
            p.ts(keep, stok, m1n.ap[:, 0:1], ALU.add, extra_reads=[m1n])
            p.ts(keep, keep, 0.0, ALU.is_equal)
            p.stt(mask4, keep, -1e30, stok, ALU.mult, ALU.add)
            p.reduce(m2, mask4, ALU.max)
            p.ts(keep, stok, m2.ap[:, 0:1], ALU.is_ge, extra_reads=[m2])
            p.tt(stok, stok, keep, ALU.mult)
            p.op(p.tens, lambda e, c=c: e.transpose(
                ps3.ap[0:4, 0:128], stok.ap, sh.ident_f32.ap),
                reads=[stok, sh.ident_f32], writes=[ps3])
            p.copy(w_fm, ps3, oap=w_fm.ap[:, c * 128:(c + 1) * 128],
                   aap=ps3.ap[0:4, 0:128], eng=p.scal)
        p.dma(p.qgp, ddr_w, w_fm)
        p.dma(p.qgp, w_flat, ddr_w.ap.rearrange("a b -> (a b)"), reads=[ddr_w])
        ps_w = p.pbank(2)
        for e_i in range(4):
            p.mm(ps_w, sh.ones_row_bf, w_flat, start=True, stop=True,
                 rap=w_flat.ap[0:1, e_i * T:(e_i + 1) * T])
            p.copy(wbc[e_i], ps_w, eng=p.scal)
        for mf in range(32):
            ar = wi_ar[mf % 2]
            ps_base = p.pbank(0 if mf % 2 == 0 else 2)
            p.dma(p.qsync, ar, strip_batch_ap(wid.ap, 0, 8, mf * 128, 128, 4096),
                  reads=[wid])
            for k in range(8):
                p.mm(ps_base, ar, xn[k], start=(k == 0), stop=(k == 7),
                     lap=ar.ap[:, k * 128:(k + 1) * 128])
            bsb = base_sb2[mf % 2]
            p.copy(bsb, ps_base, eng=p.scal)
            for e_i in range(4):
                ps_e = p.pbank(1 if e_i % 2 == 0 else 3)
                p.mm(ps_e, sh.ident_bf, bsb, start=True, stop=False)
                p.mm(ps_e, bstr[e_i], low_e[e_i], start=False, stop=True,
                     lap=bstr[e_i].ap[:, mf * 128:(mf + 1) * 128])
                hb = hscr[e_i % 2]
                p.act(hb, ps_e, AF.Relu)
                p.tt(eprod[e_i], hb, wbc[e_i], ALU.mult,
                     eng=(p.vect if e_i < 2 else p.pool))
            p.tt(htmp, eprod[0], eprod[1], ALU.add)
            p.tt(htmp2, eprod[2], eprod[3], ALU.add, eng=p.pool)
            p.tt(hsum[mf], htmp, htmp2, ALU.add)
        banks = [p.pbank(i) for i in range(8)]
        for k in range(32):
            ks = wo_ar[k % 2]
            p.dma(p.qsync, ks, wod.ap[k * 128:(k + 1) * 128, :], reads=[wod])
            for m in range(8):
                p.mm(banks[m], ks, hsum[k], start=(k == 0), stop=(k == 31),
                     lap=ks.ap[:, m * 128:(m + 1) * 128], inc=(m == 7))
        for m in range(8):
            p.copy(yb[m], banks[m], eng=p.scal)
        ln.stats(yb, double=False)
        ln.apply_add(yb, xf, xf)
        store_x(p, xf, x_out, tb, T)



def emit_l2attn(p: Prog, sh: Shared, ln: "LNArena", xs: "XStream",
                x_in, x_out, wc: WCache, T=512):
    nt = SEQ // T
    wq = wc.get("l2a_WqT", p.hostw["L2_attn_WqT"])
    wk = wc.get("l2a_WkT", p.hostw["L2_attn_WkT"])
    wv = wc.get("l2a_WvT", p.hostw["L2_attn_WvT"])
    wo = wc.get("l2a_WoT", p.hostw["L2_attn_WoT"])
    bo_cols = load_cols(p, sh, p.dram_in("L2_attn_bo", (D,)), D, "l2a_bo")

    q_ddr = p.dram_tmp_buf("l2a_q", (D, SEQ), BF16)
    k_ddr = p.dram_tmp_buf("l2a_k", (D, SEQ + 64), BF16)
    v_ddr = p.dram_tmp_buf("l2a_v", (SEQ + 64, D), BF16)

    band = np.full((128, 192), -1e30, np.float32)
    for r in range(128):
        band[r, r:r + 64] = 0.0
    band_sb = p.sbuf("l2a_band", (128, 192))
    p.dma(p.qsync, band_sb, p.const("c_l2a_band", band))

    xn = [p.sbuf("l2a_xn", (128, T), BF16) for _ in range(DT)]
    wstr = [p.sbuf("l2a_wstr", (128, 4096), BF16) for _ in range(2)]
    ev = [p.sbuf("l2a_ev", (128, 512), BF16) for _ in range(2)]
    zero_sb = p.sbuf("l2a_zero", (128, 1024), BF16)
    p.memset(zero_sb, 0.0)
    for kk in range(8):
        p.dma(p.qgp, k_ddr.ap[kk * 128:(kk + 1) * 128, 0:32], zero_sb.ap[:, 0:32],
              reads=[zero_sb], writes=[k_ddr])
        p.dma(p.qgp, k_ddr.ap[kk * 128:(kk + 1) * 128, SEQ + 32:SEQ + 64],
              zero_sb.ap[:, 0:32], reads=[zero_sb], writes=[k_ddr])
    p.dma(p.qgp, v_ddr.ap[0:32, :], zero_sb.ap[0:32, :], reads=[zero_sb], writes=[v_ddr])
    p.dma(p.qgp, v_ddr.ap[SEQ + 32:SEQ + 64, :], zero_sb.ap[0:32, :],
          reads=[zero_sb], writes=[v_ddr])

    for tb in range(nt):
        t0 = tb * T
        xs.load(x_in, tb, buf=0)
        xf, xb = xs.x[0], xs.xb[0]
        ln.stats(xb, double=False)
        ln.apply(xb, xn)
        for (wd, dst, coff) in ((wq, q_ddr, 0), (wk, k_ddr, 32)):
            for m in range(8):
                ps = p.pbank(m % 2)
                ar = wstr[m % 2]
                p.op(p.qsync, lambda e, ar=ar, wd=wd, m=m: e.dma_start(
                    out=ar.ap[:, 0:1024],
                    in_=strip_batch_ap(wd.ap, 0, 8, m * 128, 128, D)),
                    reads=[wd], writes=[ar])
                for k in range(8):
                    p.mm(ps, ar, xn[k], start=(k == 0), stop=(k == 7),
                         lap=ar.ap[:, k * 128:(k + 1) * 128])
                e = ev[m % 2]
                p.copy(e, ps, eng=p.scal)
                p.dma(p.qgp, dst.ap[m * 128:(m + 1) * 128, coff + t0:coff + t0 + T],
                      e, writes=[dst])
        for tc in range(T // 128):
            for half in range(2):
                ps = p.pbank(2 + (tc * 2 + half) % 2)
                ar = wstr[(tc * 2 + half) % 2]
                p.dma(p.qsync, ar,
                      strip_batch_ap(wv.ap, 0, 8, half * 512, 512, D),
                      reads=[wv])
                for k in range(8):
                    p.mm(ps, xn[k], ar, start=(k == 0), stop=(k == 7),
                         lap=xn[k].ap[:, tc * 128:(tc + 1) * 128],
                         rap=ar.ap[:, k * 512:(k + 1) * 512])
                e = ev[(tc * 2 + half) % 2]
                p.copy(e, ps, eng=p.scal)
                p.dma(p.qgp, v_ddr.ap[32 + t0 + tc * 128:32 + t0 + (tc + 1) * 128,
                                      half * 512:(half + 1) * 512],
                      e, writes=[v_ddr])

    q_sb = p.sbuf("l2a_qsb", (128, 8 * 128), BF16)
    k_sb = p.sbuf("l2a_ksb", (128, 8 * 192), BF16)
    v_sb1 = p.sbuf("l2a_vsb1", (128, 1024), BF16)
    v_sb2 = p.sbuf("l2a_vsb2", (128, 1024), BF16)
    sc_m = p.sbuf("l2a_scm", (128, 192))
    negmax = p.sbuf("l2a_negmax", (128, 1))
    rowsum = p.sbuf("l2a_rowsum", (128, 16))
    p_exp = p.sbuf("l2a_pexp", (128, 192))
    pT_sb = p.sbuf("l2a_pT", (128, 256), BF16)
    att_tok = p.sbuf("l2a_atok", (128, 1024))
    att_fm = [p.sbuf("l2a_afm", (128, T), BF16) for _ in range(DT)]
    yb = [p.sbuf("l2a_y", (128, T), BF16) for _ in range(DT)]
    for qt in range(32):
        p.dma(p.qsync, q_sb,
              q_ddr.ap[:, qt * 128:(qt + 1) * 128].rearrange("(k p) m -> p k m", p=128),
              reads=[q_ddr])
        p.dma(p.qsync, k_sb,
              k_ddr.ap[:, qt * 128:qt * 128 + 192].rearrange("(k p) m -> p k m", p=128),
              reads=[k_ddr])
        p.dma(p.qsync, v_sb1, v_ddr.ap[qt * 128:(qt + 1) * 128, :], reads=[v_ddr])
        p.op(p.qsync, lambda e, qt=qt: e.dma_start(
            out=v_sb2.ap[0:64, :],
            in_=v_ddr.ap[(qt + 1) * 128:(qt + 1) * 128 + 64, :]),
            reads=[v_ddr], writes=[v_sb2])
        ps_s = p.pbank(0)
        ps_t = p.pbank(1)
        ps_o = [p.pbank(2), p.pbank(3)]
        for h in range(16):
            pb = (h % 2) * 64
            ks = h // 2
            p.mm(ps_s, q_sb, k_sb, start=True, stop=True,
                 lap=q_sb.ap[pb:pb + 64, ks * 128:(ks + 1) * 128],
                 rap=k_sb.ap[pb:pb + 64, ks * 192:(ks + 1) * 192],
                 oap=ps_s.ap[:, 0:192])
            p.stt(sc_m, ps_s, 0.125, band_sb, ALU.mult, ALU.add,
                  aap=ps_s.ap[:, 0:192])
            p.reduce(negmax, sc_m, ALU.max, negate=True)
            p.act(p_exp, sc_m, AF.Exp, bias=negmax.ap[:, 0:1],
                  accum=rowsum, accum_ap=rowsum.ap[:, h:h + 1],
                  extra_reads=[negmax])
            p.op(p.tens, lambda e: e.transpose(
                ps_t.ap[:, 0:128], p_exp.ap[:, 0:128], sh.ident_f32.ap),
                reads=[p_exp, sh.ident_f32], writes=[ps_t])
            p.copy(pT_sb, ps_t, oap=pT_sb.ap[:, 0:128], aap=ps_t.ap[:, 0:128])
            p.op(p.tens, lambda e: e.transpose(
                ps_t.ap[0:64, 128:256], p_exp.ap[:, 128:192], sh.ident_f32.ap),
                reads=[p_exp, sh.ident_f32], writes=[ps_t])
            p.copy(pT_sb, ps_t, oap=pT_sb.ap[0:64, 128:256],
                   aap=ps_t.ap[0:64, 128:256], eng=p.scal)
            po = ps_o[h // 8]
            oc = (h % 8) * 64
            p.mm(po, pT_sb, v_sb1, start=True, stop=False,
                 lap=pT_sb.ap[:, 0:128], rap=v_sb1.ap[:, h * 64:(h + 1) * 64],
                 oap=po.ap[:, oc:oc + 64])
            p.mm(po, pT_sb, v_sb2, start=False, stop=True,
                 lap=pT_sb.ap[0:64, 128:256], rap=v_sb2.ap[0:64, h * 64:(h + 1) * 64],
                 oap=po.ap[:, oc:oc + 64])
        p.op(p.vect, lambda e: e.reciprocal(out=rowsum.ap, in_=rowsum.ap),
             reads=[rowsum], writes=[rowsum])
        for h in range(16):
            po = ps_o[h // 8]
            oc = (h % 8) * 64
            p.ts(att_tok, po, rowsum.ap[:, h:h + 1], ALU.mult,
                 aap=po.ap[:, oc:oc + 64], oap=att_tok.ap[:, h * 64:(h + 1) * 64],
                 extra_reads=[rowsum])
        co = (qt % 4) * 128
        for dd in range(DT):
            p.op(p.tens, lambda e, dd=dd: e.transpose(
                ps_t.ap[:, 0:128], att_tok.ap[:, dd * 128:(dd + 1) * 128],
                sh.ident_f32.ap), reads=[att_tok, sh.ident_f32], writes=[ps_t])
            p.copy(att_fm[dd], ps_t, oap=att_fm[dd].ap[:, co:co + 128],
                   aap=ps_t.ap[:, 0:128])
        if qt % 4 == 3:
            tb = qt // 4
            xs.load(x_in, tb, buf=0)
            xf = xs.x[0]
            for m in range(8):
                ps = p.pbank(0 if m % 2 == 0 else 1)
                ar = wstr[m % 2]
                p.op(p.qsync, lambda e, ar=ar, m=m: e.dma_start(
                    out=ar.ap[:, 0:1024],
                    in_=strip_batch_ap(wo.ap, 0, 8, m * 128, 128, D)),
                    reads=[wo], writes=[ar])
                for k in range(8):
                    p.mm(ps, ar, att_fm[k], start=(k == 0), stop=(k == 7),
                         lap=ar.ap[:, k * 128:(k + 1) * 128])
                p.ts(yb[m], ps, bo_cols[m].ap[:, 0:1], ALU.add,
                     extra_reads=[bo_cols[m]])
            ln.stats(yb, double=False)
            ln.apply_add(yb, xf, xf)
            store_x(p, xf, x_out, tb, T)


def emit_l0attn(p: Prog, sh: Shared, ln: "LNArena", xs: "XStream",
                x_in, x_out, wc: WCache, T=512):
    nt = SEQ // T
    DI = 2048
    NS = 16
    ipw = wc.get("l0a_ipT", p.hostw["L0_attn_in_projT"])
    xpw = wc.get("l0a_xpT", p.hostw["L0_attn_x_projT"])
    dtw = wc.get("l0a_dtT", p.hostw["L0_attn_dt_projT"])
    opw = wc.get("l0a_opT", p.hostw["L0_attn_out_projT"])

    def colload(name, arr2d, mw):
        c = p.const(name, np.ascontiguousarray(arr2d.astype(np.float32)))
        nk = arr2d.shape[0] // 128
        b = p.sbuf(name + "_sb", (128, nk * mw))
        p.op(p.qsync, lambda e: e.dma_start(
            out=b.ap, in_=strip_batch_ap(c, 0, nk, 0, mw, mw),
            allow_slow_non_contiguous=True), writes=[b])
        return b

    convw = colload("l0a_convw", p.hostw["L0_attn_convw"], 4)
    ipb = colload("l0a_ipb", p.hostw["L0_attn_in_proj_b2"], 1)
    dtb = colload("l0a_dtb", p.hostw["L0_attn_dt_proj_b2"], 1)
    cvb = colload("l0a_cvb", p.hostw["L0_attn_conv_b2"], 1)
    opb = colload("l0a_opb", p.hostw["L0_attn_out_proj_b2"], 1)
    Dpar = colload("l0a_D", p.hostw["L0_attn_D_param2"], 1)
    nrmw = colload("l0a_nw", p.hostw["L0_attn_norm_w2"], 1)
    nrmb = colload("l0a_nb", p.hostw["L0_attn_norm_b2"], 1)
    alog = colload("l0a_alog", p.hostw["L0_attn_A_log"], NS)
    xpb_c = load_cols(p, sh, p.dram_in("L0_attn_x_proj_b", (32,)), 32, "l0a_xpb")
    negA = p.sbuf("l0a_negA", (128, 16 * NS))
    p.act(negA, alog, AF.Exp)
    negdtb = p.sbuf("l0a_negdtb", (128, 16))
    p.ts(negdtb, dtb, -1.0, ALU.mult)
    neg_ones_bf = p.sbuf("l0a_negones", (1, 128), BF16)
    p.dma(p.qsync, neg_ones_bf, p.const("c_neg_ones_bf", np.full((1, 128), -1.0, BF_NP)))

    xpstr = p.sbuf("l0a_xpstr", (128, 16 * 32), BF16)
    p.op(p.qgp, lambda e: e.dma_start(
        out=xpstr.ap, in_=strip_batch_ap(xpw.ap, 0, 16, 0, 32, 32)),
        reads=[xpw], writes=[xpstr])

    h_sb = [p.sbuf("l0a_h", (128, NS)) for _ in range(16)]
    for _b in h_sb:
        p.memset(_b, 0.0)
    tail = p.sbuf("l0a_tail", (128, 16 * 3))
    p.memset(tail, 0.0)

    xn = [p.sbuf("l0a_xn", (128, T), BF16) for _ in range(DT)]
    xc = [p.sbuf("l0a_xc", (128, T + 3)) for _ in range(16)]
    z = [p.sbuf("l0a_z", (128, T), BF16) for _ in range(16)]
    u = [p.sbuf("l0a_u", (128, T), BF16) for _ in range(16)]
    du = [p.sbuf("l0a_du", (128, T), BF16) for _ in range(16)]
    y = [p.sbuf("l0a_y", (128, T)) for _ in range(16)]
    ar2 = [p.sbuf("l0a_ar", (128, 2048), BF16) for _ in range(2)]
    ari = ar2
    cscr = p.sbuf("l0a_cscr", (128, T))
    dA = [p.sbuf("l0a_dA", (128, T)) for _ in range(2)]
    dBu = [p.sbuf("l0a_dBu", (128, T)) for _ in range(2)]
    hs = [p.sbuf("l0a_hs", (128, T)) for _ in range(2)]
    cscr2 = [cscr, p.sbuf("l0a_cscr2", (128, T))]
    psb_sb = [p.sbuf("l0a_psbs", (128, T), BF16) for _ in range(2)]
    psc_sb = [p.sbuf("l0a_pscs", (128, T), BF16) for _ in range(2)]
    sz = p.sbuf("l0a_sz", (128, T), BF16)
    ddr_bc = p.dram_tmp_buf("l0a_bc", (32, T), BF16)
    bc_sb = p.sbuf("l0a_bcsb", (32, T), BF16)
    flatB = p.sbuf("l0a_flatB", (1, 4 * T), BF16)
    flatC = p.sbuf("l0a_flatC", (1, 4 * T), BF16)
    yo = xn

    z1s = [p.sbuf("l0a_z1s", (128, T), BF16), sz]
    for tb in range(nt):
        xs.load(x_in, tb, buf=0)
        xf, xb = xs.x[0], xs.xb[0]
        ln.stats(xb, double=True, also_single=True)
        ln.apply(xb, xn)
        for k in range(DT):
            p.ts(xn[k], xn[k], nrmw.ap[:, k:k + 1], ALU.mult,
                 scalar2=nrmb.ap[:, k:k + 1], op1=ALU.add,
                 extra_reads=[nrmw, nrmb])
        for m in range(16):
            if tb == 0:
                p.memset(xc[m], 0.0, oap=xc[m].ap[:, 0:3])
            else:
                p.copy(xc[m], tail, oap=xc[m].ap[:, 0:3],
                       aap=tail.ap[:, m * 3:(m + 1) * 3])
        for m in range(32):
            ps = p.pbank(m % 2)
            ar = ari[m % 2]
            p.op(p.qsync, lambda e, ar=ar, m=m: e.dma_start(
                out=ar.ap[:, 0:1024],
                in_=strip_batch_ap(ipw.ap, 0, 8, m * 128, 128, 4096)),
                reads=[ipw], writes=[ar])
            for k in range(DT):
                p.mm(ps, ar, xn[k], start=(k == 0), stop=(k == 7),
                     lap=ar.ap[:, k * 128:(k + 1) * 128])
            if m < 16:
                p.ts(xc[m], ps, ipb.ap[:, m:m + 1], ALU.add,
                     oap=xc[m].ap[:, 3:T + 3], extra_reads=[ipb])
            else:
                p.ts(z[m - 16], ps, ipb.ap[:, m:m + 1], ALU.add,
                     extra_reads=[ipb])
        for m in range(16):
            p.ts(cscr, xc[m], convw.ap[:, m * 4:m * 4 + 1], ALU.mult,
                 aap=xc[m].ap[:, 0:T], extra_reads=[convw])
            for tap in range(1, 4):
                p.stt(cscr, xc[m], convw.ap[:, m * 4 + tap:m * 4 + tap + 1],
                      cscr, ALU.mult, ALU.add, aap=xc[m].ap[:, tap:tap + T],
                      extra_reads=[convw])
            p.act(u[m], cscr, AF.Silu, bias=cvb.ap[:, m:m + 1],
                  extra_reads=[cvb])
            p.copy(tail, xc[m], oap=tail.ap[:, m * 3:(m + 1) * 3],
                   aap=xc[m].ap[:, T:T + 3])
        psx = p.pbank(2)
        for k in range(16):
            p.mm(psx, xpstr, u[k], start=(k == 0), stop=(k == 15),
                 lap=xpstr.ap[:, k * 32:(k + 1) * 32], oap=psx.ap[0:32, :])
        p.ts(bc_sb, psx, xpb_c[0].ap[0:32, 0:1], ALU.add, aap=psx.ap[0:32, :],
             extra_reads=[xpb_c[0]])
        p.dma(p.qgp, ddr_bc, bc_sb)
        for m in range(16):
            ps = p.pbank(m % 2)
            ar = ar2[m % 2]
            p.op(p.qsync, lambda e, ar=ar, m=m: e.dma_start(
                out=ar.ap, in_=strip_batch_ap(dtw.ap, 0, 16, m * 128, 128, 2048)),
                reads=[dtw], writes=[ar])
            for k in range(16):
                p.mm(ps, ar, u[k], start=(k == 0), stop=(k == 15),
                     lap=ar.ap[:, k * 128:(k + 1) * 128])
            p.act(cscr, ps, AF.Sigmoid, scale=-1.0, bias=negdtb.ap[:, m:m + 1],
                  extra_reads=[negdtb])
            p.act(xc[m], cscr, AF.Ln, oap=xc[m].ap[:, 0:T])
            p.tt(du[m], u[m], xc[m], ALU.mult, bap=xc[m].ap[:, 0:T])
        for s in range(NS):
            s4 = s % 4
            if s4 == 0:
                h0 = (s // 4) * 4
                p.op(p.qgp, lambda e, h0=h0: e.dma_start(
                    out=flatB.ap,
                    in_=ddr_bc.ap[h0:h0 + 4, :].rearrange("a b -> (a b)")),
                    reads=[ddr_bc], writes=[flatB])
                p.op(p.qgp, lambda e, h0=h0: e.dma_start(
                    out=flatC.ap,
                    in_=ddr_bc.ap[16 + h0:16 + h0 + 4, :].rearrange("a b -> (a b)")),
                    reads=[ddr_bc], writes=[flatC])
            psb = p.pbank(2)
            psc = p.pbank(3)
            p.mm(psb, neg_ones_bf, flatB, start=True, stop=True,
                 rap=flatB.ap[0:1, s4 * T:(s4 + 1) * T])
            p.mm(psc, sh.ones_row_bf, flatC, start=True, stop=True,
                 rap=flatC.ap[0:1, s4 * T:(s4 + 1) * T])
            pb_s, pc_s = psb_sb[s % 2], psc_sb[s % 2]
            p.copy(pb_s, psb, eng=p.scal)
            p.copy(pc_s, psc, eng=p.scal)
            for m in range(16):
                j = m % 2
                p.act(dA[j], xc[m], AF.Exp, scale=negA.ap[:, m * NS + s:m * NS + s + 1],
                      aap=xc[m].ap[:, 0:T], extra_reads=[negA])
                p.tt(dBu[j], du[m], pb_s, ALU.mult, eng=p.pool)
                p.scan(hs[j], dA[j], dBu[j],
                       initial=h_sb[m].ap[:, s:s + 1],
                       extra_reads=[h_sb[m]])
                p.copy(h_sb[m], hs[j], oap=h_sb[m].ap[:, s:s + 1],
                       aap=hs[j].ap[:, T - 1:T], eng=p.scal)
                if s == 0:
                    p.tt(y[m], hs[j], pc_s, ALU.mult, eng=p.pool)
                else:
                    p.tt(cscr2[j], hs[j], pc_s, ALU.mult, eng=p.pool)
                    p.tt(y[m], y[m], cscr2[j], ALU.add)
        for m in range(16):
            p.stt(y[m], u[m], Dpar.ap[:, m:m + 1], y[m], ALU.mult, ALU.add,
                  extra_reads=[Dpar])
            p.act(sz, z[m], AF.Silu)
            p.tt(z[m], y[m], sz, ALU.mult)
        ln.bcast_single(p.pbank(2), p.pbank(3))
        for m in range(8):
            ps = p.pbank(m % 2)
            ar = ar2[m % 2]
            p.op(p.qsync, lambda e, ar=ar, m=m: e.dma_start(
                out=ar.ap, in_=strip_batch_ap(opw.ap, 0, 16, m * 128, 128, 1024)),
                reads=[opw], writes=[ar])
            s0 = ln.scr[m % 2]
            p.tt(s0, xb[m], p.pbank(2), ALU.mult)
            p.tt(z1s[m % 2], s0, p.pbank(3), ALU.subtract)
            for k in range(16):
                p.mm(ps, ar, z[k], start=(k == 0), stop=False,
                     lap=ar.ap[:, k * 128:(k + 1) * 128])
            p.mm(ps, sh.ident_bf, z1s[m % 2], start=False, stop=True)
            p.ts(yo[m], ps, opb.ap[:, m:m + 1], ALU.add, extra_reads=[opb])
        ln.stats(yo, double=False)
        ln.apply_add(yo, xf, xf)
        store_x(p, xf, x_out, tb, T)



def emit_l1attn(p: Prog, sh: Shared, ln: "LNArena", xs: "XStream",
                x_in, x_out, wc: WCache, T=512):
    nt = SEQ // T
    NH, PD, NSt, CH = 32, 64, 64, 64
    ipw = wc.get("l1a_ipT", p.hostw["L1_attn_in_projT"])
    opw = wc.get("l1a_opT", p.hostw["L1_attn_out_projT"])

    def colload(name, arr2d, mw):
        c = p.const(name, np.ascontiguousarray(arr2d.astype(np.float32)))
        nk = arr2d.shape[0] // 128
        b = p.sbuf(name + "_sb", (128, nk * mw))
        p.op(p.qsync, lambda e: e.dma_start(
            out=b.ap, in_=strip_batch_ap(c, 0, nk, 0, mw, mw),
            allow_slow_non_contiguous=True), writes=[b])
        return b

    convw = colload("l1a_convw", p.hostw["L1_attn_convw17"], 4)
    cvb = colload("l1a_cvb", p.hostw["L1_attn_conv_b17"], 1)
    nrw = colload("l1a_nrw", p.hostw["L1_attn_norm_w2"], 1)
    dtb_row = p.sbuf("l1a_dtbr", (1, 32))
    p.dma(p.qsync, dtb_row, p.dram_in("L1_attn_dt_bias", (32,)).rearrange("(o n) -> o n", o=1))
    alog_row = p.sbuf("l1a_alogr", (1, 32))
    p.dma(p.qsync, alog_row, p.dram_in("L1_attn_A_log", (32,)).rearrange("(o n) -> o n", o=1))
    D_row = p.sbuf("l1a_Dr", (1, 32))
    p.dma(p.qsync, D_row, p.dram_in("L1_attn_D", (32,)).rearrange("(o n) -> o n", o=1))
    ps4 = p.pbank(4)
    def row_to_col(row, name):
        p.op(p.tens, lambda e: e.transpose(
            ps4.ap[0:32, 0:1], row.ap, sh.ident_f32.ap[0:1, 0:1]),
            reads=[row, sh.ident_f32], writes=[ps4])
        c = p.sbuf(name, (32, 1))
        p.copy(c, ps4, aap=ps4.ap[0:32, 0:1], eng=p.scal)
        return c
    dtb_col = row_to_col(dtb_row, "l1a_dtbc")
    alog_col = row_to_col(alog_row, "l1a_alogc")
    negdtb = p.sbuf("l1a_negdtb", (32, 1))
    p.ts(negdtb, dtb_col, -1.0, ALU.mult)
    negA2 = p.sbuf("l1a_negA2", (32, 1))
    p.act(negA2, alog_col, AF.Exp)
    p.mm(ps4, sh.ones_row_f32, D_row, start=True, stop=True,
         lap=sh.ones_row_f32.ap[0:1, 0:64], oap=ps4.ap[0:64, 0:32])
    Dbc = p.sbuf("l1a_Dbc", (64, 32))
    p.copy(Dbc, ps4, aap=ps4.ap[0:64, 0:32], eng=p.scal)

    t64 = np.zeros((64, 64), np.float32)
    for s_ in range(64):
        t64[s_, s_:] = 1.0
    TRI64 = p.sbuf("l1a_TRI64", (64, 64))
    p.dma(p.qsync, TRI64, p.const("c_l1a_TRI64", t64))
    BD64 = p.sbuf("l1a_BD64", (64, 64))
    p.dma(p.qsync, BD64, p.const("c_l1a_BD64", np.ones((64, 64), np.float32)))
    sel01 = np.zeros((32, 2048), np.float32)
    for h in range(32):
        sel01[h, h * 64:(h + 1) * 64] = 1.0
    SEL01 = p.sbuf("l1a_SEL01", (32, 2048))
    p.dma(p.qsync, SEL01, p.const("c_l1a_SEL01", sel01))
    mbd = np.zeros((64, 64), np.float32)
    for s_ in range(64):
        for l_ in range(64):
            if s_ > l_:
                mbd[s_, l_] = -1e30
    MBD = p.sbuf("l1a_MBD", (64, 64))
    p.dma(p.qsync, MBD, p.const("c_l1a_MBD", mbd))

    S0 = np.transpose(np.asarray(p.hostw["L1_attn_init_states"], np.float32),
                      (2, 0, 1)).reshape(64, 2048)
    S_T = p.sbuf("l1a_ST", (64, 2048))
    p.dma(p.qsync, S_T, p.const("c_l1a_S0", np.ascontiguousarray(S0)))
    S_bf = p.sbuf("l1a_STbf", (64, 2048), BF16)
    p.copy(S_bf, S_T, eng=p.scal)

    xn = [p.sbuf("l1a_xn", (128, T), BF16) for _ in range(DT)]
    z_t = [p.sbuf("l1a_z", (128, T), BF16) for _ in range(16)]
    xc = [p.sbuf("l1a_xc", (128, T + 3), BF16) for _ in range(17)]
    u_x = [p.sbuf("l1a_ux", (128, T), BF16) for _ in range(16)]
    BCs = p.sbuf("l1a_BCs", (128, T), BF16)
    tail = p.sbuf("l1a_tail", (128, 17 * 3), BF16)
    p.memset(tail, 0.0)
    Ldt = p.sbuf("l1a_Ldt", (32, T))
    adt = p.sbuf("l1a_adt", (32, T))
    cscr = p.sbuf("l1a_cscr", (128, T))
    ari = [p.sbuf("l1a_ari", (128, 2048), BF16) for _ in range(2)]
    acsT = p.sbuf("l1a_acsT", (32, 64))
    adt_c = p.sbuf("l1a_adtc", (64, 32))
    acs_c = p.sbuf("l1a_acsc", (64, 32))
    acsF_c = p.sbuf("l1a_acsFc", (64, 32))
    sdo_c = p.sbuf("l1a_sdoc", (64, 32))
    ds_c = p.sbuf("l1a_dsc", (64, 32))
    dt_c = p.sbuf("l1a_dtc", (64, 32))
    X_c = p.sbuf("l1a_Xc", (64, 2048), BF16)
    Xds_c = p.sbuf("l1a_Xdsc", (64, 2048), BF16)
    x_c = p.sbuf("l1a_xc_tok", (64, 2048), BF16)
    B_c = p.sbuf("l1a_Bc", (64, 64), BF16)
    acl_fm = p.sbuf("l1a_acl", (32, T // CH))
    R1 = p.sbuf("l1a_R1", (32, 2048))
    CBs = p.sbuf("l1a_CBs", (64, 64))
    pa_u = [p.sbuf("l1a_pau", (64, 512)) for _ in range(2)]
    pa_e = [p.sbuf("l1a_pae", (64, 512)) for _ in range(2)]

    mt_all = p.sbuf("l1a_mt", (64, 2048), BF16)
    dec_row = p.sbuf("l1a_decrow", (1, 2048))
    Y_c = p.sbuf("l1a_Yc", (64, 2048))
    C_fm = p.sbuf("l1a_Cfm", (64, T), BF16)
    f32scr = p.sbuf("l1a_f32scr", (128, 128))
    f32scr2 = [f32scr, p.sbuf("l1a_f32scrB", (128, 128))]
    y_fm = [p.sbuf("l1a_yfm", (128, T), BF16) for _ in range(16)]
    for _b in y_fm:
        p.memset(_b, 0.0)
    sq16 = [p.sbuf("l1a_sq16", (128, T), BF16)] * 2
    yo = xn
    nch2 = T // 128

    for tb in range(nt):
        t0 = tb * T
        xs.load(x_in, tb, buf=0)
        xf, xb = xs.x[0], xs.xb[0]
        ln.stats(xb, double=False)
        ln.apply(xb, xn)
        for m in range(17):
            if tb == 0:
                p.memset(xc[m], 0.0, oap=xc[m].ap[:, 0:3])
            else:
                p.copy(xc[m], tail, oap=xc[m].ap[:, 0:3],
                       aap=tail.ap[:, m * 3:(m + 1) * 3])
        for m in range(34):
            ps = p.pbank(m % 2)
            ar = ari[m % 2]
            mw = 128 if m < 33 else 32
            p.op(p.qsync, lambda e, ar=ar, m=m, mw=mw: e.dma_start(
                out=ar.ap[:, 0:8 * mw],
                in_=strip_batch_ap(ipw.ap, 0, 8, m * 128, mw, 4256)),
                reads=[ipw], writes=[ar])
            for k in range(DT):
                p.mm(ps, ar, xn[k], start=(k == 0), stop=(k == 7),
                     lap=ar.ap[:, k * mw:(k + 1) * mw],
                     oap=ps.ap[0:mw, :] if mw != 128 else None)
            if m < 16:
                p.copy(z_t[m], ps, eng=p.scal)
            elif m < 32:
                p.copy(xc[m - 16], ps, oap=xc[m - 16].ap[:, 3:T + 3], eng=p.scal)
            elif m == 32:
                p.copy(xc[16], ps, oap=xc[16].ap[:, 3:T + 3], eng=p.scal)
            else:
                p.act(cscr, ps, AF.Sigmoid, scale=-1.0,
                      bias=negdtb.ap[:, 0:1], aap=ps.ap[0:32, :],
                      oap=cscr.ap[0:32, :], extra_reads=[negdtb])
                p.act(Ldt, cscr, AF.Ln, aap=cscr.ap[0:32, :])
                p.ts(adt, Ldt, negA2.ap[:, 0:1], ALU.mult, extra_reads=[negA2])
        for m in range(17):
            p.ts(cscr, xc[m], convw.ap[:, m * 4:m * 4 + 1], ALU.mult,
                 aap=xc[m].ap[:, 0:T], extra_reads=[convw])
            for tap in range(1, 4):
                p.stt(cscr, xc[m], convw.ap[:, m * 4 + tap:m * 4 + tap + 1],
                      cscr, ALU.mult, ALU.add, aap=xc[m].ap[:, tap:tap + T],
                      extra_reads=[convw])
            dst = u_x[m] if m < 16 else BCs
            p.act(dst, cscr, AF.Silu, bias=cvb.ap[:, m:m + 1],
                  extra_reads=[cvb])
            p.copy(tail, xc[m], oap=tail.ap[:, m * 3:(m + 1) * 3],
                   aap=xc[m].ap[:, T:T + 3])
        p.reduce(acl_fm, adt, ALU.add,
                 aap=adt.ap.rearrange("h (c l) -> h c l", l=CH))
        p.copy(C_fm, BCs, aap=BCs.ap[64:128, :])
        for c in range(T // CH):
            ct0 = c * CH
            ps6 = p.pbank(6)
            ps5 = p.pbank(5)
            p.op(p.tens, lambda e, ct0=ct0: e.transpose(
                ps6.ap[0:64, 0:32], adt.ap[0:32, ct0:ct0 + CH],
                sh.ident_f32.ap[0:32, 0:32]), reads=[adt, sh.ident_f32],
                writes=[ps6])
            p.copy(adt_c, ps6, aap=ps6.ap[0:64, 0:32])
            p.mm(ps5, TRI64, adt_c, start=True, stop=True, oap=ps5.ap[0:64, 0:32])
            p.copy(acs_c, ps5, aap=ps5.ap[0:64, 0:32])
            p.mm(ps5, BD64, adt_c, start=True, stop=True, oap=ps5.ap[0:64, 0:32])
            p.copy(acsF_c, ps5, aap=ps5.ap[0:64, 0:32])
            p.act(sdo_c, acs_c, AF.Exp)
            p.tt(ds_c, acsF_c, acs_c, ALU.subtract)
            p.act(ds_c, ds_c, AF.Exp)
            p.op(p.tens, lambda e: e.transpose(
                ps6.ap[0:32, 0:64], acs_c.ap, sh.ident_f32.ap[0:64, 0:64]),
                reads=[acs_c, sh.ident_f32], writes=[ps6])
            p.copy(acsT, ps6, aap=ps6.ap[0:32, 0:64], eng=p.scal)
            p.op(p.tens, lambda e, ct0=ct0: e.transpose(
                ps6.ap[0:64, 0:32], Ldt.ap[0:32, ct0:ct0 + CH],
                sh.ident_f32.ap[0:32, 0:32]), reads=[Ldt, sh.ident_f32],
                writes=[ps6])
            p.ts(dt_c, ps6, -1.0, ALU.mult, aap=ps6.ap[0:64, 0:32])
            for m in range(16):
                fs = f32scr2[m % 2]
                pst = p.pbank(5 if m % 2 else 6)
                p.copy(fs, u_x[m], aap=u_x[m].ap[:, ct0:ct0 + CH],
                       oap=fs.ap[:, 0:64])
                p.op(p.tens, lambda e, m=m, pst=pst, fs=fs: e.transpose(
                    pst.ap[0:64, 0:128], fs.ap[:, 0:64], sh.ident_f32.ap),
                    reads=[fs, sh.ident_f32], writes=[pst])
                p.copy(x_c, pst, oap=x_c.ap[:, m * 128:(m + 1) * 128],
                       aap=pst.ap[0:64, 0:128],
                       eng=(p.scal if m % 2 else p.vect))
            x3 = x_c.ap.rearrange("p (h s) -> p h s", h=32)
            p.tt(X_c, x_c, dt_c, ALU.mult,
                 oap=X_c.ap.rearrange("p (h s) -> p h s", h=32), aap=x3,
                 bap=dt_c.ap.unsqueeze(2).broadcast_to((64, 32, 64)))
            p.tt(Xds_c, X_c, ds_c, ALU.mult,
                 oap=Xds_c.ap.rearrange("p (h s) -> p h s", h=32),
                 aap=X_c.ap.rearrange("p (h s) -> p h s", h=32),
                 bap=ds_c.ap.unsqueeze(2).broadcast_to((64, 32, 64)))
            p.copy(f32scr, BCs, aap=BCs.ap[0:64, ct0:ct0 + CH],
                   oap=f32scr.ap[0:64, 0:64])
            p.op(p.tens, lambda e: e.transpose(
                ps6.ap[0:64, 0:64], f32scr.ap[0:64, 0:64],
                sh.ident_f32.ap[0:64, 0:64]),
                reads=[f32scr, sh.ident_f32], writes=[ps6])
            p.copy(B_c, ps6, aap=ps6.ap[0:64, 0:64])
            ps4b = p.pbank(4)
            p.mm(ps4b, BCs, C_fm, start=True, stop=True,
                 lap=BCs.ap[0:64, ct0:ct0 + CH],
                 rap=C_fm.ap[:, ct0:ct0 + CH],
                 oap=ps4b.ap[0:64, 0:64])
            p.copy(CBs, ps4b, aap=ps4b.ap[0:64, 0:64], eng=p.scal)
            p.tt(R1, SEL01, acsT, ALU.mult,
                 oap=R1.ap.rearrange("p (h s) -> p h s", h=32),
                 aap=SEL01.ap.rearrange("p (h s) -> p h s", h=32),
                 bap=acsT.ap.unsqueeze(1).broadcast_to((32, 32, 64)))
            for q in range(4):
                bq = p.pbank(q)
                p.mm(bq, BD64, R1, start=True, stop=True,
                     lap=BD64.ap[0:32, 0:64],
                     rap=R1.ap[:, q * 512:(q + 1) * 512],
                     oap=bq.ap[0:64, :])
                u0 = pa_u[q % 2]
                p.tt(u0, bq, acs_c, ALU.subtract,
                     oap=u0.ap.rearrange("p (h s) -> p h s", h=8),
                     aap=bq.ap[0:64, :].rearrange("p (h s) -> p h s", h=8),
                     bap=acs_c.ap[:, q * 8:(q + 1) * 8].unsqueeze(2)
                     .broadcast_to((64, 8, 64)))
                p.tt(u0, u0, MBD, ALU.add,
                     oap=u0.ap.rearrange("p (h s) -> p h s", h=8),
                     aap=u0.ap.rearrange("p (h s) -> p h s", h=8),
                     bap=MBD.ap.unsqueeze(1).broadcast_to((64, 8, 64)))
                u1 = pa_e[q % 2]
                p.act(u1, u0, AF.Exp)
                p.tt(mt_all, u1, CBs, ALU.mult,
                     oap=mt_all.ap[:, q * 512:(q + 1) * 512]
                     .rearrange("p (h s) -> p h s", h=8),
                     aap=u1.ap.rearrange("p (h s) -> p h s", h=8),
                     bap=CBs.ap.unsqueeze(1).broadcast_to((64, 8, 64)))
            ybanks = [p.pbank(i) for i in range(4)]
            dbanks = [p.pbank(4 + i) for i in range(4)]
            for i in range(4):
                p.mm(ybanks[i], C_fm, S_bf, start=True, stop=True,
                     lap=C_fm.ap[:, ct0:ct0 + CH],
                     rap=S_bf.ap[:, i * 512:(i + 1) * 512],
                     oap=ybanks[i].ap[0:64, :])
            for h in range(NH):
                p.mm(dbanks[h // 8], mt_all, X_c,
                     start=(h % 8 == 0), stop=(h % 8 == 7),
                     lap=mt_all.ap[0:64, h * 64:(h + 1) * 64],
                     rap=X_c.ap[:, h * 64:(h + 1) * 64],
                     oap=dbanks[h // 8].ap[0:64, (h % 8) * 64:(h % 8) * 64 + 64],
                     inc=True)
            for q in range(4):
                u0 = pa_u[q % 2]
                u1 = pa_e[q % 2]
                sl = slice(q * 512, (q + 1) * 512)
                p.tt(u0, ybanks[q], sdo_c, ALU.mult,
                     oap=u0.ap.rearrange("p (h s) -> p h s", h=8),
                     aap=ybanks[q].ap[0:64, :].rearrange("p (h s) -> p h s", h=8),
                     bap=sdo_c.ap[:, q * 8:(q + 1) * 8].unsqueeze(2)
                     .broadcast_to((64, 8, 64)))
                p.tt(u0, u0, dbanks[q], ALU.add,
                     bap=dbanks[q].ap[0:64, :])
                p.tt(u1, x_c, Dbc, ALU.mult,
                     oap=u1.ap.rearrange("p (h s) -> p h s", h=8),
                     aap=x_c.ap[:, sl].rearrange("p (h s) -> p h s", h=8),
                     bap=Dbc.ap[:, q * 8:(q + 1) * 8].unsqueeze(2)
                     .broadcast_to((64, 8, 64)))
                p.tt(Y_c, u0, u1, ALU.add, oap=Y_c.ap[:, sl])
            for i in range(4):
                ps5c = p.pbank(5)
                p.mm(ps5c, acl_fm, SEL01, start=True, stop=True,
                     lap=acl_fm.ap[:, c:c + 1],
                     rap=SEL01.ap[:, i * 512:(i + 1) * 512],
                     oap=ps5c.ap[0:1, 0:512])
                p.act(dec_row, ps5c, AF.Exp, aap=ps5c.ap[0:1, 0:512],
                      oap=dec_row.ap[0:1, i * 512:(i + 1) * 512])
                ps5e = p.pbank(5)
                p.mm(ps5e, sh.ones_row_f32, dec_row, start=True, stop=True,
                     lap=sh.ones_row_f32.ap[0:1, 0:64],
                     oap=ps5e.ap[0:64, 0:512],
                     rap=dec_row.ap[0:1, i * 512:(i + 1) * 512])
                p.tt(S_T, S_T, ps5e, ALU.mult,
                     aap=S_T.ap[:, i * 512:(i + 1) * 512],
                     oap=S_T.ap[:, i * 512:(i + 1) * 512],
                     bap=ps5e.ap[0:64, 0:512])
            for i in range(4):
                sb_ = p.pbank(4 + (i % 2))
                p.mm(sb_, B_c, Xds_c, start=True, stop=True,
                     lap=B_c.ap[:, 0:64],
                     rap=Xds_c.ap[:, i * 512:(i + 1) * 512],
                     oap=sb_.ap[0:64, :])
                p.tt(S_T, S_T, sb_, ALU.add,
                     aap=S_T.ap[:, i * 512:(i + 1) * 512],
                     oap=S_T.ap[:, i * 512:(i + 1) * 512],
                     bap=sb_.ap[0:64, :])
            p.copy(S_bf, S_T, eng=p.scal)
            for m in range(16):
                pst = p.pbank(5 if m % 2 else 6)
                p.op(p.tens, lambda e, m=m, pst=pst: e.transpose(
                    pst.ap[:, 0:64], Y_c.ap[:, m * 128:(m + 1) * 128],
                    sh.ident_f32.ap[0:64, 0:64]),
                    reads=[Y_c, sh.ident_f32], writes=[pst])
                p.copy(y_fm[m], pst, oap=y_fm[m].ap[:, ct0:ct0 + CH],
                       aap=pst.ap[:, 0:64],
                       eng=(p.scal if m % 2 else p.vect))
        for m in range(16):
            p.act(sq16[m % 2], z_t[m], AF.Silu)
            p.tt(y_fm[m], y_fm[m], sq16[m % 2], ALU.mult)
        for m in range(16):
            p.act(sq16[m % 2], y_fm[m], AF.Square)
            p.mm(p.pbank(5), sh.c1024_col, sq16[m % 2], start=(m == 0),
                 stop=(m == 15), oap=p.pbank(5).ap[0:1, :], inc=True)
        rrow = ln.row_var
        p.ts(rrow, p.pbank(5), 1024.0 / 2048.0, ALU.mult,
             scalar2=EPS, op1=ALU.add, aap=p.pbank(5).ap[0:1, :])
        p.act(rrow, rrow, AF.Ln)
        p.act(rrow, rrow, AF.Exp, scale=-0.5)
        p.mm(p.pbank(6), sh.ones_row_f32, rrow, start=True, stop=True)
        for m in range(16):
            p.tt(cscr, y_fm[m], p.pbank(6), ALU.mult)
            p.ts(y_fm[m], cscr, nrw.ap[:, m:m + 1], ALU.mult, extra_reads=[nrw])
        for m in range(8):
            ps = p.pbank(m % 2)
            ar = ari[m % 2]
            p.op(p.qsync, lambda e, ar=ar, m=m: e.dma_start(
                out=ar.ap[:, 0:2048],
                in_=strip_batch_ap(opw.ap, 0, 16, m * 128, 128, 1024)),
                reads=[opw], writes=[ar])
            for k in range(16):
                p.mm(ps, ar, y_fm[k], start=(k == 0), stop=(k == 15),
                     lap=ar.ap[:, k * 128:(k + 1) * 128])
            p.copy(yo[m], ps, eng=p.scal)
        ln.stats(yo, double=False)
        ln.apply_add(yo, xf, xf)
        store_x(p, xf, x_out, tb, T)


def _build_fused(hostw, quiet_inputs=()):
    p = Prog()
    p.hostw = hostw
    sh = Shared(p)
    x_in = p.dram_in("x_tm_in", (SEQ, D))
    x_out_final = p.dram_out("x_tm_out", (SEQ, D))
    sA = p.dram_tmp("x_scrA", (D, SEQ))
    sB = p.dram_tmp("x_scrB", (D, SEQ))
    sh.setup(need_ident_bf=True, need_ident_f32=True)
    ln = LNArena(p, sh, 512)
    xs = XStream(p, 512)
    wc = WCache(p)
    qb = p.sbuf("quiet_all", (1, 16), F32)
    for nm, shape in quiet_inputs:
        apq = p.dram_in(nm, shape)
        flat = ("a -> (a)" if len(shape) == 1 else
                "a b -> (a b)" if len(shape) == 2 else "a b c -> (a b c)")
        v = apq.rearrange(flat)[0:16].rearrange("(o n) -> o n", o=1)
        p.op(p.qsync, lambda e, v=v, qb=qb: e.dma_start(out=qb.ap, in_=v),
             writes=[qb])

    p.open_scope()
    emit_entry_transpose(p, sh, x_in, sA)
    p.close_scope()
    cur, nxt = sA, sB
    for name, em, wc_ in BLOCK_EMITTERS:
        p.open_scope()
        emitter = globals()[em]
        kw = {"wc": wc} if wc_ else {}
        emitter(p, sh, ln, xs, cur, nxt, **kw)
        p.close_scope()
        cur, nxt = nxt, cur
    p.open_scope()
    emit_exit_transpose(p, sh, cur, x_out_final)
    p.close_scope()
    _finalize_waits(p)
    return p


_FUSED = {}


def kernel_fused(**inputs):
    hostw = host_weights(inputs)
    x_full = np.asarray(inputs["L0_attn_x"], np.float32)
    quiet = [("L2_ffn_Wg", (D, 16)), ("L2_ffn_Wq", (16, D, 64)),
             ("L2_ffn_bq", (16, 64)), ("L2_ffn_Wkv", (16, D, 128)),
             ("L2_ffn_bkv", (16, 128)), ("L2_ffn_Wo", (16, 64, D)),
             ("L2_ffn_bo", (D,))]
    p = _build_fused(hostw, quiet)
    extra = [{"x_tm_in": np.ascontiguousarray(x_full[c])} for c in range(NCORES)]
    res = _run_prog(p, extra, inputs)
    return np.stack([np.asarray(r["x_tm_out"]) for r in res], 0)



BLOCK_EMITTERS = [
    ("L0_attn", "emit_l0attn", True),
    ("L0_ffn", "emit_l0ffn", True),
    ("L1_attn", "emit_l1attn", True),
    ("L1_ffn", "emit_l1ffn", False),
    ("L2_attn", "emit_l2attn", True),
    ("L2_ffn", "emit_l2ffn", False),
    ("L3_attn", "emit_l3attn", True),
    ("L3_ffn", "emit_l3ffn", True),
]


def _finalize_waits(p):
    def fin(e, p=p):
        e.wait_ge(p.qsync.sem, p.qsync.n)
        e.wait_ge(p.qgp.sem, p.qgp.n)
    p.qsync.push(fin)


def _build_entry():
    p = Prog()
    sh = Shared(p)
    x_in = p.dram_in("x_tm_in", (SEQ, D))
    x_out = p.dram_out("x_res_out", (D, SEQ))
    sh.setup(need_ident_f32=True)
    emit_entry_transpose(p, sh, x_in, x_out)
    _finalize_waits(p)
    return p


def _build_exit():
    p = Prog()
    sh = Shared(p)
    x_in = p.dram_in("x_res_in", (D, SEQ))
    x_out = p.dram_out("x_tm_out", (SEQ, D))
    sh.setup(need_ident_f32=True)
    emit_exit_transpose(p, sh, x_in, x_out)
    _finalize_waits(p)
    return p


def _build_block(name, emitter_name, needs_wc, hostw):
    p = Prog()
    p.hostw = hostw
    sh = Shared(p)
    x_in = p.dram_in("x_res_in", (D, SEQ))
    x_out = p.dram_out("x_res_out", (D, SEQ))
    sh.setup(need_ident_bf=True, need_ident_f32=True)
    ln = LNArena(p, sh, 512)
    xs = XStream(p, 512)
    emitter = globals()[emitter_name]
    kw = {"wc": WCache(p)} if needs_wc else {}
    emitter(p, sh, ln, xs, x_in, x_out, **kw)
    _finalize_waits(p)
    return p


_PROG_CACHE = {}


def _run_prog(p, per_core_extra, inputs):
    nc = p.finish() if not getattr(p, "_finished", False) else p.nc
    p._finished = True
    in_maps = []
    for c in range(NCORES):
        im = {}
        im.update(p.consts)
        for n in getattr(p, "extern_names", []):
            if n in per_core_extra[c]:
                im[n] = per_core_extra[c][n]
            elif n in inputs:
                im[n] = np.ascontiguousarray(np.asarray(inputs[n], np.float32))
            else:
                raise KeyError(n)
        in_maps.append(im)
    res = run_bass_kernel_spmd(nc, in_maps, core_ids=list(range(NCORES)))
    return res.results


def kernel(**inputs):
    return kernel_fused(**inputs)
    hostw = host_weights(inputs)
    x_full = np.asarray(inputs["L0_attn_x"], np.float32)

    p = _build_entry()
    res = _run_prog(p, [{"x_tm_in": np.ascontiguousarray(x_full[c])}
                        for c in range(NCORES)], inputs)
    xres = [np.asarray(r["x_res_out"]) for r in res]

    for name, em, wc_ in BLOCK_EMITTERS:
        p = _build_block(name, em, wc_, hostw)
        res = _run_prog(p, [{"x_res_in": xres[c]} for c in range(NCORES)], inputs)
        xres = [np.asarray(r["x_res_out"]) for r in res]

    p = _build_exit()
    res = _run_prog(p, [{"x_res_in": xres[c]} for c in range(NCORES)], inputs)
    out = np.stack([np.asarray(r["x_tm_out"]) for r in res], 0)
    return out


